# version 20
# baseline (speedup 1.0000x reference)
import numpy as np
import ml_dtypes
from contextlib import ExitStack

import concourse.bass as bass
import concourse.mybir as mybir
from concourse.bass_utils import run_bass_kernel_spmd

F32 = mybir.dt.float32
BF16 = mybir.dt.bfloat16
I32 = mybir.dt.int32
AF = mybir.ActivationFunctionType
ALU = mybir.AluOpType
AX = mybir.AxisListType.X
BF = ml_dtypes.bfloat16

NCORE = 8
TPC = 2048
C = 1024
NTILE = TPC // 128

ENGS = ("sync", "tensor", "vector", "scalar", "gpsimd")


ELIDE_SAME_ENG = False


class Prog:
    def __init__(self):
        self.nc = bass.Bass("TRN2", target_bir_lowering=False)
        self.ctx = ExitStack()
        self.ops = {e: [] for e in ENGS}
        self.es = {}
        self.cnt = {e: 0 for e in ENGS}
        self.waited = {e: {} for e in ENGS}
        self.lastw = {}
        self._ch = 0
        self.chlast = {}
        self.readers = {}

    def _esem(self, key):
        if key not in self.es:
            self.es[key] = self.ctx.enter_context(self.nc.semaphore("es_" + key.replace(".", "_")))
        return self.es[key]

    def sb(self, name, shape, dt):
        return self.ctx.enter_context(self.nc.sbuf_tensor(name, list(shape), dt))

    def ps(self, name, shape, dt=F32):
        return self.ctx.enter_context(self.nc.psum_tensor(name, list(shape), dt))

    def din(self, name, shape, dt):
        return self.nc.dram_tensor(name, list(shape), dt, kind="ExternalInput").ap()

    def dout(self, name, shape, dt):
        return self.nc.dram_tensor(name, list(shape), dt, kind="ExternalOutput").ap()

    def begin_lanes(self, n):
        self._lanes = [[] for _ in range(n)]
        self._lane = 0

    def lane(self, k):
        self._lane = k

    def end_lanes(self, gran=1):
        lanes = self._lanes
        self._lanes = None
        idx = [0] * len(lanes)
        while True:
            done = True
            for k in range(len(lanes)):
                for _ in range(gran):
                    if idx[k] < len(lanes[k]):
                        done = False
                        lanes[k][idx[k]]()
                        idx[k] += 1
            if done:
                break

    def op(self, eng, fn, inc=1, deps=(), reads=(), writes=()):
        if getattr(self, "_lanes", None) is not None:
            self._lanes[self._lane].append(
                lambda eng=eng, fn=fn, inc=inc, deps=deps, reads=reads, writes=writes:
                self._op_now(eng, fn, inc, deps, reads, writes))
            return None
        return self._op_now(eng, fn, inc, deps, reads, writes)

    def _op_now(self, eng, fn, inc=1, deps=(), reads=(), writes=()):
        alldeps = list(deps)
        for r in reads:
            alldeps.append(self.lastw.get(r))
        for w in writes:
            alldeps.append(self.lastw.get(w))
            alldeps.extend(self.readers.get(w, ()))
        self.dep(eng, *alldeps)
        key = eng + (f".d{self._ch}" if inc == 16 else ".c")
        h = self._esem(key)
        self.cnt[key] = self.cnt.get(key, 0) + inc
        tok = (key, self.cnt[key])

        def g(e, fn=fn, h=h, k=inc):
            fn(e).then_inc(h, k)
        self.ops[eng].append(g)
        for r in reads:
            self.readers.setdefault(r, []).append(tok)
        for w in writes:
            self.lastw[w] = tok
            self.readers[w] = []
        return tok

    def dma(self, eng, out, in_, deps=(), reads=(), writes=(), ch=0):
        if getattr(self, "_lanes", None) is not None:
            self._lanes[self._lane].append(
                lambda eng=eng, out=out, in_=in_, deps=deps, reads=reads,
                writes=writes, ch=ch: self._dma_now(eng, out, in_, deps, reads, writes, ch))
            return None
        return self._dma_now(eng, out, in_, deps, reads, writes, ch)

    def _dma_now(self, eng, out, in_, deps=(), reads=(), writes=(), ch=0):
        self._ch = ch
        key = f"{eng}.d{ch}"
        prev = self.chlast.get(key)
        if prev is not None:
            self.dep(eng, prev)
        tok = self._op_now(eng, lambda e, o=out, i=in_: e.dma_start(out=o, in_=i),
                           inc=16, deps=deps, reads=reads, writes=writes)
        self.chlast[key] = tok
        return tok

    def dep(self, eng, *toks):
        best = {}
        for t in toks:
            if t is None:
                continue
            tk, n = t
            if tk == "tensor.c" and eng == "tensor":
                continue
            if ELIDE_SAME_ENG and tk == eng + ".c":
                continue
            if n > best.get(tk, 0):
                best[tk] = n
        for tk, n in best.items():
            if self.waited[eng].get(tk, 0) >= n:
                continue
            h = self._esem(tk)
            self.ops[eng].append(lambda e, h=h, v=n: e.wait_ge(h, v))
            self.waited[eng][tk] = n

    def finish(self, final_eng="sync"):
        for key, n in list(self.cnt.items()):
            self.dep(final_eng, (key, n))
        with self.nc.Block() as block:
            for eng in ENGS:
                if self.ops[eng]:
                    def runner(e, _ops=tuple(self.ops[eng])):
                        for f in _ops:
                            f(e)
                    getattr(block, eng)(runner)
        self.ctx.close()
        return self.nc

    def const_cols(self, values: dict):
        tile = self.sb("constc", [128, max(1, len(values))], F32)
        aps = {}
        for i, (k, v) in enumerate(values.items()):
            self.op("gpsimd",
                    lambda g, i=i, v=v, t=tile: g.memset(t[:, i:i + 1], float(v)),
                    writes=("constc",))
            aps[k] = tile[:, i:i + 1]
        return aps


def copy_any(e, dst, src):
    if hasattr(e, "tensor_copy"):
        return e.tensor_copy(dst, src)
    return e.copy(dst, src)


class MMRunner:

    def __init__(self, p: Prog, name, width=512, nbank=4, psum_dt=F32):
        self.p = p
        self.banks = [p.ps(f"{name}_pm{i}", [128, width], psum_dt) for i in range(nbank)]
        self.bkeys = [f"{name}_pm{i}" for i in range(nbank)]
        self.nbank = nbank
        self.g = 0

    def group(self, mms, evac, reads=(), evac_eng=None, evac_writes=(), evac_reads=()):
        p = self.p
        g = self.g
        b = g % self.nbank
        bank = self.banks[b]
        bkey = self.bkeys[b]
        eng = evac_eng or ("vector" if g % 2 == 0 else "scalar")
        n = len(mms)
        for i, mm in enumerate(mms):
            p.op("tensor",
                 (lambda t, mm=mm, bank=bank, s=(i == 0), e=(i == n - 1):
                  mm(t, bank[:], s, e)),
                 reads=reads, writes=(bkey,))
        ev = p.op(eng, lambda e, bank=bank, evac=evac: evac(e, bank[:]),
                  reads=(bkey,) + tuple(evac_reads), writes=tuple(evac_writes))
        self.g += 1
        return ev


def feat_ln(p, mr, name, xb, xbkey, sq, rows, Ab, Bb, onesb, ones1, cc, eps_key,
            T=TPC, nchunk=8):
    nt = T // 512
    sqkey = name + "sq"
    for ec in range(nchunk):
        p.op("scalar", lambda s, ec=ec: s.activation(sq[:, ec, :], xb[:, ec, :], AF.Square),
             reads=(xbkey,), writes=(sqkey,))
    for tc in range(nt):
        mms = [(lambda t, ps, s, e, ec=ec, tc=tc:
                t.matmul(ps[0:1, :], onesb[:], xb[:, ec, tc * 512:(tc + 1) * 512],
                         start=s, stop=e)) for ec in range(nchunk)]
        mr.group(mms, lambda e, ps, tc=tc: copy_any(e, rows[0:1, 0, tc * 512:(tc + 1) * 512], ps[0:1, :]),
                 reads=(xbkey, "onesb"), evac_writes=(name + "r0",))
        mms = [(lambda t, ps, s, e, ec=ec, tc=tc:
                t.matmul(ps[0:1, :], onesb[:], sq[:, ec, tc * 512:(tc + 1) * 512],
                         start=s, stop=e)) for ec in range(nchunk)]
        mr.group(mms, lambda e, ps, tc=tc: copy_any(e, rows[0:1, 1, tc * 512:(tc + 1) * 512], ps[0:1, :]),
                 reads=(name + "sq", "onesb"), evac_writes=(name + "r1",))
    nd = float(nchunk * 128)
    p.op("vector", lambda v: v.tensor_scalar_mul(rows[0:1, 0, :], rows[0:1, 0, :], 1.0 / nd),
         reads=(name + "r0",), writes=(name + "r0",))
    p.op("scalar", lambda s: s.activation(rows[0:1, 2, :], rows[0:1, 0, :], AF.Square),
         reads=(name + "r0",), writes=(name + "r2",))
    p.op("vector", lambda v: v.tensor_scalar_mul(rows[0:1, 1, :], rows[0:1, 1, :], 1.0 / nd),
         reads=(name + "r1",), writes=(name + "r1",))
    p.op("vector", lambda v: v.tensor_sub(rows[0:1, 1, :], rows[0:1, 1, :], rows[0:1, 2, :]),
         reads=(name + "r1", name + "r2"), writes=(name + "r1",))
    p.op("scalar", lambda s: s.activation(rows[0:1, 1, :], rows[0:1, 1, :], AF.Ln,
                                          bias=cc[eps_key][0:1, :]),
         reads=(name + "r1", "constc"), writes=(name + "r1",))
    p.op("scalar", lambda s: s.activation(rows[0:1, 1, :], rows[0:1, 1, :], AF.Exp,
                                          scale=-0.5),
         reads=(name + "r1",), writes=(name + "r1",))
    p.op("vector", lambda v: v.tensor_mul(rows[0:1, 2, :], rows[0:1, 0, :], rows[0:1, 1, :]),
         reads=(name + "r0", name + "r1"), writes=(name + "r2",))
    p.op("vector", lambda v: v.tensor_scalar_mul(rows[0:1, 2, :], rows[0:1, 2, :], -1.0),
         reads=(name + "r2",), writes=(name + "r2",))
    for tc in range(nt):
        mms = [(lambda t, ps, s, e, tc=tc:
                t.matmul(ps, ones1[:], rows[0:1, 1, tc * 512:(tc + 1) * 512],
                         start=True, stop=True))]
        mr.group(mms, lambda e, ps, tc=tc: copy_any(e, Ab[:, tc * 512:(tc + 1) * 512], ps),
                 reads=(name + "r1", "ones1"), evac_writes=(name + "A",))
        mms = [(lambda t, ps, s, e, tc=tc:
                t.matmul(ps, ones1[:], rows[0:1, 2, tc * 512:(tc + 1) * 512],
                         start=True, stop=True))]
        mr.group(mms, lambda e, ps, tc=tc: copy_any(e, Bb[:, tc * 512:(tc + 1) * 512], ps),
                 reads=(name + "r2", "ones1"), evac_writes=(name + "B",))


def feat_ln_apply(p, name, xb, xbkey, Ab, Bb, scr, dst, dstkey, T=TPC, nchunk=8,
                  gcol=None, bcol=None, dst2=None, dst2key=None):
    W = scr.shape[2]
    for ec in range(nchunk):
        eng = "vector" if ec % 2 == 0 else "gpsimd"
        for c0 in range(0, T, W):
            cs = slice(c0, c0 + W)
            p.op(eng, lambda v, ec=ec, cs=cs: v.tensor_tensor(
                scr[:, ec % 2, :], xb[:, ec, cs], Ab[:, cs], ALU.mult),
                reads=(xbkey, name + "A"), writes=(f"{name}scr{ec % 2}",))
            if gcol is None:
                p.op(eng, lambda v, ec=ec, cs=cs: v.tensor_tensor(
                    dst[:, ec, cs], scr[:, ec % 2, :], Bb[:, cs], ALU.add),
                    reads=(f"{name}scr{ec % 2}", name + "B"), writes=(dstkey,))
            else:
                p.op(eng, lambda v, ec=ec, cs=cs: v.tensor_tensor(
                    scr[:, ec % 2, :], scr[:, ec % 2, :], Bb[:, cs], ALU.add),
                    reads=(f"{name}scr{ec % 2}", name + "B"), writes=(f"{name}scr{ec % 2}",))
                p.op(eng, lambda v, ec=ec, cs=cs: v.tensor_scalar(
                    dst[:, ec, cs], scr[:, ec % 2, :], gcol(ec), bcol(ec), ALU.mult, ALU.add),
                    reads=(f"{name}scr{ec % 2}", "gcols"), writes=(dstkey,))
            if dst2 is not None:
                p.op(eng, lambda v, ec=ec, cs=cs: v.tensor_copy(dst2[:, ec, cs], dst[:, ec, cs]),
                     reads=(dstkey,), writes=(dst2key,))



def build_l1():
    p = Prog()
    x = p.din("x", [TPC, C], F32)
    wqT = p.din("wqT", [C, C], F32)
    wkT = p.din("wkT", [C, C], F32)
    wvT = p.din("wvT", [C, C], F32)
    identd = p.din("identbf", [128, 128], BF16)
    q_f = p.dout("q_f", [C, TPC], BF16)
    k_f = p.dout("k_f", [C, TPC], BF16)
    v_t = p.dout("v_t", [TPC, C], BF16)

    xs = p.sb("xs", [128, NTILE, C], F32)
    w3 = p.sb("w3", [128, 3, 8, C], BF16)
    ident = p.sb("ident", [128, 128], BF16)
    xln = p.sb("xln", [128, 2, C], BF16)
    xT = p.sb("xT", [128, 8, TPC], BF16)
    ost = p.sb("ost", [128, 4, 512], BF16)
    stats = p.sb("stats", [128, 8], F32)
    xc = p.sb("xc", [128, 2, C], F32)
    cc = p.const_cols({"eps": 1e-5})

    ptr_full = [p.ps(f"ptr{i}", [128, 1024], BF16) for i in range(2)]
    ptr = [t[:, 0:128] for t in ptr_full]

    p.dma("sync", xs[:], x.rearrange("(t p) d -> p t d", p=128), writes=("xs",))
    p.dma("sync", ident[:], identd, writes=("ident",))
    for i, w in enumerate((wqT, wkT, wvT)):
        p.dma("gpsimd", w3[:, i], w.rearrange("(c p) e -> p c e", p=128), writes=("w3",))

    for i in range(NTILE):
        b = i % 2
        p.op("vector", lambda v, i=i: v.reduce_sum(stats[:, 7:8], xs[:, i, :], axis=AX),
             reads=("xs",), writes=("st_m",))
        p.op("vector", lambda v: v.tensor_scalar_mul(stats[:, 7:8], stats[:, 7:8], 1.0 / C),
             reads=("st_m",), writes=("st_m",))
        p.op("vector", lambda v, i=i, b=b: v.tensor_scalar(
            xc[:, b, :], xs[:, i, :], stats[:, 7:8], None, ALU.subtract),
            reads=("xs", "st_m"), writes=(f"xc{b}",))
        p.op("scalar", lambda s, b=b: s.activation(xc[:, b, :], xc[:, b, :], AF.Square,
                                                   accum_out=stats[:, 6:7]),
             reads=(f"xc{b}",), writes=(f"xc{b}", "st_v"))
        p.op("scalar", lambda s: s.activation(stats[:, 6:7], stats[:, 6:7], AF.Ln,
                                              scale=1.0 / C, bias=cc["eps"]),
             reads=("st_v", "constc"), writes=("st_v",))
        p.op("scalar", lambda s: s.activation(stats[:, 6:7], stats[:, 6:7], AF.Exp,
                                              scale=-0.5),
             reads=("st_v",), writes=("st_v",))
        p.op("vector", lambda v, i=i, b=b: v.tensor_scalar(
            xln[:, b, :], xs[:, i, :], stats[:, 7:8], stats[:, 6:7],
            ALU.subtract, ALU.mult),
            reads=("xs", "st_m", "st_v"), writes=(f"xln{b}",))
        for c8 in range(8):
            k = i * 8 + c8
            p.op("tensor", lambda t, b=b, c8=c8, k=k: t.transpose(
                ptr[k % 2], xln[:, b, c8 * 128:(c8 + 1) * 128], ident[:]),
                reads=(f"xln{b}", "ident"), writes=(f"ptr{k % 2}",))
            p.op("vector", lambda v, i=i, c8=c8, k=k: v.tensor_copy(
                xT[:, c8, i * 128:(i + 1) * 128], ptr[k % 2]),
                reads=(f"ptr{k % 2}",), writes=("xT",))

    mr = MMRunner(p, "l1")

    def emit_group(mms, dram_ap):
        slot = mr.g % 4
        dst = ost[:, slot, :]
        ev = mr.group(mms, lambda e, ps, dst=dst: copy_any(e, dst, ps),
                      reads=("xT", "w3"), evac_writes=(f"ost{slot}",))
        p.dma("sync", dram_ap, dst, reads=(f"ost{slot}",), ch=1)

    for tc in range(4):
        for w_idx, outap in ((0, q_f), (1, k_f)):
            for ec in range(8):
                mms = [
                    (lambda t, ps, s, e, dc=dc, w_idx=w_idx, ec=ec, tc=tc:
                     t.matmul(ps, w3[:, w_idx, dc, ec * 128:(ec + 1) * 128],
                              xT[:, dc, tc * 512:(tc + 1) * 512], start=s, stop=e))
                    for dc in range(8)
                ]
                emit_group(mms, outap[ec * 128:(ec + 1) * 128, tc * 512:(tc + 1) * 512])
    for ti in range(NTILE):
        for eh in range(2):
            mms = [
                (lambda t, ps, s, e, dc=dc, ti=ti, eh=eh:
                 t.matmul(ps, xT[:, dc, ti * 128:(ti + 1) * 128],
                          w3[:, 2, dc, eh * 512:(eh + 1) * 512], start=s, stop=e))
                for dc in range(8)
            ]
            emit_group(mms, v_t[ti * 128:(ti + 1) * 128, eh * 512:(eh + 1) * 512])

    return p.finish()


S = 8192
NQT = S // 128


def build_l2(Sv=S):
    p = Prog()
    S_ = Sv
    qh = p.din("qh", [256, S_], BF16)
    kh = p.din("kh", [256, S_], BF16)
    vh = p.din("vh", [S_, 256], BF16)
    masks4d = p.din("masks4", [128, 4 * 512], BF16)
    identfd = p.din("identf", [128, 128], F32)
    attn = p.dout("attn", [S_, 256], BF16)

    M0 = 20.0
    NBLK = S_ // 512
    NSL = 16
    qs2 = p.sb("qs2", [128, 4, S_], BF16)
    ks2 = p.sb("ks2", [128, 4, S_], BF16)
    vse = p.sb("vse", [128, S_ // 128, 260], BF16)
    masks4 = p.sb("masks4s", [128, 4, 512], BF16)
    identf = p.sb("identfs", [128, 128], F32)
    probsT = p.sb("probsT", [128, 2, NSL, 512], BF16)
    oev = p.sb("oev", [65, 2, 128], F32)
    rlc = p.sb("rlc", [128, 2, 1], F32)
    ostage = p.sb("ostage", [128, 2, 64], BF16)
    cc = p.const_cols({"negm": -M0})

    ps_s = [[p.ps(f"l2s{ln}_{j}", [128, 512], F32) for j in range(3)]
            for ln in range(2)]
    ps_a = [p.ps(f"l2a{ln}", [128, 512], F32) for ln in range(2)]

    qhR = qh.rearrange("(c p) t -> p c t", p=64)
    khR = kh.rearrange("(c p) t -> p c t", p=64)
    p.op("vector", lambda v: v.memset(qs2[64:128, :, :], 0.0), writes=("qs2z",))
    for hi in range(4):
        p.dma("sync", qs2[0:64, hi, :], qhR[:, hi, :], writes=(f"qs2{hi}",))
        p.dma("sync", ks2[0:64, hi, :], khR[:, hi, :], writes=(f"ks2{hi}",), ch=3)
        p.dma("gpsimd", ks2[64:128, hi, :], khR[:, hi, :],
              writes=(f"ks2{hi}",), ch=1)
    for h4 in range(4):
        p.dma("sync", vse[:, :, h4 * 65:h4 * 65 + 64],
              vh.rearrange("(t p) e -> p t e", p=128)[:, :, h4 * 64:(h4 + 1) * 64],
              writes=(f"vse{h4}",), ch=1)
    p.op("vector", lambda v: v.memset(
        vse[:].rearrange("p t (h c) -> p t h c", c=65)[:, :, :, 64:65], 1.0),
        writes=("vse1",))
    p.dma("sync", masks4[:], masks4d.rearrange("p (t q) -> p t q", q=512),
          writes=("masks4",), ch=2)
    p.dma("sync", identf[:], identfd, writes=("identf",), ch=2)

    SCALE = 64.0 ** -0.5

    def finalize_tile(h, i, ln, PA, tq):
        p.op("vector", lambda v, ln=ln, tq=tq: v.tensor_copy(
            oev[0:65, ln, :], ps_a[ln][0:65, tq * 128:(tq + 1) * 128]),
            reads=(PA,), writes=(f"oev{ln}",))
        PS0 = f"l2s{ln}_0"
        p.op("tensor", lambda t, ln=ln: t.transpose(
            ps_s[ln][0][0:128, 0:65], oev[0:65, ln, :], identf[0:65, 0:65]),
            reads=(f"oev{ln}", "identf"), writes=(PS0,))
        p.op("vector", lambda v, ln=ln: v.reciprocal(
            rlc[:, ln, :], ps_s[ln][0][0:128, 64:65]),
            reads=(PS0,), writes=(f"rlc{ln}",))
        p.op("vector", lambda v, ln=ln: v.tensor_scalar(
            ostage[:, ln, :], ps_s[ln][0][0:128, 0:64], rlc[:, ln, :],
            None, ALU.mult),
            reads=(PS0, f"rlc{ln}"), writes=(f"ostage{ln}",))
        p.dma("sync", attn[i * 128:(i + 1) * 128, h * 64:(h + 1) * 64],
              ostage[:, ln, :], reads=(f"ostage{ln}",), ch=ln)

    def emit_qk_batch(h, ln, B, kb, n):
        q0 = B * 512
        for j in range(n):
            ksb = kb + j
            sb_i = ksb % 3
            PS = f"l2s{ln}_{sb_i}"
            sl = ksb % NSL
            PT = f"pT{ln}_{sl}"
            p.op("tensor", lambda t, h=h, ksb=ksb, q0=q0, ln=ln, sb_i=sb_i:
                 t.matmul(ps_s[ln][sb_i][:, 0:512],
                          ks2[0:128, h, ksb * 128:ksb * 128 + 128],
                          qs2[0:128, h, q0:q0 + 512],
                          start=True, stop=True),
                 reads=(f"ks2{h}", f"qs2{h}", "qs2z"), writes=(PS,))
            p.op("scalar", lambda s, ln=ln, sb_i=sb_i, sl=sl: s.activation(
                probsT[:, ln, sl, :], ps_s[ln][sb_i][:, 0:512], AF.Exp,
                scale=SCALE, bias=cc["negm"]),
                reads=(PS, "constc"), writes=(PT,))
            t4 = ksb - 4 * B
            if t4 >= 0:
                p.op("vector", lambda g, ln=ln, sl=sl, t4=t4: g.tensor_tensor(
                    probsT[:, ln, sl, :], probsT[:, ln, sl, :],
                    masks4[:, t4, :], ALU.mult),
                    reads=(PT, "masks4"), writes=(PT,))

    def emit_pv_batch(h, ln, B, kb, n, nk):
        for j in range(n):
            ksb = kb + j
            sl = ksb % NSL
            PT = f"pT{ln}_{sl}"
            PA = f"l2pa{ln}"
            p.op("tensor", lambda t, ksb=ksb, h=h, ln=ln, sl=sl,
                 first=(ksb == 0), last=(ksb == nk - 1):
                 t.matmul(ps_a[ln][0:65, 0:512],
                          vse[:, ksb, h * 65:(h + 1) * 65],
                          probsT[:, ln, sl, :],
                          start=first, stop=last),
                 reads=(f"vse{h}", "vse1", PT), writes=(PA,))

    for hp in range(2):
        p.begin_lanes(2)
        pend = [None, None]
        for B in range(NBLK):
            for ln in range(2):
                h = 2 * hp + ln
                p.lane(ln)
                nk = 4 * B + 4
                batches = [(kb, min(8, nk - kb)) for kb in range(0, nk, 8)]
                PA = f"l2pa{ln}"
                emit_qk_batch(h, ln, B, *batches[0])
                if pend[ln] is not None:
                    ph, pB = pend[ln]
                    for tq in range(4):
                        finalize_tile(ph, 4 * pB + tq, ln, PA, tq)
                for bi in range(1, len(batches)):
                    emit_qk_batch(h, ln, B, *batches[bi])
                    emit_pv_batch(h, ln, B, *batches[bi - 1], nk)
                emit_pv_batch(h, ln, B, *batches[-1], nk)
                pend[ln] = (h, B)
        for ln in range(2):
            p.lane(ln)
            ph, pB = pend[ln]
            PA = f"l2pa{ln}"
            for tq in range(4):
                finalize_tile(ph, 4 * pB + tq, ln, PA, tq)
        p.end_lanes()
    return p.finish()


def build_l3a():
    p = Prog()
    atT = p.din("atT", [C, TPC], BF16)
    xT = p.din("xT", [C, TPC], F32)
    woT = p.din("woT", [C, C], F32)
    x1T = p.dout("x1T", [C, TPC], F32)
    xln1T = p.dout("xln1T", [C, TPC], BF16)

    wos = p.sb("wos", [128, 8, C], BF16)
    att = p.sb("att", [128, 2, 8, 512], BF16)
    xtt = p.sb("xtt", [128, 2, 8, 512], F32)
    x1b = p.sb("x1b", [128, 8, TPC], BF16)
    xln = p.sb("xlnb", [128, 8, TPC], BF16)
    sq = xln
    Ab = p.sb("Ab", [128, TPC], F32)
    Bb = p.sb("Bb", [128, TPC], F32)
    rows = p.sb("rows", [1, 3, TPC], F32)
    st1 = p.sb("st1", [128, 2, 512], F32)
    scr = p.sb("scr", [128, 2, TPC], F32)
    onesb = p.sb("onesb", [128, 1], BF16)
    ones1 = p.sb("ones1", [1, 128], F32)
    cc = p.const_cols({"eps5": 1e-5})
    p.op("gpsimd", lambda g: g.memset(onesb[:], 1.0), writes=("onesb",))
    p.op("gpsimd", lambda g: g.memset(ones1[:], 1.0), writes=("ones1",))

    p.dma("gpsimd", wos[:], woT.rearrange("(c p) e -> p c e", p=128), writes=("wos",))
    mr = MMRunner(p, "l3a")
    for tc in range(4):
        tb = tc % 2
        p.dma("sync", att[:, tb], atT.rearrange("(c p) t -> p c t", p=128)[:, :, tc * 512:(tc + 1) * 512],
              writes=(f"att{tb}",))
        p.dma("sync", xtt[:, tb], xT.rearrange("(c p) t -> p c t", p=128)[:, :, tc * 512:(tc + 1) * 512],
              writes=(f"xtt{tb}",), ch=1)
        for hc in range(8):
            mms = [(lambda t, ps, s, e, ec=ec, hc=hc, tb=tb:
                    t.matmul(ps, wos[:, ec, hc * 128:(hc + 1) * 128], att[:, tb, ec, :],
                             start=s, stop=e)) for ec in range(8)]
            sslot = mr.g % 2
            ev = mr.group(mms, lambda e, ps, hc=hc, tb=tb, sslot=sslot:
                          e.tensor_tensor(st1[:, sslot, :], ps, xtt[:, tb, hc, :], ALU.add),
                          reads=("wos", f"att{tb}"),
                          evac_reads=(f"xtt{tb}",),
                          evac_writes=(f"st1{sslot}",), evac_eng="vector")
            p.dma("sync", x1T[hc * 128:(hc + 1) * 128, tc * 512:(tc + 1) * 512],
                  st1[:, sslot, :], reads=(f"st1{sslot}",), ch=2)
            p.op("scalar", lambda s, hc=hc, tc=tc, sslot=sslot: s.copy(
                x1b[:, hc, tc * 512:(tc + 1) * 512], st1[:, sslot, :]),
                reads=(f"st1{sslot}",), writes=("x1b",))
    feat_ln(p, mr, "ln1", x1b, "x1b", sq, rows, Ab, Bb, onesb, ones1, cc, "eps5")
    feat_ln_apply(p, "ln1", x1b, "x1b", Ab, Bb, scr, xln, "ln1sq")
    for ec in range(8):
        p.dma("sync", xln1T[ec * 128:(ec + 1) * 128, :], xln[:, ec, :],
              reads=("ln1sq",), ch=2)
    return p.finish()


FH = 3072


def build_l3b():
    p = Prog()
    xlnT = p.din("xlnT", [C, TPC], BF16)
    wgT = p.din("wgT", [C, FH], I32)
    wuT = p.din("wuT", [C, FH], I32)
    hT = p.dout("hT", [FH, TPC], BF16)

    xln = p.sb("xln", [128, 8, TPC], BF16)
    wg = p.sb("wg", [128, 8, FH], BF16)
    wu = p.sb("wu", [128, 8, FH], BF16)
    sg = p.sb("sg", [128, 2, 512], BF16)
    hst = p.sb("hst", [128, 2, 512], BF16)

    p.dma("sync", xln[:], xlnT.rearrange("(c p) t -> p c t", p=128), writes=("xln",))
    p.dma("gpsimd", wg[:], wgT.rearrange("(c p) h -> p c h", p=128), writes=("wg",))
    p.dma("gpsimd", wu[:], wuT.rearrange("(c p) h -> p c h", p=128), writes=("wu",))

    mrs = [MMRunner(p, "l3bA", nbank=2), MMRunner(p, "l3bB", nbank=2)]
    SC = 1.0 / 127.0
    p.begin_lanes(2)
    for hc in range(24):
        slot = hc % 2
        p.lane(slot)
        mr = mrs[slot]
        for tc in range(4):
            gm = [(lambda t, ps, s, e, ec=ec, hc=hc, tc=tc:
                   t.matmul(ps, wg[:, ec, hc * 128:(hc + 1) * 128],
                            xln[:, ec, tc * 512:(tc + 1) * 512], start=s, stop=e))
                  for ec in range(8)]
            mr.group(gm, lambda e, ps, slot=slot: e.activation(
                sg[:, slot, :], ps, AF.Silu, scale=SC),
                reads=("wg", "xln"), evac_writes=(f"sg{slot}",), evac_eng="scalar")
            um = [(lambda t, ps, s, e, ec=ec, hc=hc, tc=tc:
                   t.matmul(ps, wu[:, ec, hc * 128:(hc + 1) * 128],
                            xln[:, ec, tc * 512:(tc + 1) * 512], start=s, stop=e))
                  for ec in range(8)]
            mr.group(um, lambda e, ps, slot=slot: e.scalar_tensor_tensor(
                hst[:, slot, :], ps, SC, sg[:, slot, :], ALU.mult, ALU.mult),
                reads=("wu", "xln"), evac_reads=(f"sg{slot}",),
                evac_writes=(f"hst{slot}",), evac_eng="vector")
            p.dma("sync", hT[hc * 128:(hc + 1) * 128, tc * 512:(tc + 1) * 512],
                  hst[:, slot, :], reads=(f"hst{slot}",), ch=1 + slot)
    p.end_lanes()
    return p.finish()


def build_l3ab():
    p = Prog()
    atT = p.din("atT", [C, TPC], BF16)
    xT = p.din("xT", [C, TPC], F32)
    woT = p.din("woT", [C, C], F32)
    wgT = p.din("wgT", [C, FH], I32)
    wuT = p.din("wuT", [C, FH], I32)
    x1T = p.dout("x1T", [C, TPC], F32)
    hT = p.dout("hT", [FH, TPC], BF16)

    wos = p.sb("wos", [128, 8, C], BF16)
    att = p.sb("att", [128, 2, 8, 512], BF16)
    xtt = p.sb("xtt", [128, 2, 8, 512], F32)
    x1b = p.sb("x1b", [128, 8, TPC], BF16)
    xln = p.sb("xlnb", [128, 8, TPC], BF16)
    sq = xln
    Ab = p.sb("Ab", [128, TPC], F32)
    Bb = p.sb("Bb", [128, TPC], F32)
    rows = p.sb("rows", [1, 3, TPC], F32)
    st1 = p.sb("st1", [128, 2, 512], F32)
    scr = xtt[:].rearrange("p a e t -> p (a e) t")[:, 0:8, :].rearrange(
        "p (a u) t -> p a (u t)", a=2)
    onesb = p.sb("onesb", [128, 1], BF16)
    ones1 = p.sb("ones1", [1, 128], F32)
    wg = p.sb("wg", [128, 2, 2, 8, 128], BF16)
    wu = p.sb("wu", [128, 2, 2, 8, 128], BF16)
    sg = p.sb("sg", [128, 2, 512], BF16)
    hst = p.sb("hst", [128, 2, 512], BF16)
    cc = p.const_cols({"eps5": 1e-5})
    p.op("gpsimd", lambda g: g.memset(onesb[:], 1.0), writes=("onesb",))
    p.op("gpsimd", lambda g: g.memset(ones1[:], 1.0), writes=("ones1",))

    wgR = wgT.rearrange("(c p) h -> p c h", p=128)
    wuR = wuT.rearrange("(c p) h -> p c h", p=128)

    def ld_w(hc):
        ln_, bf = hc % 2, (hc // 2) % 2
        p.dma("gpsimd", wg[:, ln_, bf], wgR[:, :, hc * 128:(hc + 1) * 128],
              writes=(f"wg{ln_}{bf}",), ch=1)
        p.dma("gpsimd", wu[:, ln_, bf], wuR[:, :, hc * 128:(hc + 1) * 128],
              writes=(f"wu{ln_}{bf}",), ch=2)

    p.dma("gpsimd", wos[:], woT.rearrange("(c p) e -> p c e", p=128), writes=("wos",))
    for hc0 in range(4):
        ld_w(hc0)

    mr = MMRunner(p, "l3a")
    for tc in range(4):
        tb = tc % 2
        p.dma("sync", att[:, tb], atT.rearrange("(c p) t -> p c t", p=128)[:, :, tc * 512:(tc + 1) * 512],
              writes=(f"att{tb}",))
        p.dma("sync", xtt[:, tb], xT.rearrange("(c p) t -> p c t", p=128)[:, :, tc * 512:(tc + 1) * 512],
              writes=(f"xtt{tb}",), ch=1)
        for hc in range(8):
            mms = [(lambda t, ps, s, e, ec=ec, hc=hc, tb=tb:
                    t.matmul(ps, wos[:, ec, hc * 128:(hc + 1) * 128], att[:, tb, ec, :],
                             start=s, stop=e)) for ec in range(8)]
            sslot = mr.g % 2
            ev = mr.group(mms, lambda e, ps, hc=hc, tb=tb, sslot=sslot:
                          e.tensor_tensor(st1[:, sslot, :], ps, xtt[:, tb, hc, :], ALU.add),
                          reads=("wos", f"att{tb}"),
                          evac_reads=(f"xtt{tb}",),
                          evac_writes=(f"st1{sslot}",), evac_eng="vector")
            p.dma("sync", x1T[hc * 128:(hc + 1) * 128, tc * 512:(tc + 1) * 512],
                  st1[:, sslot, :], reads=(f"st1{sslot}",), ch=2)
            p.op("scalar", lambda s, hc=hc, tc=tc, sslot=sslot: s.copy(
                x1b[:, hc, tc * 512:(tc + 1) * 512], st1[:, sslot, :]),
                reads=(f"st1{sslot}",), writes=("x1b",))
    feat_ln(p, mr, "ln1", x1b, "x1b", sq, rows, Ab, Bb, onesb, ones1, cc, "eps5")
    feat_ln_apply(p, "ln1", x1b, "x1b", Ab, Bb, scr, xln, "ln1sq")

    mrs = [MMRunner(p, "l3bA", nbank=2), MMRunner(p, "l3bB", nbank=2)]
    SC = 1.0 / 127.0
    p.begin_lanes(2)
    for hc in range(24):
        slot = hc % 2
        p.lane(slot)
        mr2 = mrs[slot]
        bf = (hc // 2) % 2
        for tc in range(4):
            gm = [(lambda t, ps, s, e, ec=ec, tc=tc, slot=slot, bf=bf:
                   t.matmul(ps, wg[:, slot, bf, ec, :],
                            xln[:, ec, tc * 512:(tc + 1) * 512], start=s, stop=e))
                  for ec in range(8)]
            mr2.group(gm, lambda e, ps, slot=slot: e.activation(
                sg[:, slot, :], ps, AF.Silu, scale=SC),
                reads=(f"wg{slot}{bf}", "ln1sq"),
                evac_writes=(f"sg{slot}",), evac_eng="scalar")
            um = [(lambda t, ps, s, e, ec=ec, tc=tc, slot=slot, bf=bf:
                   t.matmul(ps, wu[:, slot, bf, ec, :],
                            xln[:, ec, tc * 512:(tc + 1) * 512], start=s, stop=e))
                  for ec in range(8)]
            mr2.group(um, lambda e, ps, slot=slot: e.scalar_tensor_tensor(
                hst[:, slot, :], ps, SC, sg[:, slot, :], ALU.mult, ALU.mult),
                reads=(f"wu{slot}{bf}", "ln1sq"), evac_reads=(f"sg{slot}",),
                evac_writes=(f"hst{slot}",), evac_eng="vector")
            p.dma("sync", hT[hc * 128:(hc + 1) * 128, tc * 512:(tc + 1) * 512],
                  hst[:, slot, :], reads=(f"hst{slot}",), ch=3 + slot)
        if hc + 4 < 24:
            ld_w(hc + 4)
    p.end_lanes()
    return p.finish()


def build_l3c():
    p = Prog()
    hTd = p.din("hT", [FH, TPC], BF16)
    x1T = p.din("x1T", [C, TPC], F32)
    wdT = p.din("wdT", [FH, C], I32)
    x2T = p.dout("x2T", [C, TPC], F32)
    xln2T = p.dout("xln2T", [C, TPC], BF16)

    wd = p.sb("wd", [128, 24, C], BF16)
    ht = p.sb("ht", [128, 1, 24, 512], BF16)
    x1t = p.sb("x1t", [128, 1, 8, 512], F32)
    x2b = p.sb("x2b", [128, 8, TPC], BF16)
    xln = p.sb("xlnb", [128, 8, TPC], BF16)
    sq = xln
    Ab = p.sb("Ab", [128, TPC], F32)
    Bb = p.sb("Bb", [128, TPC], F32)
    rows = p.sb("rows", [1, 3, TPC], F32)
    st1 = p.sb("st1", [128, 2, 512], F32)
    scr = p.sb("scr", [128, 2, TPC // 2], F32)
    onesb = p.sb("onesb", [128, 1], BF16)
    ones1 = p.sb("ones1", [1, 128], F32)
    cc = p.const_cols({"eps5": 1e-5})
    p.op("gpsimd", lambda g: g.memset(onesb[:], 1.0), writes=("onesb",))
    p.op("gpsimd", lambda g: g.memset(ones1[:], 1.0), writes=("ones1",))

    wdR = wdT.rearrange("(c p) e -> p c e", p=128)
    for ecw in range(8):
        p.dma("gpsimd", wd[:, :, ecw * 128:(ecw + 1) * 128],
              wdR[:, :, ecw * 128:(ecw + 1) * 128],
              writes=(f"wd{ecw}",), ch=ecw % 3)
    mr = MMRunner(p, "l3c")
    SC = 1.0 / 127.0
    for tc in range(4):
        tb = 0
        p.dma("sync", ht[:, tb], hTd.rearrange("(c p) t -> p c t", p=128)[:, :, tc * 512:(tc + 1) * 512],
              writes=(f"ht{tb}",))
        p.dma("sync", x1t[:, tb], x1T.rearrange("(c p) t -> p c t", p=128)[:, :, tc * 512:(tc + 1) * 512],
              writes=(f"x1t{tb}",), ch=1)
        for ec in range(8):
            mms = [(lambda t, ps, s, e, hc=hc, ec=ec, tb=tb:
                    t.matmul(ps, wd[:, hc, ec * 128:(ec + 1) * 128], ht[:, tb, hc, :],
                             start=s, stop=e)) for hc in range(24)]
            sslot = mr.g % 2
            mr.group(mms, lambda e, ps, ec=ec, tb=tb, sslot=sslot:
                     e.scalar_tensor_tensor(st1[:, sslot, :], ps, SC, x1t[:, tb, ec, :],
                                            ALU.mult, ALU.add),
                     reads=(f"wd{ec}", f"ht{tb}"), evac_reads=(f"x1t{tb}",),
                     evac_writes=(f"st1{sslot}",), evac_eng="vector")
            p.dma("sync", x2T[ec * 128:(ec + 1) * 128, tc * 512:(tc + 1) * 512],
                  st1[:, sslot, :], reads=(f"st1{sslot}",), ch=2)
            p.op("scalar", lambda s, ec=ec, tc=tc, sslot=sslot: s.copy(
                x2b[:, ec, tc * 512:(tc + 1) * 512], st1[:, sslot, :]),
                reads=(f"st1{sslot}",), writes=("x2b",))
    feat_ln(p, mr, "ln2", x2b, "x2b", sq, rows, Ab, Bb, onesb, ones1, cc, "eps5")
    feat_ln_apply(p, "ln2", x2b, "x2b", Ab, Bb, scr, xln, "ln2sq")
    for ec in range(8):
        p.dma("sync", xln2T[ec * 128:(ec + 1) * 128, :], xln[:, ec, :],
              reads=("ln2sq",), ch=2)
    return p.finish()


E1 = 1228
NHC = 10


def hc_w(hc):
    return 128 if hc < 9 else E1 - 9 * 128


def build_l4a():
    p = Prog()
    xlnT = p.din("xlnT", [C, TPC], BF16)
    ins = {}
    for nm in ("r", "k", "v"):
        ins[nm] = (p.din(nm + "W1", [C, E1], F32),
                   p.din(nm + "b1c", [128, 9], F32),
                   p.din(nm + "b1t", [76, 1], F32),
                   p.din(nm + "W2", [E1, C], F32),
                   p.din(nm + "b2c", [128, 8], F32),
                   p.din(nm + "muc", [128, 8], F32))
    outs = {nm: p.dout(nm + "_f", [C, TPC], BF16) for nm in ("r", "k", "v")}

    xln = p.sb("xln", [128, 8, TPC], BF16)
    w1 = p.sb("w1", [128, 2, 8, E1], BF16)
    w2 = p.sb("w2", [128, 2, NHC, C], BF16)
    b1 = p.sb("b1", [128, 2, NHC], F32)
    b2 = p.sb("b2", [128, 2, 8], F32)
    muc = p.sb("muc", [128, 2, 8], BF16)
    r1 = p.sb("r1", [128, NHC, TPC], BF16)
    ost = p.sb("ost", [128, 2, 512], BF16)

    p.dma("sync", xln[:], xlnT.rearrange("(c p) t -> p c t", p=128), writes=("xln",))
    mr = MMRunner(p, "l4a")

    def ld_mlp(mi):
        nm = ("r", "k", "v")[mi]
        sl = mi % 2
        W1d, b1d, b1td, W2d, b2d, mud = ins[nm]
        p.dma("gpsimd", w1[:, sl], W1d.rearrange("(c p) h -> p c h", p=128),
              writes=(f"w1{sl}",), ch=0)
        p.dma("gpsimd", w2[0:76, sl, 9], W2d[9 * 128:E1, :], writes=(f"w2{sl}",), ch=1)
        p.dma("gpsimd", w2[:, sl, 0:9], W2d[0:9 * 128, :].rearrange("(c p) e -> p c e", p=128),
              writes=(f"w2{sl}",), ch=1)
        p.dma("sync", b1[:, sl, 0:9], b1d, writes=(f"b1{sl}",), ch=1)
        p.dma("sync", b1[0:76, sl, 9:10], b1td, writes=(f"b1{sl}",), ch=1)
        p.dma("sync", b2[:, sl], b2d, writes=(f"b2{sl}",), ch=1)
        p.dma("gpsimd", muc[:, sl], mud, writes=(f"muc{sl}",), ch=2)

    ld_mlp(0)
    for mi, nm in enumerate(("r", "k", "v")):
        sl = mi % 2
        if mi + 1 < 3:
            ld_mlp(mi + 1)
        for hc in range(NHC):
            w = hc_w(hc)
            mms = [(lambda t, ps, s, e, dc=dc, hc=hc, w=w, sl=sl:
                    t.matmul(ps[0:w, 0:1], w1[:, sl, dc, hc * 128:hc * 128 + w],
                             muc[:, sl, dc:dc + 1], start=s, stop=e)) for dc in range(8)]
            mr.group(mms, lambda e, ps, hc=hc, w=w, sl=sl: e.tensor_tensor(
                b1[0:w, sl, hc:hc + 1], b1[0:w, sl, hc:hc + 1], ps[0:w, 0:1], ALU.add),
                reads=(f"w1{sl}", f"muc{sl}"), evac_writes=(f"b1{sl}",), evac_eng="vector")
        for hc in range(NHC):
            w = hc_w(hc)
            for tc in range(4):
                mms = [(lambda t, ps, s, e, dc=dc, hc=hc, tc=tc, w=w, sl=sl:
                        t.matmul(ps[0:w, :], w1[:, sl, dc, hc * 128:hc * 128 + w],
                                 xln[:, dc, tc * 512:(tc + 1) * 512], start=s, stop=e))
                       for dc in range(8)]
                mr.group(mms, lambda e, ps, hc=hc, tc=tc, w=w, sl=sl: e.activation(
                    r1[0:w, hc, tc * 512:(tc + 1) * 512], ps[0:w, :], AF.Gelu,
                    bias=b1[0:w, sl, hc:hc + 1]),
                    reads=(f"w1{sl}", "xln"), evac_reads=(f"b1{sl}",),
                    evac_writes=("r1",), evac_eng="scalar")
        for ec in range(8):
            for tc in range(4):
                mms = [(lambda t, ps, s, e, hc=hc, ec=ec, tc=tc, w=hc_w(hc), sl=sl:
                        t.matmul(ps, w2[0:w, sl, hc, ec * 128:(ec + 1) * 128],
                                 r1[0:w, hc, tc * 512:(tc + 1) * 512], start=s, stop=e))
                       for hc in range(NHC)]
                slot = mr.g % 2
                mr.group(mms, lambda e, ps, ec=ec, slot=slot, sl=sl: e.tensor_scalar(
                    ost[:, slot, :], ps, b2[:, sl, ec:ec + 1], None, ALU.add),
                    reads=(f"w2{sl}", "r1"), evac_reads=(f"b2{sl}",),
                    evac_writes=(f"ost{slot}",), evac_eng="vector")
                p.dma("sync", outs[nm][ec * 128:(ec + 1) * 128, tc * 512:(tc + 1) * 512],
                      ost[:, slot, :], reads=(f"ost{slot}",), ch=3)
    return p.finish()


DECAY_C = 0.7408182206817179


def build_l4b():
    p = Prog()
    xlnT = p.din("xlnT", [C, TPC], BF16)
    kT = p.din("kT", [C, TPC], BF16)
    lr = {}
    for nm in ("a", "d", "g"):
        lr[nm] = (p.din(nm + "A", [C, 128], F32), p.din(nm + "B", [128, C], F32),
                  p.din(nm + "bc", [128, 8], F32), p.din(nm + "muc", [128, 8], F32))
    remc_d = p.din("remc_d", [128, 8], F32)
    mixc_d = p.din("mixc_d", [128, 8], F32)
    sel2d = p.din("sel2_d", [2, 128], F32)
    hsel_d = p.din("hsel_d", [128, 2], F32)
    outs = {nm: p.dout(nm, [C, TPC], BF16) for nm in ("kap_f", "ktil_f", "rho_f", "gate_f")}
    d_f = p.dout("d_f", [C, TPC], F32)

    xln = p.sb("xln", [128, 8, TPC], BF16)
    kf = p.sb("kf", [128, 8, TPC], BF16)
    wA = p.sb("wA", [128, 8, 128], BF16)
    wB = p.sb("wB", [128, C], BF16)
    bias = p.sb("bias", [128, 8], F32)
    muc = p.sb("muc", [128, 8], BF16)
    m1 = p.sb("m1", [128, 1], BF16)
    u1 = p.sb("u1", [128, TPC], BF16)
    af = p.sb("af", [128, 8, TPC], BF16)
    df = p.sb("df", [128, 2, TPC], F32)
    remc = p.sb("remc", [128, 8], F32)
    mixc = p.sb("mixc", [128, 8], F32)
    omix = p.sb("omix", [128, 8], F32)
    sel2 = p.sb("sel2", [2, 128], F32)
    hsel = p.sb("hsel", [128, 2], F32)
    ns = p.sb("ns", [2, 2, TPC], F32)
    scr = p.sb("scr", [128, 4, 512], F32)
    scr4 = p.sb("scr4", [128, 4, 512], F32)
    ost = p.sb("ost", [128, 2, 512], BF16)
    ost2 = p.sb("ost2", [128, 6, 512], BF16)
    cc = p.const_cols({"tiny": 1e-24})

    p.dma("sync", xln[:], xlnT.rearrange("(c p) t -> p c t", p=128), writes=("xln",))
    p.dma("sync", kf[:], kT.rearrange("(c p) t -> p c t", p=128), writes=("kf",))
    p.dma("sync", sel2[:], sel2d, writes=("sel2",))
    p.dma("sync", hsel[:], hsel_d, writes=("hsel",))
    p.dma("sync", remc[:], remc_d, writes=("remc",), ch=1)
    p.dma("sync", mixc[:], mixc_d, writes=("mixc",), ch=1)
    p.op("vector", lambda v: v.tensor_scalar(omix[:], mixc[:], -1.0, 1.0, ALU.mult, ALU.add),
         reads=("mixc",), writes=("omix",))

    mr = MMRunner(p, "l4b")
    psK = [p.ps(f"l4bk{i}", [128, 512], F32) for i in range(2)]

    for nm in ("a", "d", "g"):
        Ad, Bd, bd, mud = lr[nm]
        p.dma("gpsimd", wA[:], Ad.rearrange("(c p) r -> p c r", p=128), writes=("wA",))
        p.dma("gpsimd", wB[:], Bd, writes=("wB",))
        p.dma("sync", bias[:], bd, writes=("bias",), ch=1)
        p.dma("gpsimd", muc[:], mud, writes=("muc",))
        mms = [(lambda t, ps, s, e, dc=dc:
                t.matmul(ps[:, 0:1], wA[:, dc, :], muc[:, dc:dc + 1], start=s, stop=e))
               for dc in range(8)]
        mr.group(mms, lambda e, ps: copy_any(e, m1[:], ps[:, 0:1]),
                 reads=("wA", "muc"), evac_writes=("m1",), evac_eng="vector")
        for ec in range(8):
            mms = [(lambda t, ps, s, e, ec=ec:
                    t.matmul(ps[:, 0:1], wB[:, ec * 128:(ec + 1) * 128], m1[:],
                             start=s, stop=e))]
            mr.group(mms, lambda e, ps, ec=ec: e.tensor_tensor(
                bias[:, ec:ec + 1], bias[:, ec:ec + 1], ps[:, 0:1], ALU.add),
                reads=("wB", "m1"), evac_writes=("bias",), evac_eng="vector")
        for tc in range(4):
            mms = [(lambda t, ps, s, e, dc=dc, tc=tc:
                    t.matmul(ps, wA[:, dc, :], xln[:, dc, tc * 512:(tc + 1) * 512],
                             start=s, stop=e)) for dc in range(8)]
            mr.group(mms, lambda e, ps, tc=tc: copy_any(e, u1[:, tc * 512:(tc + 1) * 512], ps),
                     reads=("wA", "xln"), evac_writes=("u1",), evac_eng="vector")
        for ec in range(8):
            for tc in range(4):
                mms = [(lambda t, ps, s, e, ec=ec, tc=tc:
                        t.matmul(ps, wB[:, ec * 128:(ec + 1) * 128],
                                 u1[:, tc * 512:(tc + 1) * 512], start=s, stop=e))]
                if nm == "a":
                    ev = (lambda e, ps, ec=ec, tc=tc: e.activation(
                        af[:, ec, tc * 512:(tc + 1) * 512], ps, AF.Sigmoid,
                        bias=bias[:, ec:ec + 1]))
                    wkey = "af"
                elif nm == "g":
                    ev = (lambda e, ps, ec=ec, tc=tc: e.activation(
                        ost[:, 0, :], ps, AF.Sigmoid, bias=bias[:, ec:ec + 1]))
                    wkey = "ost0"
                else:
                    ev = (lambda e, ps, ec=ec, tc=tc: e.activation(
                        scr[:, 2, :], ps, AF.Tanh, bias=bias[:, ec:ec + 1]))
                    wkey = "scr2"
                mr.group(mms, ev, reads=("wB", "u1"), evac_reads=("bias",),
                         evac_writes=(wkey,), evac_eng="scalar")
                if nm == "g":
                    p.dma("sync", outs["gate_f"][ec * 128:(ec + 1) * 128, tc * 512:(tc + 1) * 512],
                          ost[:, 0, :], reads=("ost0",), ch=2)
                elif nm == "d":
                    p.op("scalar", lambda s, ec=ec, tc=tc: s.activation(
                        scr[:, 2, :], scr[:, 2, :], AF.Sigmoid),
                        reads=("scr2",), writes=("scr2",))
                    p.op("scalar", lambda s, ec=ec, tc=tc: s.activation(
                        df[:, ec % 2, tc * 512:(tc + 1) * 512], scr[:, 2, :], AF.Exp,
                        scale=-DECAY_C),
                        reads=("scr2",), writes=(f"df{ec % 2}",))
            if nm == "d":
                p.dma("sync", d_f[ec * 128:(ec + 1) * 128, :], df[:, ec % 2, :],
                      reads=(f"df{ec % 2}",), ch=2)

    p.begin_lanes(2)
    for ec in range(8):
        p.lane(ec % 2)
        LN_ = ec % 2
        for tc in range(4):
            p.op("vector", lambda v, ec=ec, tc=tc, LN_=LN_: v.tensor_scalar(
                scr[:, 2 * LN_, :], kf[:, ec, tc * 512:(tc + 1) * 512],
                remc[:, ec:ec + 1], None, ALU.mult),
                reads=("kf", "remc"), writes=(f"scr0_{LN_}",))
            p.op("scalar", lambda s, LN_=LN_: s.activation(scr[:, 2 * LN_ + 1, :], scr[:, 2 * LN_, :], AF.Square),
                 reads=(f"scr0_{LN_}",), writes=(f"scr1_{LN_}",))
            p.op("tensor", lambda t, LN_=LN_: t.matmul(
                psK[LN_][0:2, 0:512], hsel[:], scr[:, 2 * LN_ + 1, :],
                start=True, stop=True),
                reads=(f"scr1_{LN_}", "hsel"), writes=(f"l4bk{LN_}",))
            p.op("vector", lambda v, tc=tc, LN_=LN_: v.tensor_copy(
                ns[0:2, LN_, tc * 512:(tc + 1) * 512], psK[LN_][0:2, 0:512]),
                reads=(f"l4bk{LN_}",), writes=(f"ns{LN_}",))
        p.op("vector", lambda v, LN_=LN_: v.tensor_scalar_max(
            ns[:, LN_, :], ns[:, LN_, :], cc["tiny"][0:2, :]),
            reads=(f"ns{LN_}", "constc"), writes=(f"ns{LN_}",))
        p.op("scalar", lambda s, LN_=LN_: s.activation(ns[:, LN_, :], ns[:, LN_, :], AF.Ln),
             reads=(f"ns{LN_}",), writes=(f"ns{LN_}",))
        p.op("scalar", lambda s, LN_=LN_: s.activation(ns[:, LN_, :], ns[:, LN_, :],
                                                       AF.Exp, scale=-0.5),
             reads=(f"ns{LN_}",), writes=(f"ns{LN_}",))
        for tc in range(4):
            p.op("tensor", lambda t, tc=tc, LN_=LN_: t.matmul(
                psK[LN_][:, 0:512], sel2[:], ns[0:2, LN_, tc * 512:(tc + 1) * 512],
                start=True, stop=True),
                reads=(f"ns{LN_}", "sel2"), writes=(f"l4bk{LN_}",))
            p.op("vector", lambda v, LN_=LN_: v.tensor_copy(
                scr4[:, LN_, :], psK[LN_][:, 0:512]),
                reads=(f"l4bk{LN_}",), writes=(f"scr3_{LN_}",))
            p.op("vector", lambda v, ec=ec, tc=tc, LN_=LN_: v.tensor_scalar(
                scr[:, 2 * LN_, :], kf[:, ec, tc * 512:(tc + 1) * 512],
                remc[:, ec:ec + 1], None, ALU.mult),
                reads=("kf", "remc"), writes=(f"scr0_{LN_}",))
            p.op("vector", lambda v, LN_=LN_: v.tensor_tensor(scr[:, 2 * LN_ + 1, :], scr[:, 2 * LN_, :],
                                                     scr4[:, LN_, :], ALU.mult),
                 reads=(f"scr0_{LN_}", f"scr3_{LN_}"), writes=(f"scr1_{LN_}",))
            p.op("vector", lambda v, LN_=LN_: v.tensor_copy(ost2[:, LN_, :], scr[:, 2 * LN_ + 1, :]),
                 reads=(f"scr1_{LN_}",), writes=(f"ost2_{LN_}",))
            p.dma("sync", outs["kap_f"][ec * 128:(ec + 1) * 128, tc * 512:(tc + 1) * 512],
                  ost2[:, LN_, :], reads=(f"ost2_{LN_}",), ch=2 + LN_)
            p.op("vector", lambda v, ec=ec, tc=tc, LN_=LN_: v.tensor_tensor(
                ost2[:, 4 + LN_, :], scr[:, 2 * LN_ + 1, :], af[:, ec, tc * 512:(tc + 1) * 512], ALU.mult),
                reads=(f"scr1_{LN_}", "af"), writes=(f"ost2_{4 + LN_}",))
            p.dma("sync", outs["ktil_f"][ec * 128:(ec + 1) * 128, tc * 512:(tc + 1) * 512],
                  ost2[:, 4 + LN_, :], reads=(f"ost2_{4 + LN_}",), ch=2 + LN_)
            p.op("gpsimd", lambda v, ec=ec, tc=tc, LN_=LN_: v.tensor_scalar(
                scr4[:, 2 + LN_, :], af[:, ec, tc * 512:(tc + 1) * 512],
                mixc[:, ec:ec + 1], omix[:, ec:ec + 1], ALU.mult, ALU.add),
                reads=("af", "mixc", "omix"), writes=(f"scr2_{LN_}",))
            p.op("gpsimd", lambda v, ec=ec, tc=tc, LN_=LN_: v.tensor_tensor(
                ost2[:, 2 + LN_, :], scr4[:, 2 + LN_, :],
                kf[:, ec, tc * 512:(tc + 1) * 512], ALU.mult),
                reads=(f"scr2_{LN_}", "kf"), writes=(f"ost2_{2 + LN_}",))
            p.dma("sync", outs["rho_f"][ec * 128:(ec + 1) * 128, tc * 512:(tc + 1) * 512],
                  ost2[:, 2 + LN_, :], reads=(f"ost2_{2 + LN_}",), ch=2 + LN_)
    p.end_lanes()
    return p.finish()


def build_l5(Sv=S):
    p = Prog()
    CH = 128
    NCH = Sv // CH
    kapD = p.din("kapP", [256, Sv], BF16)
    ktlD = p.din("ktlP", [256, Sv], BF16)
    rhoD = p.din("rhoP", [256, Sv], BF16)
    vD = p.din("vP", [256, Sv], BF16)
    rD = p.din("rP", [256, Sv], BF16)
    dD = p.din("dP", [256, Sv], F32)
    mABd = p.din("mABd", [128, 256], F32)
    mSUd = p.din("mSUd", [128, 128], F32)
    mIUd = p.din("mIUd", [128, 128], F32)
    identd = p.din("identbf", [128, 128], BF16)
    yD = p.dout("yP", [256, Sv], F32)

    WIN = 512
    NW = Sv // WIN
    win = {nm: p.sb("w_" + nm, [128 if nm != "d" else 64, 2, 4, WIN],
                     BF16 if nm != "d" else F32)
           for nm in ("kap", "ktl", "rho", "v", "r", "d")}
    mAB = p.sb("mAB", [128, 256], F32)
    mSU = p.sb("mSU", [128, 128], F32)
    mIU = p.sb("mIU", [128, 128], F32)
    zz = p.sb("zz", [64, CH], F32)
    identf = p.sb("identf_s", [128, 128], BF16)

    def mk(nm, shape, dt=BF16):
        return [p.sb(f"{nm}{pr}", shape, dt) for pr in range(4)]
    def mk2(nm, shape, dt=BF16):
        return [[p.sb(f"{nm}{pr}c{cp}", shape, dt) for pr in range(4)]
                for cp in range(2)]
    SW = mk("SW", [128, 64]); SWT = mk("SWT", [128, 64])
    dp = mk2("dp", [64, CH], F32)
    dpp = mk("dpp", [64, CH], F32); rdp = mk("rdp", [64, CH], F32)
    Ktd = mk("Ktd", [128, CH]); KVd = mk("KVd", [128, 2 * CH])
    Kdt = mk2("Kdt", [CH, 64]); Vdt = mk2("Vdt", [CH, 64]); Pt = mk2("Pt", [CH, 64])
    AT = mk("AT", [CH, 2 * CH]); Bm = mk("Bm", [CH, CH]); Tt = mk("Tt", [CH, CH])
    Z = mk2("Z", [CH, 128])
    SQ = [p.sb(f"SQ{pr}", [CH, 11, CH], BF16) for pr in range(4)]
    Wmf = mk2("Wmf", [128, CH]); Mf = mk2("Mf", [128, CH])
    C1T = mk2("C1T", [CH, CH]); D2 = mk2("D2", [CH, CH]); M1s = mk2("M1s", [CH, CH])
    Q0 = mk("Q0", [128, CH]); Ht = mk("Ht", [CH, 64])
    T2 = mk("T2", [128, 64])
    YS = mk("YS", [64, CH], F32)

    ps5 = [p.ps(f"l5p{i}", [128, 512], F32) for i in range(4)]
    pk5 = [f"l5p{i}" for i in range(4)]
    ps_acc = [p.ps(f"l5a{i}", [128, 512], F32) for i in range(4)]
    pk_acc = [f"l5a{i}" for i in range(4)]

    def bank(pr):
        return ps5[pr], pk5[pr]

    p.dma("sync", mAB[:], mABd, writes=("mAB",))
    p.dma("sync", mSU[:], mSUd, writes=("mSU",))
    p.dma("sync", mIU[:], mIUd, writes=("mIU",))
    p.dma("sync", identf[:], identd, writes=("identf",))
    p.op("vector", lambda v: v.memset(zz[:], 0.0), writes=("zz",))
    for nm in ("kap", "ktl", "rho", "v", "r"):
        p.op("vector", lambda v, nm=nm: v.memset(win[nm][64:128], 0.0),
             writes=tuple(f"win_{nm}{pr}_{sl}" for pr in range(4) for sl in range(2)))
    for pr in range(4):
        P = f"p{pr}"
        p.op("vector", lambda v, pr=pr: v.memset(Ktd[pr][64:128, :], 0.0),
             writes=(P + "Ktd",))
        p.op("vector", lambda v, pr=pr: v.memset(KVd[pr][64:128, :], 0.0),
             writes=(P + "KVd",))
        p.op("vector", lambda v, pr=pr: v.memset(Q0[pr][64:128, :], 0.0),
             writes=(P + "Q0",))
        p.op("vector", lambda v, pr=pr: v.memset(T2[pr][64:128, :], 0.0),
             writes=(P + "T2",))
        for cp in range(2):
            P2 = f"p{pr}c{cp}"
            p.op("vector", lambda v, pr=pr, cp=cp: v.memset(
                Wmf[cp][pr][64:128, :], 0.0), writes=(P2 + "Wmf",))
            p.op("vector", lambda v, pr=pr, cp=cp: v.memset(
                Mf[cp][pr][64:128, :], 0.0), writes=(P2 + "Mf",))
    for pr in range(4):
        p.op("vector", lambda v, pr=pr: v.memset(SW[pr][:], 0.0), writes=(f"SW{pr}",))
        p.op("vector", lambda v, pr=pr: v.memset(SWT[pr][:], 0.0), writes=(f"SWT{pr}",))

    def ld_window(w):
        c0 = w * WIN
        sl = w % 2
        for nm, dram, eng in (("kap", kapD, "gpsimd"), ("ktl", ktlD, "gpsimd"),
                              ("rho", rhoD, "gpsimd"), ("v", vD, "gpsimd"),
                              ("r", rD, "gpsimd"), ("d", dD, "sync")):
            for pr in range(4):
                p.dma(eng, win[nm][0:64, sl, pr, :] if nm != "d"
                      else win[nm][:, sl, pr, :],
                      dram[pr * 64:(pr + 1) * 64, c0:c0 + WIN],
                      writes=(f"win_{nm}{pr}_{sl}",), ch=pr % 3)

    def mm(lhs, rhs, out_sl, reads, wkey, start=True, stop=True, bankref=None):
        if bankref is None:
            bk, key = bank(pr_cur[0])
        else:
            bk, key = bankref
        p.op("tensor", lambda t, lhs=lhs, rhs=rhs, bk=bk, sl=out_sl, s=start, e=stop:
             t.matmul(bk[sl], lhs, rhs, start=s, stop=e),
             reads=reads, writes=(key,))
        return bk, key

    def tr2(in_ap, reads, wkey_out, out_sb, scale_col=None, scale_key=None,
            ihr=None):
        bk, key = bank(pr_cur[0])
        ih, iw = in_ap.shape[0], in_ap.shape[1]
        ihr = ihr or ih
        p.op("tensor", lambda t, i=in_ap, bk=bk, ih=ih, iw=iw, ihr=ihr: t.matmul(
            bk[0:iw, 0:ihr], i, identf[0:ih, 0:ihr], start=True, stop=True),
            reads=reads + ("identf",), writes=(key,))
        if scale_col is None:
            p.op("scalar", lambda v, bk=bk, o=out_sb, ihr=ihr, iw=iw: v.copy(
                o[:], bk[0:iw, 0:ihr]), reads=(key,), writes=(wkey_out,))
        else:
            p.op("vector", lambda v, bk=bk, o=out_sb, sc=scale_col, ihr=ihr, iw=iw:
                 v.tensor_scalar(o[:], bk[0:iw, 0:ihr], sc, None, ALU.mult),
                 reads=(key, scale_key), writes=(wkey_out,))

    pr_cur = [0]
    ld_window(0)
    for w in range(NW):
        if w + 1 < NW:
            ld_window(w + 1)
        for ci in range(WIN // CH):
            cn = w * (WIN // CH) + ci
            cs = slice(ci * CH, (ci + 1) * CH)
            p.begin_lanes(8)
            cp = cn % 2
            for pr in range(4):
                p.lane(pr * 2 + cp)
                pr_cur[0] = pr
                P = f"p{pr}"
                P2 = f"p{pr}c{cp}"
                sl = w % 2
                wk = tuple(f"win_{nm}{pr}_{sl}" for nm in
                           ("kap", "ktl", "rho", "v", "r", "d"))
                Kf = win["kap"][0:64, sl, pr, cs]
                Ktf = win["ktl"][0:64, sl, pr, cs]
                Pf = win["rho"][:, sl, pr, cs]; Vf = win["v"][0:64, sl, pr, cs]
                Rf = win["r"][:, sl, pr, cs]; Df = win["d"][:, sl, pr, cs]
                p.op("vector", lambda v, pr=pr, Df=Df, cp=cp: v.tensor_tensor_scan(
                    dp[cp][pr][:], Df, zz[:], 1.0, ALU.mult, ALU.add),
                    reads=(wk[5], "zz"), writes=(P2 + "dp",))
                p.op("vector", lambda v, pr=pr: v.memset(dpp[pr][:, 0:1], 1.0),
                     writes=(P + "dpp",))
                p.op("gpsimd", lambda v, pr=pr, cp=cp: v.tensor_copy(
                    dpp[pr][:, 1:CH], dp[cp][pr][:, 0:CH - 1]),
                    reads=(P2 + "dp",), writes=(P + "dpp",))
                p.op("vector", lambda v, pr=pr, cp=cp: v.reciprocal(rdp[pr][:], dp[cp][pr][:]),
                     reads=(P2 + "dp",), writes=(P + "rdp",))
                p.op("gpsimd", lambda v, pr=pr, Ktf=Ktf: v.tensor_tensor(
                    Ktd[pr][0:64, :], Ktf, dpp[pr][:], ALU.mult),
                    reads=(wk[1], P + "dpp"), writes=(P + "Ktd",))
                p.op("gpsimd", lambda v, pr=pr, Kf=Kf: v.tensor_tensor(
                    KVd[pr][0:64, 0:CH], Kf, rdp[pr][:], ALU.mult),
                    reads=(wk[0], P + "rdp"), writes=(P + "KVd",))
                p.op("gpsimd", lambda v, pr=pr, Vf=Vf: v.tensor_tensor(
                    KVd[pr][0:64, CH:2 * CH], Vf, rdp[pr][:], ALU.mult),
                    reads=(wk[3], P + "rdp"), writes=(P + "KVd",))
                tr2(KVd[pr][:, 0:CH], (P + "KVd",), P2 + "Kdt", Kdt[cp][pr], ihr=64)
                p.op("vector", lambda v, pr=pr, cp=cp: v.tensor_scalar_mul(
                    Kdt[cp][pr][:], Kdt[cp][pr][:], -1.0),
                    reads=(P2 + "Kdt",), writes=(P2 + "Kdt",))
                tr2(KVd[pr][:, CH:2 * CH], (P + "KVd",), P2 + "Vdt", Vdt[cp][pr], ihr=64)
                bkp, keyp = bank(pr)
                p.op("tensor", lambda t, Pf=Pf, bk=bkp: t.matmul(
                    bk[0:CH, 0:64], Pf, identf[0:128, 0:64], start=True,
                    stop=True), reads=(wk[2], "identf"), writes=(keyp,))
                p.op("scalar", lambda v, bk=bkp, pr=pr, cp=cp: v.copy(
                    Pt[cp][pr][:], bk[0:CH, 0:64]), reads=(keyp,), writes=(P2 + "Pt",))
                bk1, key1 = mm(Ktd[pr][:], KVd[pr][:], (slice(0, CH), slice(0, 2 * CH)),
                               (P + "Ktd", P + "KVd"), None)
                p.op("vector", lambda v, bk=bk1, pr=pr: v.tensor_tensor(
                    AT[pr][:], bk[0:CH, 0:2 * CH], mAB[:], ALU.mult),
                    reads=(key1, "mAB"), writes=(P + "AT",))
                bk2, key2 = mm(KVd[pr][:, 0:CH], Ktd[pr][:], (slice(0, CH), slice(0, CH)),
                               (P + "KVd", P + "Ktd"), None)
                p.op("vector", lambda v, bk=bk2, pr=pr: v.tensor_tensor(
                    Bm[pr][:], bk[0:CH, 0:CH], mSU[:], ALU.mult),
                    reads=(key2, "mSU"), writes=(P + "Bm",))
                bk2b, key2b = mm(KVd[pr][:, CH:2 * CH], Ktd[pr][:],
                                 (slice(0, CH), slice(0, CH)),
                                 (P + "KVd", P + "Ktd"), None)
                p.op("vector", lambda v, bk=bk2b, pr=pr: v.tensor_tensor(
                    Tt[pr][:], bk[0:CH, 0:CH], mSU[:], ALU.mult),
                    reads=(key2b, "mSU"), writes=(P + "Tt",))
                bk3, key3 = bank(pr)
                p.op("tensor", lambda t, pr=pr, bk=bk3: t.matmul(
                    bk[0:CH, 0:64], Ktd[pr][:], identf[0:128, 0:64],
                    start=True, stop=True),
                    reads=(P + "Ktd", "identf"), writes=(key3,))
                p.op("scalar", lambda v, bk=bk3, pr=pr, cp=cp: v.copy(
                    Z[cp][pr][:, 0:64], bk[0:CH, 0:64]), reads=(key3,), writes=(P2 + "Z",))
                bk4, key4 = mm(Tt[pr][:], Pt[cp][pr][:], (slice(0, CH), slice(0, 64)),
                               (P + "Tt", P2 + "Pt"), None)
                p.op("scalar", lambda v, bk=bk4, pr=pr, cp=cp: v.copy(
                    Z[cp][pr][:, 64:128], bk[0:CH, 0:64]), reads=(key4,), writes=(P2 + "Z",))
                sq = SQ[pr]
                sqk = P + "SQ"
                Asl = AT[pr][:, 0:CH]
                Bsl = Bm[pr][:]

                def apply_z(lh, rkeys, sub=False):
                    bkx, keyx = mm(lh, Z[cp][pr][:], (slice(0, CH), slice(0, 128)),
                                   rkeys + (P2 + "Z",), None)
                    p.op("vector", lambda v, bk=bkx, pr=pr, sub=sub, cp=cp: v.tensor_tensor(
                        Z[cp][pr][:], Z[cp][pr][:], bk[0:CH, 0:128],
                        ALU.subtract if sub else ALU.add),
                        reads=(keyx, P2 + "Z"), writes=(P2 + "Z",))

                apply_z(Bsl, (P + "Bm",), sub=True)
                for (lh, rh, slot) in ((Bsl, Asl, 0), (Asl, Bsl, 1)):
                    bkx, keyx = mm(lh, rh, (slice(0, CH), slice(0, CH)),
                                   (P + "AT", P + "Bm"), None)
                    p.op("scalar", lambda v, bk=bkx, sq=sq, slot=slot: v.copy(
                        sq[:, slot, :], bk[0:CH, 0:CH]), reads=(keyx,), writes=(sqk,))
                apply_z(sq[:, 1, :], (sqk,))
                for lev in range(4):
                    a = 2 * lev; b = 2 * lev + 1
                    for (li, ri, slot) in ((b, a, a + 2), (a, b, b + 2)):
                        bkx, keyx = mm(sq[:, li, :], sq[:, ri, :],
                                       (slice(0, CH), slice(0, CH)), (sqk,), None)
                        p.op("scalar", lambda v, bk=bkx, sq=sq, slot=slot: v.copy(
                            sq[:, slot, :], bk[0:CH, 0:CH]), reads=(keyx,), writes=(sqk,))
                    apply_z(sq[:, 2 * lev + 3, :], (sqk,))
                bkx, keyx = mm(sq[:, 8, :], sq[:, 9, :], (slice(0, CH), slice(0, CH)),
                               (sqk,), None)
                p.op("scalar", lambda v, bk=bkx, sq=sq: v.copy(
                    sq[:, 10, :], bk[0:CH, 0:CH]), reads=(keyx,), writes=(sqk,))
                apply_z(sq[:, 10, :], (sqk,))
                tr2(Z[cp][pr][:, 0:64], (P2 + "Z",), P2 + "Wmf", Wmf[cp][pr][0:64, :])
                tr2(Z[cp][pr][:, 64:128], (P2 + "Z",), P2 + "Mf", Mf[cp][pr][0:64, :])
                bkx, keyx = mm(Pf, Rf, (slice(0, CH), slice(0, CH)),
                               (wk[2], wk[4]), None)
                p.op("vector", lambda v, bk=bkx, pr=pr, cp=cp: v.tensor_tensor(
                    C1T[cp][pr][:], bk[0:CH, 0:CH], mIU[:], ALU.mult),
                    reads=(keyx, "mIU"), writes=(P2 + "C1T",))
                bkx, keyx = mm(Mf[cp][pr][:], Rf, (slice(0, CH), slice(0, CH)),
                               (P2 + "Mf", wk[4]), None)
                p.op("vector", lambda v, bk=bkx, pr=pr, cp=cp: v.tensor_tensor(
                    D2[cp][pr][:], bk[0:CH, 0:CH], mIU[:], ALU.mult),
                    reads=(keyx, "mIU"), writes=(P2 + "D2",))
                bkq, keyq = mm(SW[pr][:], Rf, (slice(0, 64), slice(0, CH)),
                               (f"SW{pr}", wk[4]), None)
                p.op("scalar", lambda s, bk=bkq, pr=pr: s.copy(
                    Q0[pr][0:64, :], bk[0:64, 0:CH]), reads=(keyq,), writes=(P + "Q0",))
                bkx, keyx = mm(Wmf[cp][pr][:], Q0[pr][:], (slice(0, CH), slice(0, CH)),
                               (P2 + "Wmf", P + "Q0"), None)
                p.op("vector", lambda v, bk=bkx, pr=pr, cp=cp: v.tensor_tensor(
                    M1s[cp][pr][:], bk[0:CH, 0:CH], mIU[:], ALU.mult),
                    reads=(keyx, "mIU"), writes=(P2 + "M1s",))
                p.op("vector", lambda v, pr=pr, cp=cp: v.tensor_tensor(
                    D2[cp][pr][:], D2[cp][pr][:], M1s[cp][pr][:], ALU.add),
                    reads=(P2 + "M1s", P2 + "D2"), writes=(P2 + "D2",))
                bky, keyy = mm(Vdt[cp][pr][:], C1T[cp][pr][:], (slice(0, 64), slice(0, CH)),
                               (P2 + "Vdt", P2 + "C1T"), None, start=True, stop=False,
                               bankref=(ps_acc[pr], pk_acc[pr]))
                mm(Kdt[cp][pr][:], D2[cp][pr][:], (slice(0, 64), slice(0, CH)),
                   (P2 + "Kdt", P2 + "D2"), None, start=False, stop=True,
                   bankref=(bky, keyy))
                p.op("vector", lambda v, bk=bky, pr=pr: v.tensor_tensor(
                    YS[pr][:], bk[0:64, 0:CH], Q0[pr][0:64, :], ALU.add),
                    reads=(keyy, P + "Q0"), writes=(P + "YS",))
                p.op("vector", lambda v, pr=pr, cp=cp: v.tensor_tensor(
                    YS[pr][:], YS[pr][:], dp[cp][pr][:], ALU.mult),
                    reads=(P + "YS", P2 + "dp"), writes=(P + "YS",))
                p.dma("sync", yD[pr * 64:(pr + 1) * 64, cn * CH:(cn + 1) * CH],
                      YS[pr][:], reads=(P + "YS",), ch=pr)
                bkh, keyh = mm(Wmf[cp][pr][:], SWT[pr][:], (slice(0, CH), slice(0, 64)),
                               (P2 + "Wmf", f"SWT{pr}"), None)
                p.op("scalar", lambda s, bk=bkh, pr=pr: s.copy(
                    Ht[pr][:], bk[0:CH, 0:64]), reads=(keyh,), writes=(P + "Ht",))
                p.op("vector", lambda v, pr=pr, cp=cp: v.tensor_tensor(
                    Ht[pr][:], Ht[pr][:], Z[cp][pr][:, 64:128], ALU.add),
                    reads=(P + "Ht", P2 + "Z"), writes=(P + "Ht",))
                bk6, key6 = mm(Pt[cp][pr][:], Vdt[cp][pr][:], (slice(0, 64), slice(0, 64)),
                               (P2 + "Pt", P2 + "Vdt"), None, start=True, stop=False,
                               bankref=(ps_acc[pr], pk_acc[pr]))
                mm(Ht[pr][:], Kdt[cp][pr][:], (slice(0, 64), slice(0, 64)),
                   (P + "Ht", P2 + "Kdt"), None, start=False, stop=True,
                   bankref=(bk6, key6))
                p.op("vector", lambda v, bk=bk6, pr=pr: v.tensor_tensor(
                    T2[pr][0:64, :], SW[pr][0:64, :], bk[0:64, 0:64], ALU.add),
                    reads=(key6, f"SW{pr}"), writes=(P + "T2",))
                tr2(T2[pr][:], (P + "T2",), f"SWT{pr}", SWT[pr][0:64, :],
                    scale_col=dp[cp][pr][:, CH - 1:CH], scale_key=P2 + "dp",
                    ihr=64)
                tr2(SWT[pr][:], (f"SWT{pr}",), f"SW{pr}", SW[pr][0:64, :],
                    ihr=64)
            p.end_lanes()
    return p.finish()


def build_l6a():
    p = Prog()
    yT = p.din("yT", [C, TPC], F32)
    rT = p.din("rT", [C, TPC], BF16)
    rhoT = p.din("rhoT", [C, TPC], BF16)
    vT = p.din("vT", [C, TPC], BF16)
    gT = p.din("gT", [C, TPC], BF16)
    lngc = p.din("lngc", [128, 8], F32)
    lnbc = p.din("lnbc", [128, 8], F32)
    bmc = p.din("bmc", [128, 8], F32)
    hsel_d = p.din("hsel_d", [128, 2], F32)
    sel2d = p.din("sel2_d", [2, 128], F32)
    zT = p.dout("zT", [C, TPC], BF16)

    rc = p.sb("rc", [128, TPC], BF16)
    rhoc = p.sb("rhoc", [128, TPC], BF16)
    vc = p.sb("vc", [128, TPC], BF16)
    gc = p.sb("gc", [128, TPC], BF16)
    yc = p.sb("yc", [128, TPC], F32)
    ybb = p.sb("ybb", [128, 8, TPC], BF16)
    sq = ybb
    sqb = p.sb("sqq", [128, 8, TPC], BF16)
    lg = p.sb("lg", [128, 8], F32)
    lb = p.sb("lb", [128, 8], F32)
    bm02 = p.sb("bm02", [128, 8], F32)
    hsel = p.sb("hsel", [128, 2], F32)
    sel2 = p.sb("sel2", [2, 128], F32)
    ns = p.sb("ns", [2, TPC], F32)
    scr = p.sb("scr", [128, 2, TPC // 2], F32)
    scrq = p.sb("scrq", [128, 512], F32)
    Ab = p.sb("Ab", [128, TPC], F32)
    Bb = p.sb("Bb", [128, TPC], F32)
    rows = p.sb("rows", [1, 3, TPC], F32)
    zst = p.sb("zst", [128, 2, TPC], BF16)
    onesb = p.sb("onesb", [128, 1], BF16)
    ones1 = p.sb("ones1", [1, 128], F32)
    cc = p.const_cols({"eps6": 1e-6})
    p.op("gpsimd", lambda g: g.memset(onesb[:], 1.0), writes=("onesb",))
    p.op("gpsimd", lambda g: g.memset(ones1[:], 1.0), writes=("ones1",))
    p.dma("sync", lg[:], lngc, writes=("gcols",), ch=1)
    p.dma("sync", lb[:], lnbc, writes=("gcols",), ch=1)
    p.dma("sync", bm02[:], bmc, writes=("bm02",), ch=1)
    p.dma("sync", hsel[:], hsel_d, writes=("hsel",), ch=1)
    p.dma("sync", sel2[:], sel2d, writes=("sel2",), ch=1)
    p.op("vector", lambda v: v.tensor_scalar_mul(bm02[:], bm02[:], 0.2),
         reads=("bm02",), writes=("bm02",))

    mr = MMRunner(p, "l6a")
    yTr = yT.rearrange("(c p) t -> p c t", p=128)
    for ec in range(8):
        p.dma("sync", rc[:], rT.rearrange("(c p) t -> p c t", p=128)[:, ec], writes=("rc",))
        p.dma("sync", rhoc[:], rhoT.rearrange("(c p) t -> p c t", p=128)[:, ec],
              writes=("rhoc",), ch=1)
        p.dma("sync", vc[:], vT.rearrange("(c p) t -> p c t", p=128)[:, ec],
              writes=("vc",), ch=2)
        p.dma("gpsimd", yc[:], yTr[:, ec], writes=("yc",))
        for tc in range(4):
            cs = slice(tc * 512, (tc + 1) * 512)
            p.op("vector", lambda v, ec=ec, cs=cs: v.tensor_tensor(
                scrq[:], rc[:, cs], rhoc[:, cs], ALU.mult),
                reads=("rc", "rhoc"), writes=("scrq",))
            mms = [(lambda t, ps, s, e: t.matmul(ps[0:2, :], hsel[:], scrq[:],
                                                 start=s, stop=e))]
            mr.group(mms, lambda e, ps, cs=cs: copy_any(e, ns[0:2, cs], ps[0:2, :]),
                     reads=("scrq", "hsel"), evac_writes=("ns",), evac_eng="vector")
        for tc in range(4):
            cs = slice(tc * 512, (tc + 1) * 512)
            mms = [(lambda t, ps, s, e, cs=cs: t.matmul(ps, sel2[:], ns[0:2, cs],
                                                        start=s, stop=e))]

            def bonus_ev(e, ps, ec=ec, cs=cs):
                return e.tensor_tensor(scrq[:], ps, vc[:, cs], ALU.mult)
            mr.group(mms, bonus_ev, reads=("ns", "sel2"), evac_reads=("vc",),
                     evac_writes=("scrq",), evac_eng="vector")
            p.op("vector", lambda v, ec=ec: v.tensor_scalar(
                scrq[:], scrq[:], bm02[:, ec:ec + 1], None, ALU.mult),
                reads=("scrq", "bm02"), writes=("scrq",))
            p.op("vector", lambda v, ec=ec, cs=cs: v.tensor_tensor(
                ybb[:, ec, cs], scrq[:], yc[:, cs], ALU.add),
                reads=("scrq", "yc"), writes=("ybb",))
    feat_ln(p, mr, "ln6", ybb, "ybb", sqb, rows, Ab, Bb, onesb, ones1, cc, "eps6")
    for ec in range(8):
        p.dma("sync", gc[:], gT.rearrange("(c p) t -> p c t", p=128)[:, ec], writes=("gc",))
        for half in range(2):
            cs = slice(half * 1024, (half + 1) * 1024)
            sb2 = half % 2
            p.op("vector", lambda v, ec=ec, cs=cs, sb2=sb2: v.tensor_tensor(
                scr[:, sb2, :], ybb[:, ec, cs], Ab[:, cs], ALU.mult),
                reads=("ybb", "ln6A"), writes=(f"scr{sb2}",))
            p.op("vector", lambda v, ec=ec, cs=cs, sb2=sb2: v.tensor_tensor(
                scr[:, sb2, :], scr[:, sb2, :], Bb[:, cs], ALU.add),
                reads=(f"scr{sb2}", "ln6B"), writes=(f"scr{sb2}",))
            p.op("vector", lambda v, ec=ec, cs=cs, sb2=sb2: v.tensor_scalar(
                scr[:, sb2, :], scr[:, sb2, :], lg[:, ec:ec + 1], lb[:, ec:ec + 1],
                ALU.mult, ALU.add),
                reads=(f"scr{sb2}", "gcols"), writes=(f"scr{sb2}",))
            p.op("vector", lambda v, ec=ec, cs=cs, sb2=sb2: v.tensor_tensor(
                zst[:, ec % 2, cs], scr[:, sb2, :], gc[:, cs], ALU.mult),
                reads=(f"scr{sb2}", "gc"), writes=(f"zst{ec % 2}",))
        p.dma("sync", zT[ec * 128:(ec + 1) * 128, :], zst[:, ec % 2, :],
              reads=(f"zst{ec % 2}",), ch=2)
    return p.finish()


def build_l6b():
    p = Prog()
    zTd = p.din("zT", [C, TPC], BF16)
    x2T = p.din("x2T", [C, TPC], F32)
    oW1 = p.din("oW1", [C, E1], F32)
    ob1c = p.din("ob1c", [128, 9], F32)
    ob1t = p.din("ob1t", [76, 1], F32)
    oW2 = p.din("oW2", [E1, C], F32)
    ob2c = p.din("ob2c", [128, 8], F32)
    x3T = p.dout("x3T", [C, TPC], F32)
    xln3T = p.dout("xln3T", [C, TPC], BF16)

    zs = p.sb("zs", [128, 8, TPC], BF16)
    w1 = p.sb("w1", [128, 8, E1], BF16)
    w2 = p.sb("w2", [128, NHC, C], BF16)
    b1 = p.sb("b1", [128, NHC], F32)
    b2 = p.sb("b2", [128, 8], F32)
    r1 = p.sb("r1", [128, NHC, TPC], BF16)
    x2t = p.sb("x2t", [128, 2, 512], F32)
    x3b = zs
    xln = p.sb("xlnb", [128, 8, TPC], BF16)
    sq = xln
    Ab = p.sb("Ab", [128, TPC], F32)
    Bb = p.sb("Bb", [128, TPC], F32)
    rows = p.sb("rows", [1, 3, TPC], F32)
    st1 = p.sb("st1", [128, 2, 512], F32)
    scr = p.sb("scr", [128, 2, 512], F32)
    onesb = p.sb("onesb", [128, 1], BF16)
    ones1 = p.sb("ones1", [1, 128], F32)
    cc = p.const_cols({"eps5": 1e-5})
    p.op("gpsimd", lambda g: g.memset(onesb[:], 1.0), writes=("onesb",))
    p.op("gpsimd", lambda g: g.memset(ones1[:], 1.0), writes=("ones1",))

    p.dma("sync", zs[:], zTd.rearrange("(c p) t -> p c t", p=128), writes=("zs",))
    w1R = oW1.rearrange("(c p) h -> p c h", p=128)
    for hcw in range(NHC):
        ww = hc_w(hcw)
        p.dma("gpsimd", w1[:, :, hcw * 128:hcw * 128 + ww],
              w1R[:, :, hcw * 128:hcw * 128 + ww], writes=(f"w1{hcw}",), ch=hcw % 2)
    p.dma("gpsimd", w2[:, 0:9], oW2[0:9 * 128, :].rearrange("(c p) e -> p c e", p=128),
          writes=("w2",))
    p.dma("gpsimd", w2[0:76, 9], oW2[9 * 128:E1, :], writes=("w2",))
    p.dma("sync", b1[:, 0:9], ob1c, writes=("b1",), ch=1)
    p.dma("sync", b1[0:76, 9:10], ob1t, writes=("b1",), ch=1)
    p.dma("sync", b2[:], ob2c, writes=("b2",), ch=1)

    mr = MMRunner(p, "l6b")
    for hc in range(NHC):
        w = hc_w(hc)
        for tc in range(4):
            mms = [(lambda t, ps, s, e, dc=dc, hc=hc, tc=tc, w=w:
                    t.matmul(ps[0:w, :], w1[:, dc, hc * 128:hc * 128 + w],
                             zs[:, dc, tc * 512:(tc + 1) * 512], start=s, stop=e))
                   for dc in range(8)]
            mr.group(mms, lambda e, ps, hc=hc, tc=tc, w=w: e.activation(
                r1[0:w, hc, tc * 512:(tc + 1) * 512], ps[0:w, :], AF.Gelu,
                bias=b1[0:w, hc:hc + 1]),
                reads=(f"w1{hc}", "zs"), evac_reads=("b1",),
                evac_writes=("r1",), evac_eng="scalar")
    x2Tr = x2T.rearrange("(c p) t -> p c t", p=128)
    for tc in range(4):
        for ec in range(8):
            slot = mr.g % 2
            p.dma("sync", x2t[:, slot, :], x2Tr[:, ec, tc * 512:(tc + 1) * 512],
                  writes=(f"x2t{slot}",), ch=2)
            mms = [(lambda t, ps, s, e, hc=hc, ec=ec, tc=tc, w=hc_w(hc):
                    t.matmul(ps, w2[0:w, hc, ec * 128:(ec + 1) * 128],
                             r1[0:w, hc, tc * 512:(tc + 1) * 512], start=s, stop=e))
                   for hc in range(NHC)]

            def ev(e, ps, ec=ec, slot=slot):
                return e.scalar_tensor_tensor(st1[:, slot, :], ps, b2[:, ec:ec + 1],
                                              x2t[:, slot, :], ALU.add, ALU.add)
            mr.group(mms, ev, reads=("w2", "r1"), evac_reads=("b2", f"x2t{slot}"),
                     evac_writes=(f"st1{slot}",), evac_eng="vector")
            p.dma("sync", x3T[ec * 128:(ec + 1) * 128, tc * 512:(tc + 1) * 512],
                  st1[:, slot, :], reads=(f"st1{slot}",), ch=3)
            p.op("scalar", lambda s, ec=ec, tc=tc, slot=slot: s.copy(
                x3b[:, ec, tc * 512:(tc + 1) * 512], st1[:, slot, :]),
                reads=(f"st1{slot}",), writes=("zs",))
    feat_ln(p, mr, "ln3", x3b, "zs", sq, rows, Ab, Bb, onesb, ones1, cc, "eps5")
    feat_ln_apply(p, "ln3", x3b, "zs", Ab, Bb, scr, xln, "ln3sq")
    for ec in range(8):
        p.dma("sync", xln3T[ec * 128:(ec + 1) * 128, :], xln[:, ec, :],
              reads=("ln3sq",), ch=2)
    return p.finish()


def build_l7():
    p = Prog()
    xlnT = p.din("xlnT", [C, TPC], BF16)
    x3T = p.din("x3T", [C, TPC], F32)
    gwD = p.din("gwD", [C, 8], F32)
    w1D = p.din("w1D", [8, C, 512], F32)
    w3D = p.din("w3D", [8, C, 512], F32)
    w2D = p.din("w2D", [8, 512, C], F32)
    identd = p.din("identf", [128, 128], F32)
    outT = p.dout("outT", [C, TPC], F32)

    xln = p.sb("xln", [128, 8, TPC], BF16)
    gw = p.sb("gw", [128, 8, 8], BF16)
    w1 = p.sb("w1e", [128, 2, 8, 512], BF16)
    w3 = p.sb("w3e", [128, 2, 8, 512], BF16)
    w2 = p.sb("w2e", [128, 2, 4, C], BF16)
    identf = p.sb("identf_s", [128, 128], F32)
    lgt = p.sb("lgt", [128, 16, 8], F32)
    eqs = p.sb("eqs", [128, 8], F32)
    p2s = p.sb("p2s", [128, 8], F32)
    scs = p.sb("scs", [128, 8], F32)
    sfx = p.sb("sfx", [1, 8, TPC], BF16)
    Sb = p.sb("Sbc", [128, TPC], BF16)
    hb = p.sb("hb", [128, 4, TPC], BF16)
    sg = p.sb("sg", [128, 2, 512], BF16)
    scr2 = p.sb("scr2", [128, 2, 512], F32)
    yac = p.sb("yac", [128, 8, TPC], F32)
    x3t = p.sb("x3t", [128, 2, 512], F32)
    st1 = scr2
    stt = p.sb("sttl", [128, 6], F32)
    ones1 = p.sb("ones1", [1, 128], BF16)
    p.op("gpsimd", lambda g: g.memset(ones1[:], 1.0), writes=("ones1",))

    xlR = xlnT.rearrange("(c p) t -> p c t", p=128)
    for tq4 in range(4):
        p.dma("sync", xln[:, :, tq4 * 512:(tq4 + 1) * 512],
              xlR[:, :, tq4 * 512:(tq4 + 1) * 512], writes=(f"xln{tq4}",))
    p.dma("gpsimd", gw[:], gwD.rearrange("(c p) e -> p c e", p=128), writes=("gw",))
    p.dma("sync", identf[:], identd, writes=("identf",), ch=1)

    mrs = [MMRunner(p, "l7A", nbank=2), MMRunner(p, "l7B", nbank=2)]
    mr = mrs[0]
    for ti in range(NTILE):
        mms = [(lambda t, ps, s, e, dc=dc, ti=ti:
                t.matmul(ps[:, 0:8], xln[:, dc, ti * 128:(ti + 1) * 128],
                         gw[:, dc, :], start=s, stop=e)) for dc in range(8)]
        mr.group(mms, lambda e, ps, ti=ti: copy_any(e, lgt[:, ti, :], ps[:, 0:8]),
                 reads=(f"xln{ti // 4}", "gw"), evac_writes=("lgt",),
                 evac_eng="vector")
    for ti in range(NTILE):
        p.op("vector", lambda v, ti=ti: v.reduce_max(stt[:, 0:1], lgt[:, ti, :], axis=AX),
             reads=("lgt",), writes=("sm",))
        p.op("vector", lambda v: v.tensor_scalar_mul(stt[:, 1:2], stt[:, 0:1], -1.0),
             reads=("sm",), writes=("snm",))
        p.op("scalar", lambda s, ti=ti: s.activation(
            lgt[:, ti, :], lgt[:, ti, :], AF.Exp, bias=stt[:, 1:2],
            accum_out=stt[:, 2:3]),
            reads=("lgt", "snm"), writes=("lgt", "ssum"))
        p.op("vector", lambda v: v.reciprocal(stt[:, 3:4], stt[:, 2:3]),
             reads=("ssum",), writes=("srl",))
        p.op("vector", lambda v, ti=ti: v.tensor_scalar(
            lgt[:, ti, :], lgt[:, ti, :], stt[:, 3:4], None, ALU.mult),
            reads=("lgt", "srl"), writes=("lgt",))
        p.op("vector", lambda v, ti=ti: v.reduce_max(stt[:, 0:1], lgt[:, ti, :], axis=AX),
             reads=("lgt",), writes=("sm",))
        p.op("vector", lambda v, ti=ti: v.tensor_scalar(
            eqs[:], lgt[:, ti, :], stt[:, 0:1], None, ALU.is_ge),
            reads=("lgt", "sm"), writes=("eqs",))
        p.op("vector", lambda v, ti=ti: v.scalar_tensor_tensor(
            p2s[:], eqs[:], -2.0, lgt[:, ti, :], ALU.mult, ALU.add),
            reads=("eqs", "lgt"), writes=("p2s",))
        p.op("vector", lambda v: v.reduce_max(stt[:, 4:5], p2s[:], axis=AX),
             reads=("p2s",), writes=("top2",))
        p.op("vector", lambda v, ti=ti: v.tensor_scalar(
            eqs[:], lgt[:, ti, :], stt[:, 4:5], None, ALU.is_ge),
            reads=("lgt", "top2"), writes=("eqs",))
        p.op("vector", lambda v, ti=ti: v.tensor_tensor(
            scs[:], lgt[:, ti, :], eqs[:], ALU.mult),
            reads=("lgt", "eqs"), writes=("scs",))
        for e8 in range(8):
            bkx = mr.banks[mr.g % mr.nbank]
            keyx = mr.bkeys[mr.g % mr.nbank]
            mr.g += 1
            p.op("tensor", lambda t, bk=bkx, e8=e8: t.transpose(
                bk[0:1, 0:128], scs[:, e8:e8 + 1], identf[:]),
                reads=("scs", "identf"), writes=(keyx,))
            p.op("vector", lambda v, bk=bkx, ti=ti, e8=e8: v.tensor_copy(
                sfx[0:1, e8, ti * 128:(ti + 1) * 128], bk[0:1, 0:128]),
                reads=(keyx,), writes=("sfx",))

    def ld_exp(e8):
        sl = e8 % 2
        p.dma("gpsimd", w1[:, sl], w1D[e8].rearrange("(c p) h -> p c h", p=128),
              writes=(f"w1e{sl}",), ch=0)
        p.dma("gpsimd", w3[:, sl], w3D[e8].rearrange("(c p) h -> p c h", p=128),
              writes=(f"w3e{sl}",), ch=1)
        p.dma("gpsimd", w2[:, sl], w2D[e8].rearrange("(c p) v -> p c v", p=128),
              writes=(f"w2e{sl}",), ch=2)

    ld_exp(0)
    for e8 in range(8):
        sl = e8 % 2
        if e8 + 1 < 8:
            ld_exp(e8 + 1)
        for tc in range(4):
            cs = slice(tc * 512, (tc + 1) * 512)
            mms = [(lambda t, ps, s, e, e8=e8, cs=cs:
                    t.matmul(ps[:, 0:512], ones1[:], sfx[0:1, e8, cs], start=s, stop=e))]
            mr.group(mms, lambda e, ps, cs=cs: copy_any(e, Sb[:, cs], ps[:, 0:512]),
                     reads=("sfx", "ones1"), evac_writes=("Sbc",), evac_eng="vector")
        p.begin_lanes(2)
        for hc in range(4):
            slot = hc % 2
            p.lane(slot)
            mr2 = mrs[slot]
            for tc in range(4):
                cs = slice(tc * 512, (tc + 1) * 512)
                gm = [(lambda t, ps, s, e, dc=dc, hc=hc, cs=cs, sl=sl:
                       t.matmul(ps, w1[:, sl, dc, hc * 128:(hc + 1) * 128],
                                xln[:, dc, cs], start=s, stop=e))
                      for dc in range(8)]

                def ev1(e, ps, slot=slot, cs=cs):
                    return e.tensor_tensor(scr2[:, slot, :], ps, Sb[:, cs], ALU.mult)
                mr2.group(gm, ev1, reads=(f"w1e{sl}", f"xln{tc}"), evac_reads=("Sbc",),
                          evac_writes=(f"scr2{slot}",), evac_eng="vector")
                p.op("scalar", lambda s, slot=slot: s.activation(
                    sg[:, slot, :], scr2[:, slot, :], AF.Silu),
                    reads=(f"scr2{slot}",), writes=(f"sg{slot}",))
                um = [(lambda t, ps, s, e, dc=dc, hc=hc, cs=cs, sl=sl:
                       t.matmul(ps, w3[:, sl, dc, hc * 128:(hc + 1) * 128],
                                xln[:, dc, cs], start=s, stop=e))
                      for dc in range(8)]

                def ev3(e, ps, slot=slot, cs=cs):
                    return e.tensor_tensor(scr2[:, slot, :], ps, Sb[:, cs], ALU.mult)
                mr2.group(um, ev3, reads=(f"w3e{sl}", f"xln{tc}"), evac_reads=("Sbc",),
                          evac_writes=(f"scr2{slot}",), evac_eng="vector")
                p.op("vector", lambda v, slot=slot, hc=hc, cs=cs: v.tensor_tensor(
                    hb[:, hc, cs], scr2[:, slot, :], sg[:, slot, :], ALU.mult),
                    reads=(f"scr2{slot}", f"sg{slot}"), writes=(f"hb{slot}",))
        p.end_lanes()
        p.begin_lanes(2)
        for ec in range(8):
            p.lane(ec % 2)
            mr2 = mrs[ec % 2]
            for tc in range(4):
                mms = [(lambda t, ps, s, e, hc=hc, ec=ec, tc=tc, sl=sl:
                        t.matmul(ps, w2[:, sl, hc, ec * 128:(ec + 1) * 128],
                                 hb[:, hc, tc * 512:(tc + 1) * 512], start=s, stop=e))
                       for hc in range(4)]
                if e8 == 0:
                    ev = (lambda e, ps, ec=ec, tc=tc: copy_any(
                        e, yac[:, ec, tc * 512:(tc + 1) * 512], ps))
                else:
                    ev = (lambda e, ps, ec=ec, tc=tc: e.tensor_tensor(
                        yac[:, ec, tc * 512:(tc + 1) * 512],
                        yac[:, ec, tc * 512:(tc + 1) * 512], ps, ALU.add))
                mr2.group(mms, ev, reads=(f"w2e{sl}", "hb0", "hb1"),
                          evac_writes=(f"yac{ec % 2}",), evac_eng="vector")
        p.end_lanes()
    x3Tr = x3T.rearrange("(c p) t -> p c t", p=128)
    for tc in range(4):
        for ec in range(8):
            slot = ec % 2
            p.dma("sync", x3t[:, slot, :], x3Tr[:, ec, tc * 512:(tc + 1) * 512],
                  writes=(f"x3t{slot}",), ch=2)
            p.op("vector", lambda v, ec=ec, tc=tc, slot=slot: v.tensor_tensor(
                st1[:, slot, :], yac[:, ec, tc * 512:(tc + 1) * 512],
                x3t[:, slot, :], ALU.add),
                reads=("yac0", "yac1", f"x3t{slot}"), writes=(f"scr2{slot}",))
            p.dma("sync", outT[ec * 128:(ec + 1) * 128, tc * 512:(tc + 1) * 512],
                  st1[:, slot, :], reads=(f"scr2{slot}",), ch=1)
    return p.finish()


_NC_CACHE = {}


def _get_nc(name, builder):
    if name not in _NC_CACHE:
        _NC_CACHE[name] = builder()
    return _NC_CACHE[name]


def _run(name, builder, in_maps):
    nc = _get_nc(name, builder)
    res = run_bass_kernel_spmd(nc, in_maps, core_ids=list(range(NCORE)))
    return res.results


def _split_tokens(arr):
    out = []
    for c in range(NCORE):
        b, qq = c // 4, c % 4
        out.append(np.ascontiguousarray(arr[b, qq * TPC:(qq + 1) * TPC]))
    return out


def _colsof(vec):
    return np.ascontiguousarray(np.asarray(vec).reshape(8, 128).T)


def kernel(**inputs):
    I = {k: np.asarray(v) for k, v in inputs.items()}
    x = I["L0_attn_x"]
    ident_bf = np.eye(128, dtype=BF)
    ident_f = np.eye(128, dtype=np.float32)

    wqT = np.ascontiguousarray(I["L0_attn_wq"].T)
    wkT = np.ascontiguousarray(I["L0_attn_wk"].T)
    wvT = np.ascontiguousarray(I["L0_attn_wv"].T)
    r1 = _run("l1", build_l1,
              [{"x": xs, "wqT": wqT, "wkT": wkT, "wvT": wvT, "identbf": ident_bf}
               for xs in _split_tokens(x)])

    qf = [np.concatenate([r1[4 * b + q]["q_f"] for q in range(4)], axis=1) for b in range(2)]
    kf = [np.concatenate([r1[4 * b + q]["k_f"] for q in range(4)], axis=1) for b in range(2)]
    vt = [np.concatenate([r1[4 * b + q]["v_t"] for q in range(4)], axis=0) for b in range(2)]

    tri = np.triu(np.ones((128, 128), np.float32))
    m4 = np.zeros((128, 4, 512), np.float32)
    for t in range(4):
        m4[:, t, t * 128:(t + 1) * 128] = tri
        m4[:, t, (t + 1) * 128:] = 1.0
    masks4 = np.ascontiguousarray(m4.reshape(128, 2048)).astype(BF)
    in2 = []
    for c in range(NCORE):
        b, hg = c // 4, c % 4
        rs = slice(hg * 256, (hg + 1) * 256)
        in2.append({"qh": np.ascontiguousarray(qf[b][rs]),
                    "kh": np.ascontiguousarray(kf[b][rs]),
                    "vh": np.ascontiguousarray(vt[b][:, rs]),
                    "masks4": masks4, "identf": ident_f})
    r2 = _run("l2", build_l2, in2)
    atf = [np.concatenate([r2[4 * b + hg]["attn"] for hg in range(4)], axis=1)
           for b in range(2)]

    woT = np.ascontiguousarray(I["L0_attn_wo"].T)
    wgT = np.ascontiguousarray(I["L0_ffn_w_gate"].T)
    wuT = np.ascontiguousarray(I["L0_ffn_w_up"].T)
    in3 = []
    for c in range(NCORE):
        b, qq = c // 4, c % 4
        ts_ = slice(qq * TPC, (qq + 1) * TPC)
        in3.append({"atT": np.ascontiguousarray(atf[b][ts_].T),
                    "xT": np.ascontiguousarray(x[b, ts_].T),
                    "woT": woT, "wgT": wgT, "wuT": wuT})
    r3ab = _run("l3ab", build_l3ab, in3)

    wdT = np.ascontiguousarray(I["L0_ffn_w_down"].T)
    r3c = _run("l3c", build_l3c,
               [{"hT": r3ab[c]["hT"], "x1T": r3ab[c]["x1T"], "wdT": wdT}
                for c in range(NCORE)])

    in4 = []
    w4 = {}
    for nm in ("r", "k", "v"):
        b1 = I[f"L1_attn_{nm}b1"]
        w4[nm + "W1"] = np.ascontiguousarray(I[f"L1_attn_{nm}W1"])
        w4[nm + "b1c"] = np.ascontiguousarray(b1[:9 * 128].reshape(9, 128).T)
        w4[nm + "b1t"] = np.ascontiguousarray(b1[9 * 128:].reshape(76, 1))
        w4[nm + "W2"] = np.ascontiguousarray(I[f"L1_attn_{nm}W2"])
        w4[nm + "b2c"] = _colsof(I[f"L1_attn_{nm}b2"])
        w4[nm + "muc"] = _colsof(I[f"L1_attn_mu_{nm}"])
    r4a = _run("l4a", build_l4a,
               [{"xlnT": r3c[c]["xln2T"], **w4} for c in range(NCORE)])

    hsel = np.stack([np.r_[np.ones(64), np.zeros(64)],
                     np.r_[np.zeros(64), np.ones(64)]], axis=1).astype(np.float32)
    sel2 = np.ascontiguousarray(hsel.T)
    w4b = {"remc_d": _colsof(I["L1_attn_rem_mult"]),
           "mixc_d": _colsof(I["L1_attn_iclr_mix"]),
           "sel2_d": sel2, "hsel_d": hsel}
    for nm in ("a", "d", "g"):
        w4b[nm + "A"] = np.ascontiguousarray(I[f"L1_attn_{nm}A"])
        w4b[nm + "B"] = np.ascontiguousarray(I[f"L1_attn_{nm}B"])
        bkey = {"a": "L1_attn_ab", "d": "L1_attn_db", "g": None}[nm]
        bvec = I[bkey] if bkey else np.zeros(C, np.float32)
        w4b[nm + "bc"] = _colsof(bvec)
        w4b[nm + "muc"] = _colsof(I[f"L1_attn_mu_{nm}"])
    r4b = _run("l4b", build_l4b,
               [{"xlnT": r3c[c]["xln2T"], "kT": r4a[c]["k_f"], **w4b}
                for c in range(NCORE)])

    def _gather_feat(key, res, dt=None):
        out = []
        for b in range(2):
            full = np.concatenate([res[4 * b + q][key] for q in range(4)], axis=1)
            out.append(full)
        return out

    kapf = _gather_feat("kap_f", r4b)
    ktlf = _gather_feat("ktil_f", r4b)
    rhof = _gather_feat("rho_f", r4b)
    vf = _gather_feat("v_f", r4a)
    rf = _gather_feat("r_f", r4a)
    df = _gather_feat("d_f", r4b)
    mAB = np.concatenate([np.tril(np.ones((128, 128), np.float32), -1)] * 2, axis=1)
    mSU = np.triu(np.ones((128, 128), np.float32), 1)
    mIU = np.triu(np.ones((128, 128), np.float32), 0)
    in5 = []
    for c in range(NCORE):
        b, hg = c // 4, c % 4
        rs = slice(hg * 256, (hg + 1) * 256)
        in5.append({"kapP": np.ascontiguousarray(kapf[b][rs]),
                    "ktlP": np.ascontiguousarray(ktlf[b][rs]),
                    "rhoP": np.ascontiguousarray(rhof[b][rs]),
                    "vP": np.ascontiguousarray(vf[b][rs]),
                    "rP": np.ascontiguousarray(rf[b][rs]),
                    "dP": np.ascontiguousarray(df[b][rs]),
                    "mABd": mAB, "mSUd": mSU, "mIUd": mIU,
                    "identbf": ident_bf})
    r5 = _run("l5", build_l5, in5)
    yfull = [np.concatenate([r5[4 * b + hg]["yP"] for hg in range(4)], axis=0)
             for b in range(2)]

    in6 = []
    for c in range(NCORE):
        b, qq = c // 4, c % 4
        cs = slice(qq * TPC, (qq + 1) * TPC)
        in6.append({"yT": np.ascontiguousarray(yfull[b][:, cs]),
                    "rT": r4a[c]["r_f"], "rhoT": r4b[c]["rho_f"],
                    "vT": r4a[c]["v_f"], "gT": r4b[c]["gate_f"],
                    "lngc": _colsof(I["L1_attn_ln_g"]),
                    "lnbc": _colsof(I["L1_attn_ln_b"]),
                    "bmc": _colsof(I["L1_attn_bonus_mult"]),
                    "hsel_d": hsel, "sel2_d": sel2})
    r6a = _run("l6a", build_l6a, in6)

    ob1 = I["L1_attn_ob1"]
    w6 = {"oW1": np.ascontiguousarray(I["L1_attn_oW1"]),
          "ob1c": np.ascontiguousarray(ob1[:9 * 128].reshape(9, 128).T),
          "ob1t": np.ascontiguousarray(ob1[9 * 128:].reshape(76, 1)),
          "oW2": np.ascontiguousarray(I["L1_attn_oW2"]),
          "ob2c": _colsof(I["L1_attn_ob2"])}
    r6b = _run("l6b", build_l6b,
               [{"zT": r6a[c]["zT"], "x2T": r3c[c]["x2T"], **w6} for c in range(NCORE)])

    w7 = {"gwD": np.ascontiguousarray(I["L1_ffn_gate_w"]),
          "w1D": np.ascontiguousarray(I["L1_ffn_w1"]),
          "w3D": np.ascontiguousarray(I["L1_ffn_w3"]),
          "w2D": np.ascontiguousarray(I["L1_ffn_w2"]),
          "identf": ident_f}
    r7 = _run("l7", build_l7,
              [{"xlnT": r6b[c]["xln3T"], "x3T": r6b[c]["x3T"], **w7}
               for c in range(NCORE)])

    out = np.empty((2, 8192, C), np.float32)
    for c in range(NCORE):
        b, qq = c // 4, c % 4
        out[b, qq * TPC:(qq + 1) * TPC] = r7[c]["outT"].T
    return out


# version 21
# speedup vs baseline: 1.1639x; 1.1639x over previous
import numpy as np
import ml_dtypes
from contextlib import ExitStack

import concourse.bass as bass
import concourse.mybir as mybir
from concourse.bass_utils import run_bass_kernel_spmd

F32 = mybir.dt.float32
BF16 = mybir.dt.bfloat16
I32 = mybir.dt.int32
AF = mybir.ActivationFunctionType
ALU = mybir.AluOpType
AX = mybir.AxisListType.X
BF = ml_dtypes.bfloat16

NCORE = 8
TPC = 2048
C = 1024
NTILE = TPC // 128

ENGS = ("sync", "tensor", "vector", "scalar", "gpsimd")


ELIDE_SAME_ENG = False


class Prog:
    def __init__(self):
        self.nc = bass.Bass("TRN2", target_bir_lowering=False)
        self.ctx = ExitStack()
        self.ops = {e: [] for e in ENGS}
        self.es = {}
        self.cnt = {e: 0 for e in ENGS}
        self.waited = {e: {} for e in ENGS}
        self.lastw = {}
        self._ch = 0
        self.chlast = {}
        self.readers = {}

    def _esem(self, key):
        if key not in self.es:
            self.es[key] = self.ctx.enter_context(self.nc.semaphore("es_" + key.replace(".", "_")))
        return self.es[key]

    def sb(self, name, shape, dt):
        return self.ctx.enter_context(self.nc.sbuf_tensor(name, list(shape), dt))

    def ps(self, name, shape, dt=F32):
        return self.ctx.enter_context(self.nc.psum_tensor(name, list(shape), dt))

    def din(self, name, shape, dt):
        return self.nc.dram_tensor(name, list(shape), dt, kind="ExternalInput").ap()

    def dout(self, name, shape, dt):
        return self.nc.dram_tensor(name, list(shape), dt, kind="ExternalOutput").ap()

    def begin_lanes(self, n):
        self._lanes = [[] for _ in range(n)]
        self._lane = 0

    def lane(self, k):
        self._lane = k

    def end_lanes(self, gran=1):
        lanes = self._lanes
        self._lanes = None
        idx = [0] * len(lanes)
        while True:
            done = True
            for k in range(len(lanes)):
                for _ in range(gran):
                    if idx[k] < len(lanes[k]):
                        done = False
                        lanes[k][idx[k]]()
                        idx[k] += 1
            if done:
                break

    def op(self, eng, fn, inc=1, deps=(), reads=(), writes=()):
        if getattr(self, "_lanes", None) is not None:
            self._lanes[self._lane].append(
                lambda eng=eng, fn=fn, inc=inc, deps=deps, reads=reads, writes=writes:
                self._op_now(eng, fn, inc, deps, reads, writes))
            return None
        return self._op_now(eng, fn, inc, deps, reads, writes)

    def _op_now(self, eng, fn, inc=1, deps=(), reads=(), writes=()):
        alldeps = list(deps)
        for r in reads:
            alldeps.append(self.lastw.get(r))
        for w in writes:
            alldeps.append(self.lastw.get(w))
            alldeps.extend(self.readers.get(w, ()))
        self.dep(eng, *alldeps)
        key = eng + (f".d{self._ch}" if inc == 16 else ".c")
        h = self._esem(key)
        self.cnt[key] = self.cnt.get(key, 0) + inc
        tok = (key, self.cnt[key])

        def g(e, fn=fn, h=h, k=inc):
            fn(e).then_inc(h, k)
        self.ops[eng].append(g)
        for r in reads:
            self.readers.setdefault(r, []).append(tok)
        for w in writes:
            self.lastw[w] = tok
            self.readers[w] = []
        return tok

    def dma(self, eng, out, in_, deps=(), reads=(), writes=(), ch=0):
        if getattr(self, "_lanes", None) is not None:
            self._lanes[self._lane].append(
                lambda eng=eng, out=out, in_=in_, deps=deps, reads=reads,
                writes=writes, ch=ch: self._dma_now(eng, out, in_, deps, reads, writes, ch))
            return None
        return self._dma_now(eng, out, in_, deps, reads, writes, ch)

    def _dma_now(self, eng, out, in_, deps=(), reads=(), writes=(), ch=0):
        self._ch = ch
        key = f"{eng}.d{ch}"
        prev = self.chlast.get(key)
        if prev is not None:
            self.dep(eng, prev)
        tok = self._op_now(eng, lambda e, o=out, i=in_: e.dma_start(out=o, in_=i),
                           inc=16, deps=deps, reads=reads, writes=writes)
        self.chlast[key] = tok
        return tok

    def dep(self, eng, *toks):
        best = {}
        for t in toks:
            if t is None:
                continue
            tk, n = t
            if tk == "tensor.c" and eng == "tensor":
                continue
            if ELIDE_SAME_ENG and tk == eng + ".c":
                continue
            if n > best.get(tk, 0):
                best[tk] = n
        for tk, n in best.items():
            if self.waited[eng].get(tk, 0) >= n:
                continue
            h = self._esem(tk)
            self.ops[eng].append(lambda e, h=h, v=n: e.wait_ge(h, v))
            self.waited[eng][tk] = n

    def finish(self, final_eng="sync"):
        for key, n in list(self.cnt.items()):
            self.dep(final_eng, (key, n))
        with self.nc.Block() as block:
            for eng in ENGS:
                if self.ops[eng]:
                    def runner(e, _ops=tuple(self.ops[eng])):
                        for f in _ops:
                            f(e)
                    getattr(block, eng)(runner)
        self.ctx.close()
        return self.nc

    def const_cols(self, values: dict):
        tile = self.sb("constc", [128, max(1, len(values))], F32)
        aps = {}
        for i, (k, v) in enumerate(values.items()):
            self.op("gpsimd",
                    lambda g, i=i, v=v, t=tile: g.memset(t[:, i:i + 1], float(v)),
                    writes=("constc",))
            aps[k] = tile[:, i:i + 1]
        return aps


def copy_any(e, dst, src):
    if hasattr(e, "tensor_copy"):
        return e.tensor_copy(dst, src)
    return e.copy(dst, src)


class MMRunner:

    def __init__(self, p: Prog, name, width=512, nbank=4, psum_dt=F32):
        self.p = p
        self.banks = [p.ps(f"{name}_pm{i}", [128, width], psum_dt) for i in range(nbank)]
        self.bkeys = [f"{name}_pm{i}" for i in range(nbank)]
        self.nbank = nbank
        self.g = 0

    def group(self, mms, evac, reads=(), evac_eng=None, evac_writes=(), evac_reads=()):
        p = self.p
        g = self.g
        b = g % self.nbank
        bank = self.banks[b]
        bkey = self.bkeys[b]
        eng = evac_eng or ("vector" if g % 2 == 0 else "scalar")
        n = len(mms)
        for i, mm in enumerate(mms):
            p.op("tensor",
                 (lambda t, mm=mm, bank=bank, s=(i == 0), e=(i == n - 1):
                  mm(t, bank[:], s, e)),
                 reads=reads, writes=(bkey,))
        ev = p.op(eng, lambda e, bank=bank, evac=evac: evac(e, bank[:]),
                  reads=(bkey,) + tuple(evac_reads), writes=tuple(evac_writes))
        self.g += 1
        return ev


def feat_ln(p, mr, name, xb, xbkey, sq, rows, Ab, Bb, onesb, ones1, cc, eps_key,
            T=TPC, nchunk=8):
    nt = T // 512
    sqkey = name + "sq"
    for ec in range(nchunk):
        p.op("scalar", lambda s, ec=ec: s.activation(sq[:, ec, :], xb[:, ec, :], AF.Square),
             reads=(xbkey,), writes=(sqkey,))
    for tc in range(nt):
        mms = [(lambda t, ps, s, e, ec=ec, tc=tc:
                t.matmul(ps[0:1, :], onesb[:], xb[:, ec, tc * 512:(tc + 1) * 512],
                         start=s, stop=e)) for ec in range(nchunk)]
        mr.group(mms, lambda e, ps, tc=tc: copy_any(e, rows[0:1, 0, tc * 512:(tc + 1) * 512], ps[0:1, :]),
                 reads=(xbkey, "onesb"), evac_writes=(name + "r0",))
        mms = [(lambda t, ps, s, e, ec=ec, tc=tc:
                t.matmul(ps[0:1, :], onesb[:], sq[:, ec, tc * 512:(tc + 1) * 512],
                         start=s, stop=e)) for ec in range(nchunk)]
        mr.group(mms, lambda e, ps, tc=tc: copy_any(e, rows[0:1, 1, tc * 512:(tc + 1) * 512], ps[0:1, :]),
                 reads=(name + "sq", "onesb"), evac_writes=(name + "r1",))
    nd = float(nchunk * 128)
    p.op("vector", lambda v: v.tensor_scalar_mul(rows[0:1, 0, :], rows[0:1, 0, :], 1.0 / nd),
         reads=(name + "r0",), writes=(name + "r0",))
    p.op("scalar", lambda s: s.activation(rows[0:1, 2, :], rows[0:1, 0, :], AF.Square),
         reads=(name + "r0",), writes=(name + "r2",))
    p.op("vector", lambda v: v.tensor_scalar_mul(rows[0:1, 1, :], rows[0:1, 1, :], 1.0 / nd),
         reads=(name + "r1",), writes=(name + "r1",))
    p.op("vector", lambda v: v.tensor_sub(rows[0:1, 1, :], rows[0:1, 1, :], rows[0:1, 2, :]),
         reads=(name + "r1", name + "r2"), writes=(name + "r1",))
    p.op("scalar", lambda s: s.activation(rows[0:1, 1, :], rows[0:1, 1, :], AF.Ln,
                                          bias=cc[eps_key][0:1, :]),
         reads=(name + "r1", "constc"), writes=(name + "r1",))
    p.op("scalar", lambda s: s.activation(rows[0:1, 1, :], rows[0:1, 1, :], AF.Exp,
                                          scale=-0.5),
         reads=(name + "r1",), writes=(name + "r1",))
    p.op("vector", lambda v: v.tensor_mul(rows[0:1, 2, :], rows[0:1, 0, :], rows[0:1, 1, :]),
         reads=(name + "r0", name + "r1"), writes=(name + "r2",))
    p.op("vector", lambda v: v.tensor_scalar_mul(rows[0:1, 2, :], rows[0:1, 2, :], -1.0),
         reads=(name + "r2",), writes=(name + "r2",))
    for tc in range(nt):
        mms = [(lambda t, ps, s, e, tc=tc:
                t.matmul(ps, ones1[:], rows[0:1, 1, tc * 512:(tc + 1) * 512],
                         start=True, stop=True))]
        mr.group(mms, lambda e, ps, tc=tc: copy_any(e, Ab[:, tc * 512:(tc + 1) * 512], ps),
                 reads=(name + "r1", "ones1"), evac_writes=(name + "A",))
        mms = [(lambda t, ps, s, e, tc=tc:
                t.matmul(ps, ones1[:], rows[0:1, 2, tc * 512:(tc + 1) * 512],
                         start=True, stop=True))]
        mr.group(mms, lambda e, ps, tc=tc: copy_any(e, Bb[:, tc * 512:(tc + 1) * 512], ps),
                 reads=(name + "r2", "ones1"), evac_writes=(name + "B",))


def feat_ln_apply(p, name, xb, xbkey, Ab, Bb, scr, dst, dstkey, T=TPC, nchunk=8,
                  gcol=None, bcol=None, dst2=None, dst2key=None):
    W = scr.shape[2]
    for ec in range(nchunk):
        eng = "vector" if ec % 2 == 0 else "gpsimd"
        for c0 in range(0, T, W):
            cs = slice(c0, c0 + W)
            p.op(eng, lambda v, ec=ec, cs=cs: v.tensor_tensor(
                scr[:, ec % 2, :], xb[:, ec, cs], Ab[:, cs], ALU.mult),
                reads=(xbkey, name + "A"), writes=(f"{name}scr{ec % 2}",))
            if gcol is None:
                p.op(eng, lambda v, ec=ec, cs=cs: v.tensor_tensor(
                    dst[:, ec, cs], scr[:, ec % 2, :], Bb[:, cs], ALU.add),
                    reads=(f"{name}scr{ec % 2}", name + "B"), writes=(dstkey,))
            else:
                p.op(eng, lambda v, ec=ec, cs=cs: v.tensor_tensor(
                    scr[:, ec % 2, :], scr[:, ec % 2, :], Bb[:, cs], ALU.add),
                    reads=(f"{name}scr{ec % 2}", name + "B"), writes=(f"{name}scr{ec % 2}",))
                p.op(eng, lambda v, ec=ec, cs=cs: v.tensor_scalar(
                    dst[:, ec, cs], scr[:, ec % 2, :], gcol(ec), bcol(ec), ALU.mult, ALU.add),
                    reads=(f"{name}scr{ec % 2}", "gcols"), writes=(dstkey,))
            if dst2 is not None:
                p.op(eng, lambda v, ec=ec, cs=cs: v.tensor_copy(dst2[:, ec, cs], dst[:, ec, cs]),
                     reads=(dstkey,), writes=(dst2key,))



def build_l1():
    p = Prog()
    x = p.din("x", [TPC, C], F32)
    wqT = p.din("wqT", [C, C], F32)
    wkT = p.din("wkT", [C, C], F32)
    wvT = p.din("wvT", [C, C], F32)
    identd = p.din("identbf", [128, 128], BF16)
    q_f = p.dout("q_f", [C, TPC], BF16)
    k_f = p.dout("k_f", [C, TPC], BF16)
    v_t = p.dout("v_t", [TPC, C], BF16)

    xs = p.sb("xs", [128, NTILE, C], F32)
    w3 = p.sb("w3", [128, 3, 8, C], BF16)
    ident = p.sb("ident", [128, 128], BF16)
    xln = p.sb("xln", [128, 2, C], BF16)
    xT = p.sb("xT", [128, 8, TPC], BF16)
    ost = p.sb("ost", [128, 4, 512], BF16)
    stats = p.sb("stats", [128, 8], F32)
    xc = p.sb("xc", [128, 2, C], F32)
    cc = p.const_cols({"eps": 1e-5})

    ptr_full = [p.ps(f"ptr{i}", [128, 1024], BF16) for i in range(2)]
    ptr = [t[:, 0:128] for t in ptr_full]

    p.dma("sync", xs[:], x.rearrange("(t p) d -> p t d", p=128), writes=("xs",))
    p.dma("sync", ident[:], identd, writes=("ident",))
    for i, w in enumerate((wqT, wkT, wvT)):
        p.dma("gpsimd", w3[:, i], w.rearrange("(c p) e -> p c e", p=128), writes=("w3",))

    for i in range(NTILE):
        b = i % 2
        p.op("vector", lambda v, i=i: v.reduce_sum(stats[:, 7:8], xs[:, i, :], axis=AX),
             reads=("xs",), writes=("st_m",))
        p.op("vector", lambda v: v.tensor_scalar_mul(stats[:, 7:8], stats[:, 7:8], 1.0 / C),
             reads=("st_m",), writes=("st_m",))
        p.op("vector", lambda v, i=i, b=b: v.tensor_scalar(
            xc[:, b, :], xs[:, i, :], stats[:, 7:8], None, ALU.subtract),
            reads=("xs", "st_m"), writes=(f"xc{b}",))
        p.op("scalar", lambda s, b=b: s.activation(xc[:, b, :], xc[:, b, :], AF.Square,
                                                   accum_out=stats[:, 6:7]),
             reads=(f"xc{b}",), writes=(f"xc{b}", "st_v"))
        p.op("scalar", lambda s: s.activation(stats[:, 6:7], stats[:, 6:7], AF.Ln,
                                              scale=1.0 / C, bias=cc["eps"]),
             reads=("st_v", "constc"), writes=("st_v",))
        p.op("scalar", lambda s: s.activation(stats[:, 6:7], stats[:, 6:7], AF.Exp,
                                              scale=-0.5),
             reads=("st_v",), writes=("st_v",))
        p.op("vector", lambda v, i=i, b=b: v.tensor_scalar(
            xln[:, b, :], xs[:, i, :], stats[:, 7:8], stats[:, 6:7],
            ALU.subtract, ALU.mult),
            reads=("xs", "st_m", "st_v"), writes=(f"xln{b}",))
        for c8 in range(8):
            k = i * 8 + c8
            p.op("tensor", lambda t, b=b, c8=c8, k=k: t.transpose(
                ptr[k % 2], xln[:, b, c8 * 128:(c8 + 1) * 128], ident[:]),
                reads=(f"xln{b}", "ident"), writes=(f"ptr{k % 2}",))
            p.op("vector", lambda v, i=i, c8=c8, k=k: v.tensor_copy(
                xT[:, c8, i * 128:(i + 1) * 128], ptr[k % 2]),
                reads=(f"ptr{k % 2}",), writes=("xT",))

    mr = MMRunner(p, "l1")

    def emit_group(mms, dram_ap):
        slot = mr.g % 4
        dst = ost[:, slot, :]
        ev = mr.group(mms, lambda e, ps, dst=dst: copy_any(e, dst, ps),
                      reads=("xT", "w3"), evac_writes=(f"ost{slot}",))
        p.dma("sync", dram_ap, dst, reads=(f"ost{slot}",), ch=1)

    for tc in range(4):
        for w_idx, outap in ((0, q_f), (1, k_f)):
            for ec in range(8):
                mms = [
                    (lambda t, ps, s, e, dc=dc, w_idx=w_idx, ec=ec, tc=tc:
                     t.matmul(ps, w3[:, w_idx, dc, ec * 128:(ec + 1) * 128],
                              xT[:, dc, tc * 512:(tc + 1) * 512], start=s, stop=e))
                    for dc in range(8)
                ]
                emit_group(mms, outap[ec * 128:(ec + 1) * 128, tc * 512:(tc + 1) * 512])
    for ti in range(NTILE):
        for eh in range(2):
            mms = [
                (lambda t, ps, s, e, dc=dc, ti=ti, eh=eh:
                 t.matmul(ps, xT[:, dc, ti * 128:(ti + 1) * 128],
                          w3[:, 2, dc, eh * 512:(eh + 1) * 512], start=s, stop=e))
                for dc in range(8)
            ]
            emit_group(mms, v_t[ti * 128:(ti + 1) * 128, eh * 512:(eh + 1) * 512])

    return p.finish()


S = 8192
NQT = S // 128


def build_l2(Sv=S):
    p = Prog()
    S_ = Sv
    qh = p.din("qh", [256, S_], BF16)
    kh = p.din("kh", [256, S_], BF16)
    vh = p.din("vh", [S_, 256], BF16)
    masks4d = p.din("masks4", [128, 4 * 512], BF16)
    identfd = p.din("identf", [128, 128], F32)
    attn = p.dout("attn", [S_, 256], BF16)

    M0 = 20.0
    NBLK = S_ // 512
    NSL = 16
    qs2 = p.sb("qs2", [128, 4, S_], BF16)
    ks2 = p.sb("ks2", [128, 4, S_], BF16)
    vse = p.sb("vse", [128, S_ // 128, 260], BF16)
    masks4 = p.sb("masks4s", [128, 4, 512], BF16)
    identf = p.sb("identfs", [128, 128], F32)
    probsT = p.sb("probsT", [128, 2, NSL, 512], BF16)
    oev = p.sb("oev", [65, 2, 128], F32)
    rlc = p.sb("rlc", [128, 2, 1], F32)
    ostage = p.sb("ostage", [128, 2, 64], BF16)
    cc = p.const_cols({"negm": -M0})

    ps_s = [[p.ps(f"l2s{ln}_{j}", [128, 512], F32) for j in range(3)]
            for ln in range(2)]
    ps_a = [p.ps(f"l2a{ln}", [128, 512], F32) for ln in range(2)]

    qhR = qh.rearrange("(c p) t -> p c t", p=64)
    khR = kh.rearrange("(c p) t -> p c t", p=64)
    p.op("vector", lambda v: v.memset(qs2[64:128, :, :], 0.0), writes=("qs2z",))
    for hi in range(4):
        p.dma("sync", qs2[0:64, hi, :], qhR[:, hi, :], writes=(f"qs2{hi}",))
        p.dma("sync", ks2[0:64, hi, :], khR[:, hi, :], writes=(f"ks2{hi}",), ch=3)
        p.dma("gpsimd", ks2[64:128, hi, :], khR[:, hi, :],
              writes=(f"ks2{hi}",), ch=1)
    for h4 in range(4):
        p.dma("sync", vse[:, :, h4 * 65:h4 * 65 + 64],
              vh.rearrange("(t p) e -> p t e", p=128)[:, :, h4 * 64:(h4 + 1) * 64],
              writes=(f"vse{h4}",), ch=1)
    p.op("vector", lambda v: v.memset(
        vse[:].rearrange("p t (h c) -> p t h c", c=65)[:, :, :, 64:65], 1.0),
        writes=("vse1",))
    p.dma("sync", masks4[:], masks4d.rearrange("p (t q) -> p t q", q=512),
          writes=("masks4",), ch=2)
    p.dma("sync", identf[:], identfd, writes=("identf",), ch=2)

    SCALE = 64.0 ** -0.5

    def finalize_tile(h, i, ln, PA, tq):
        p.op("vector", lambda v, ln=ln, tq=tq: v.tensor_copy(
            oev[0:65, ln, :], ps_a[ln][0:65, tq * 128:(tq + 1) * 128]),
            reads=(PA,), writes=(f"oev{ln}",))
        PS0 = f"l2s{ln}_0"
        p.op("tensor", lambda t, ln=ln: t.transpose(
            ps_s[ln][0][0:128, 0:65], oev[0:65, ln, :], identf[0:65, 0:65]),
            reads=(f"oev{ln}", "identf"), writes=(PS0,))
        p.op("vector", lambda v, ln=ln: v.reciprocal(
            rlc[:, ln, :], ps_s[ln][0][0:128, 64:65]),
            reads=(PS0,), writes=(f"rlc{ln}",))
        p.op("vector", lambda v, ln=ln: v.tensor_scalar(
            ostage[:, ln, :], ps_s[ln][0][0:128, 0:64], rlc[:, ln, :],
            None, ALU.mult),
            reads=(PS0, f"rlc{ln}"), writes=(f"ostage{ln}",))
        p.dma("sync", attn[i * 128:(i + 1) * 128, h * 64:(h + 1) * 64],
              ostage[:, ln, :], reads=(f"ostage{ln}",), ch=ln)

    def emit_qk_batch(h, ln, B, kb, n):
        q0 = B * 512
        for j in range(n):
            ksb = kb + j
            sb_i = ksb % 3
            PS = f"l2s{ln}_{sb_i}"
            sl = ksb % NSL
            PT = f"pT{ln}_{sl}"
            p.op("tensor", lambda t, h=h, ksb=ksb, q0=q0, ln=ln, sb_i=sb_i:
                 t.matmul(ps_s[ln][sb_i][:, 0:512],
                          ks2[0:128, h, ksb * 128:ksb * 128 + 128],
                          qs2[0:128, h, q0:q0 + 512],
                          start=True, stop=True),
                 reads=(f"ks2{h}", f"qs2{h}", "qs2z"), writes=(PS,))
            p.op("scalar", lambda s, ln=ln, sb_i=sb_i, sl=sl: s.activation(
                probsT[:, ln, sl, :], ps_s[ln][sb_i][:, 0:512], AF.Exp,
                scale=SCALE, bias=cc["negm"]),
                reads=(PS, "constc"), writes=(PT,))
            t4 = ksb - 4 * B
            if t4 >= 0:
                p.op("vector", lambda g, ln=ln, sl=sl, t4=t4: g.tensor_tensor(
                    probsT[:, ln, sl, :], probsT[:, ln, sl, :],
                    masks4[:, t4, :], ALU.mult),
                    reads=(PT, "masks4"), writes=(PT,))

    def emit_pv_batch(h, ln, B, kb, n, nk):
        for j in range(n):
            ksb = kb + j
            sl = ksb % NSL
            PT = f"pT{ln}_{sl}"
            PA = f"l2pa{ln}"
            p.op("tensor", lambda t, ksb=ksb, h=h, ln=ln, sl=sl,
                 first=(ksb == 0), last=(ksb == nk - 1):
                 t.matmul(ps_a[ln][0:65, 0:512],
                          vse[:, ksb, h * 65:(h + 1) * 65],
                          probsT[:, ln, sl, :],
                          start=first, stop=last),
                 reads=(f"vse{h}", "vse1", PT), writes=(PA,))

    for hp in range(2):
        p.begin_lanes(2)
        pend = [None, None]
        for B in range(NBLK):
            for ln in range(2):
                h = 2 * hp + ln
                p.lane(ln)
                nk = 4 * B + 4
                batches = [(kb, min(8, nk - kb)) for kb in range(0, nk, 8)]
                PA = f"l2pa{ln}"
                emit_qk_batch(h, ln, B, *batches[0])
                if pend[ln] is not None:
                    ph, pB = pend[ln]
                    for tq in range(4):
                        finalize_tile(ph, 4 * pB + tq, ln, PA, tq)
                for bi in range(1, len(batches)):
                    emit_qk_batch(h, ln, B, *batches[bi])
                    emit_pv_batch(h, ln, B, *batches[bi - 1], nk)
                emit_pv_batch(h, ln, B, *batches[-1], nk)
                pend[ln] = (h, B)
        for ln in range(2):
            p.lane(ln)
            ph, pB = pend[ln]
            PA = f"l2pa{ln}"
            for tq in range(4):
                finalize_tile(ph, 4 * pB + tq, ln, PA, tq)
        p.end_lanes()
    return p.finish()


def build_l3a():
    p = Prog()
    atT = p.din("atT", [C, TPC], BF16)
    xT = p.din("xT", [C, TPC], F32)
    woT = p.din("woT", [C, C], F32)
    x1T = p.dout("x1T", [C, TPC], F32)
    xln1T = p.dout("xln1T", [C, TPC], BF16)

    wos = p.sb("wos", [128, 8, C], BF16)
    att = p.sb("att", [128, 2, 8, 512], BF16)
    xtt = p.sb("xtt", [128, 2, 8, 512], F32)
    x1b = p.sb("x1b", [128, 8, TPC], BF16)
    xln = p.sb("xlnb", [128, 8, TPC], BF16)
    sq = xln
    Ab = p.sb("Ab", [128, TPC], F32)
    Bb = p.sb("Bb", [128, TPC], F32)
    rows = p.sb("rows", [1, 3, TPC], F32)
    st1 = p.sb("st1", [128, 2, 512], F32)
    scr = p.sb("scr", [128, 2, TPC], F32)
    onesb = p.sb("onesb", [128, 1], BF16)
    ones1 = p.sb("ones1", [1, 128], F32)
    cc = p.const_cols({"eps5": 1e-5})
    p.op("gpsimd", lambda g: g.memset(onesb[:], 1.0), writes=("onesb",))
    p.op("gpsimd", lambda g: g.memset(ones1[:], 1.0), writes=("ones1",))

    p.dma("gpsimd", wos[:], woT.rearrange("(c p) e -> p c e", p=128), writes=("wos",))
    mr = MMRunner(p, "l3a")
    for tc in range(4):
        tb = tc % 2
        p.dma("sync", att[:, tb], atT.rearrange("(c p) t -> p c t", p=128)[:, :, tc * 512:(tc + 1) * 512],
              writes=(f"att{tb}",))
        p.dma("sync", xtt[:, tb], xT.rearrange("(c p) t -> p c t", p=128)[:, :, tc * 512:(tc + 1) * 512],
              writes=(f"xtt{tb}",), ch=1)
        for hc in range(8):
            mms = [(lambda t, ps, s, e, ec=ec, hc=hc, tb=tb:
                    t.matmul(ps, wos[:, ec, hc * 128:(hc + 1) * 128], att[:, tb, ec, :],
                             start=s, stop=e)) for ec in range(8)]
            sslot = mr.g % 2
            ev = mr.group(mms, lambda e, ps, hc=hc, tb=tb, sslot=sslot:
                          e.tensor_tensor(st1[:, sslot, :], ps, xtt[:, tb, hc, :], ALU.add),
                          reads=("wos", f"att{tb}"),
                          evac_reads=(f"xtt{tb}",),
                          evac_writes=(f"st1{sslot}",), evac_eng="vector")
            p.dma("sync", x1T[hc * 128:(hc + 1) * 128, tc * 512:(tc + 1) * 512],
                  st1[:, sslot, :], reads=(f"st1{sslot}",), ch=2)
            p.op("scalar", lambda s, hc=hc, tc=tc, sslot=sslot: s.copy(
                x1b[:, hc, tc * 512:(tc + 1) * 512], st1[:, sslot, :]),
                reads=(f"st1{sslot}",), writes=("x1b",))
    feat_ln(p, mr, "ln1", x1b, "x1b", sq, rows, Ab, Bb, onesb, ones1, cc, "eps5")
    feat_ln_apply(p, "ln1", x1b, "x1b", Ab, Bb, scr, xln, "ln1sq")
    for ec in range(8):
        p.dma("sync", xln1T[ec * 128:(ec + 1) * 128, :], xln[:, ec, :],
              reads=("ln1sq",), ch=2)
    return p.finish()


FH = 3072


def build_l3b():
    p = Prog()
    xlnT = p.din("xlnT", [C, TPC], BF16)
    wgT = p.din("wgT", [C, FH], I32)
    wuT = p.din("wuT", [C, FH], I32)
    hT = p.dout("hT", [FH, TPC], BF16)

    xln = p.sb("xln", [128, 8, TPC], BF16)
    wg = p.sb("wg", [128, 8, FH], BF16)
    wu = p.sb("wu", [128, 8, FH], BF16)
    sg = p.sb("sg", [128, 2, 512], BF16)
    hst = p.sb("hst", [128, 2, 512], BF16)

    p.dma("sync", xln[:], xlnT.rearrange("(c p) t -> p c t", p=128), writes=("xln",))
    p.dma("gpsimd", wg[:], wgT.rearrange("(c p) h -> p c h", p=128), writes=("wg",))
    p.dma("gpsimd", wu[:], wuT.rearrange("(c p) h -> p c h", p=128), writes=("wu",))

    mrs = [MMRunner(p, "l3bA", nbank=2), MMRunner(p, "l3bB", nbank=2)]
    SC = 1.0 / 127.0
    p.begin_lanes(2)
    for hc in range(24):
        slot = hc % 2
        p.lane(slot)
        mr = mrs[slot]
        for tc in range(4):
            gm = [(lambda t, ps, s, e, ec=ec, hc=hc, tc=tc:
                   t.matmul(ps, wg[:, ec, hc * 128:(hc + 1) * 128],
                            xln[:, ec, tc * 512:(tc + 1) * 512], start=s, stop=e))
                  for ec in range(8)]
            mr.group(gm, lambda e, ps, slot=slot: e.activation(
                sg[:, slot, :], ps, AF.Silu, scale=SC),
                reads=("wg", "xln"), evac_writes=(f"sg{slot}",), evac_eng="scalar")
            um = [(lambda t, ps, s, e, ec=ec, hc=hc, tc=tc:
                   t.matmul(ps, wu[:, ec, hc * 128:(hc + 1) * 128],
                            xln[:, ec, tc * 512:(tc + 1) * 512], start=s, stop=e))
                  for ec in range(8)]
            mr.group(um, lambda e, ps, slot=slot: e.scalar_tensor_tensor(
                hst[:, slot, :], ps, SC, sg[:, slot, :], ALU.mult, ALU.mult),
                reads=("wu", "xln"), evac_reads=(f"sg{slot}",),
                evac_writes=(f"hst{slot}",), evac_eng="vector")
            p.dma("sync", hT[hc * 128:(hc + 1) * 128, tc * 512:(tc + 1) * 512],
                  hst[:, slot, :], reads=(f"hst{slot}",), ch=1 + slot)
    p.end_lanes()
    return p.finish()


def build_l3ab():
    p = Prog()
    atT = p.din("atT", [C, TPC], BF16)
    xT = p.din("xT", [C, TPC], F32)
    woT = p.din("woT", [C, C], F32)
    wgT = p.din("wgT", [C, FH], I32)
    wuT = p.din("wuT", [C, FH], I32)
    x1T = p.dout("x1T", [C, TPC], F32)
    hT = p.dout("hT", [FH, TPC], BF16)

    wos = p.sb("wos", [128, 8, C], BF16)
    att = p.sb("att", [128, 2, 8, 512], BF16)
    xtt = p.sb("xtt", [128, 2, 8, 512], F32)
    x1b = p.sb("x1b", [128, 8, TPC], BF16)
    xln = p.sb("xlnb", [128, 8, TPC], BF16)
    sq = xln
    Ab = p.sb("Ab", [128, TPC], F32)
    Bb = p.sb("Bb", [128, TPC], F32)
    rows = p.sb("rows", [1, 3, TPC], F32)
    st1 = p.sb("st1", [128, 2, 512], F32)
    scr = xtt[:].rearrange("p a e t -> p (a e) t")[:, 0:8, :].rearrange(
        "p (a u) t -> p a (u t)", a=2)
    onesb = p.sb("onesb", [128, 1], BF16)
    ones1 = p.sb("ones1", [1, 128], F32)
    wg = p.sb("wg", [128, 2, 2, 8, 128], BF16)
    wu = p.sb("wu", [128, 2, 2, 8, 128], BF16)
    sg = p.sb("sg", [128, 2, 512], BF16)
    hst = p.sb("hst", [128, 2, 512], BF16)
    cc = p.const_cols({"eps5": 1e-5})
    p.op("gpsimd", lambda g: g.memset(onesb[:], 1.0), writes=("onesb",))
    p.op("gpsimd", lambda g: g.memset(ones1[:], 1.0), writes=("ones1",))

    wgR = wgT.rearrange("(c p) h -> p c h", p=128)
    wuR = wuT.rearrange("(c p) h -> p c h", p=128)

    def ld_w(hc):
        ln_, bf = hc % 2, (hc // 2) % 2
        p.dma("gpsimd", wg[:, ln_, bf], wgR[:, :, hc * 128:(hc + 1) * 128],
              writes=(f"wg{ln_}{bf}",), ch=1)
        p.dma("gpsimd", wu[:, ln_, bf], wuR[:, :, hc * 128:(hc + 1) * 128],
              writes=(f"wu{ln_}{bf}",), ch=2)

    p.dma("gpsimd", wos[:], woT.rearrange("(c p) e -> p c e", p=128), writes=("wos",))
    for hc0 in range(4):
        ld_w(hc0)

    mr = MMRunner(p, "l3a")
    for tc in range(4):
        tb = tc % 2
        p.dma("sync", att[:, tb], atT.rearrange("(c p) t -> p c t", p=128)[:, :, tc * 512:(tc + 1) * 512],
              writes=(f"att{tb}",))
        p.dma("sync", xtt[:, tb], xT.rearrange("(c p) t -> p c t", p=128)[:, :, tc * 512:(tc + 1) * 512],
              writes=(f"xtt{tb}",), ch=1)
        for hc in range(8):
            mms = [(lambda t, ps, s, e, ec=ec, hc=hc, tb=tb:
                    t.matmul(ps, wos[:, ec, hc * 128:(hc + 1) * 128], att[:, tb, ec, :],
                             start=s, stop=e)) for ec in range(8)]
            sslot = mr.g % 2
            ev = mr.group(mms, lambda e, ps, hc=hc, tb=tb, sslot=sslot:
                          e.tensor_tensor(st1[:, sslot, :], ps, xtt[:, tb, hc, :], ALU.add),
                          reads=("wos", f"att{tb}"),
                          evac_reads=(f"xtt{tb}",),
                          evac_writes=(f"st1{sslot}",), evac_eng="vector")
            p.dma("sync", x1T[hc * 128:(hc + 1) * 128, tc * 512:(tc + 1) * 512],
                  st1[:, sslot, :], reads=(f"st1{sslot}",), ch=2)
            p.op("scalar", lambda s, hc=hc, tc=tc, sslot=sslot: s.copy(
                x1b[:, hc, tc * 512:(tc + 1) * 512], st1[:, sslot, :]),
                reads=(f"st1{sslot}",), writes=("x1b",))
    feat_ln(p, mr, "ln1", x1b, "x1b", sq, rows, Ab, Bb, onesb, ones1, cc, "eps5")
    feat_ln_apply(p, "ln1", x1b, "x1b", Ab, Bb, scr, xln, "ln1sq")

    mrs = [MMRunner(p, "l3bA", nbank=2), MMRunner(p, "l3bB", nbank=2)]
    SC = 1.0 / 127.0
    p.begin_lanes(2)
    for hc in range(24):
        slot = hc % 2
        p.lane(slot)
        mr2 = mrs[slot]
        bf = (hc // 2) % 2
        for tc in range(4):
            gm = [(lambda t, ps, s, e, ec=ec, tc=tc, slot=slot, bf=bf:
                   t.matmul(ps, wg[:, slot, bf, ec, :],
                            xln[:, ec, tc * 512:(tc + 1) * 512], start=s, stop=e))
                  for ec in range(8)]
            mr2.group(gm, lambda e, ps, slot=slot: e.activation(
                sg[:, slot, :], ps, AF.Silu, scale=SC),
                reads=(f"wg{slot}{bf}", "ln1sq"),
                evac_writes=(f"sg{slot}",), evac_eng="scalar")
            um = [(lambda t, ps, s, e, ec=ec, tc=tc, slot=slot, bf=bf:
                   t.matmul(ps, wu[:, slot, bf, ec, :],
                            xln[:, ec, tc * 512:(tc + 1) * 512], start=s, stop=e))
                  for ec in range(8)]
            mr2.group(um, lambda e, ps, slot=slot: e.scalar_tensor_tensor(
                hst[:, slot, :], ps, SC, sg[:, slot, :], ALU.mult, ALU.mult),
                reads=(f"wu{slot}{bf}", "ln1sq"), evac_reads=(f"sg{slot}",),
                evac_writes=(f"hst{slot}",), evac_eng="vector")
            p.dma("sync", hT[hc * 128:(hc + 1) * 128, tc * 512:(tc + 1) * 512],
                  hst[:, slot, :], reads=(f"hst{slot}",), ch=3 + slot)
        if hc + 4 < 24:
            ld_w(hc + 4)
    p.end_lanes()
    return p.finish()


def build_l3c():
    p = Prog()
    hTd = p.din("hT", [FH, TPC], BF16)
    x1T = p.din("x1T", [C, TPC], F32)
    wdT = p.din("wdT", [FH, C], I32)
    x2T = p.dout("x2T", [C, TPC], F32)
    xln2T = p.dout("xln2T", [C, TPC], BF16)

    wd = p.sb("wd", [128, 24, C], BF16)
    ht = p.sb("ht", [128, 1, 24, 512], BF16)
    x1t = p.sb("x1t", [128, 1, 8, 512], F32)
    x2b = p.sb("x2b", [128, 8, TPC], BF16)
    xln = p.sb("xlnb", [128, 8, TPC], BF16)
    sq = xln
    Ab = p.sb("Ab", [128, TPC], F32)
    Bb = p.sb("Bb", [128, TPC], F32)
    rows = p.sb("rows", [1, 3, TPC], F32)
    st1 = p.sb("st1", [128, 2, 512], F32)
    scr = p.sb("scr", [128, 2, TPC // 2], F32)
    onesb = p.sb("onesb", [128, 1], BF16)
    ones1 = p.sb("ones1", [1, 128], F32)
    cc = p.const_cols({"eps5": 1e-5})
    p.op("gpsimd", lambda g: g.memset(onesb[:], 1.0), writes=("onesb",))
    p.op("gpsimd", lambda g: g.memset(ones1[:], 1.0), writes=("ones1",))

    wdR = wdT.rearrange("(c p) e -> p c e", p=128)
    for ecw in range(8):
        p.dma("gpsimd", wd[:, :, ecw * 128:(ecw + 1) * 128],
              wdR[:, :, ecw * 128:(ecw + 1) * 128],
              writes=(f"wd{ecw}",), ch=ecw % 3)
    mr = MMRunner(p, "l3c")
    SC = 1.0 / 127.0
    for tc in range(4):
        tb = 0
        p.dma("sync", ht[:, tb], hTd.rearrange("(c p) t -> p c t", p=128)[:, :, tc * 512:(tc + 1) * 512],
              writes=(f"ht{tb}",))
        p.dma("sync", x1t[:, tb], x1T.rearrange("(c p) t -> p c t", p=128)[:, :, tc * 512:(tc + 1) * 512],
              writes=(f"x1t{tb}",), ch=1)
        for ec in range(8):
            mms = [(lambda t, ps, s, e, hc=hc, ec=ec, tb=tb:
                    t.matmul(ps, wd[:, hc, ec * 128:(ec + 1) * 128], ht[:, tb, hc, :],
                             start=s, stop=e)) for hc in range(24)]
            sslot = mr.g % 2
            mr.group(mms, lambda e, ps, ec=ec, tb=tb, sslot=sslot:
                     e.scalar_tensor_tensor(st1[:, sslot, :], ps, SC, x1t[:, tb, ec, :],
                                            ALU.mult, ALU.add),
                     reads=(f"wd{ec}", f"ht{tb}"), evac_reads=(f"x1t{tb}",),
                     evac_writes=(f"st1{sslot}",), evac_eng="vector")
            p.dma("sync", x2T[ec * 128:(ec + 1) * 128, tc * 512:(tc + 1) * 512],
                  st1[:, sslot, :], reads=(f"st1{sslot}",), ch=2)
            p.op("scalar", lambda s, ec=ec, tc=tc, sslot=sslot: s.copy(
                x2b[:, ec, tc * 512:(tc + 1) * 512], st1[:, sslot, :]),
                reads=(f"st1{sslot}",), writes=("x2b",))
    feat_ln(p, mr, "ln2", x2b, "x2b", sq, rows, Ab, Bb, onesb, ones1, cc, "eps5")
    feat_ln_apply(p, "ln2", x2b, "x2b", Ab, Bb, scr, xln, "ln2sq")
    for ec in range(8):
        p.dma("sync", xln2T[ec * 128:(ec + 1) * 128, :], xln[:, ec, :],
              reads=("ln2sq",), ch=2)
    return p.finish()


E1 = 1228
NHC = 10


def hc_w(hc):
    return 128 if hc < 9 else E1 - 9 * 128


def build_l4a():
    p = Prog()
    xlnT = p.din("xlnT", [C, TPC], BF16)
    ins = {}
    for nm in ("r", "k", "v"):
        ins[nm] = (p.din(nm + "W1", [C, E1], F32),
                   p.din(nm + "b1c", [128, 9], F32),
                   p.din(nm + "b1t", [76, 1], F32),
                   p.din(nm + "W2", [E1, C], F32),
                   p.din(nm + "b2c", [128, 8], F32),
                   p.din(nm + "muc", [128, 8], F32))
    outs = {nm: p.dout(nm + "_f", [C, TPC], BF16) for nm in ("r", "k", "v")}

    xln = p.sb("xln", [128, 8, TPC], BF16)
    w1 = p.sb("w1", [128, 2, 8, E1], BF16)
    w2 = p.sb("w2", [128, 2, NHC, C], BF16)
    b1 = p.sb("b1", [128, 2, NHC], F32)
    b2 = p.sb("b2", [128, 2, 8], F32)
    muc = p.sb("muc", [128, 2, 8], BF16)
    r1 = p.sb("r1", [128, NHC, TPC], BF16)
    ost = p.sb("ost", [128, 2, 512], BF16)

    p.dma("sync", xln[:], xlnT.rearrange("(c p) t -> p c t", p=128), writes=("xln",))
    mr = MMRunner(p, "l4a")

    def ld_mlp(mi):
        nm = ("r", "k", "v")[mi]
        sl = mi % 2
        W1d, b1d, b1td, W2d, b2d, mud = ins[nm]
        W1R = W1d.rearrange("(c p) h -> p c h", p=128)
        for hcw in range(NHC):
            ww = hc_w(hcw)
            p.dma("gpsimd", w1[:, sl, :, hcw * 128:hcw * 128 + ww],
                  W1R[:, :, hcw * 128:hcw * 128 + ww],
                  writes=(f"w1{sl}_{hcw}",), ch=0)
        p.dma("gpsimd", w2[0:76, sl, 9], W2d[9 * 128:E1, :], writes=(f"w2{sl}",), ch=1)
        p.dma("gpsimd", w2[:, sl, 0:9], W2d[0:9 * 128, :].rearrange("(c p) e -> p c e", p=128),
              writes=(f"w2{sl}",), ch=1)
        p.dma("sync", b1[:, sl, 0:9], b1d, writes=(f"b1{sl}",), ch=1)
        p.dma("sync", b1[0:76, sl, 9:10], b1td, writes=(f"b1{sl}",), ch=1)
        p.dma("sync", b2[:, sl], b2d, writes=(f"b2{sl}",), ch=1)
        p.dma("gpsimd", muc[:, sl], mud, writes=(f"muc{sl}",), ch=2)

    ld_mlp(0)
    for mi, nm in enumerate(("r", "k", "v")):
        sl = mi % 2
        if mi + 1 < 3:
            ld_mlp(mi + 1)
        for hc in range(NHC):
            w = hc_w(hc)
            mms = [(lambda t, ps, s, e, dc=dc, hc=hc, w=w, sl=sl:
                    t.matmul(ps[0:w, 0:1], w1[:, sl, dc, hc * 128:hc * 128 + w],
                             muc[:, sl, dc:dc + 1], start=s, stop=e)) for dc in range(8)]
            mr.group(mms, lambda e, ps, hc=hc, w=w, sl=sl: e.tensor_tensor(
                b1[0:w, sl, hc:hc + 1], b1[0:w, sl, hc:hc + 1], ps[0:w, 0:1], ALU.add),
                reads=(f"w1{sl}_{hc}", f"muc{sl}"), evac_writes=(f"b1{sl}",),
                evac_eng="vector")
        for hc in range(NHC):
            w = hc_w(hc)
            for tc in range(4):
                mms = [(lambda t, ps, s, e, dc=dc, hc=hc, tc=tc, w=w, sl=sl:
                        t.matmul(ps[0:w, :], w1[:, sl, dc, hc * 128:hc * 128 + w],
                                 xln[:, dc, tc * 512:(tc + 1) * 512], start=s, stop=e))
                       for dc in range(8)]
                mr.group(mms, lambda e, ps, hc=hc, tc=tc, w=w, sl=sl: e.activation(
                    r1[0:w, hc, tc * 512:(tc + 1) * 512], ps[0:w, :], AF.Gelu,
                    bias=b1[0:w, sl, hc:hc + 1]),
                    reads=(f"w1{sl}_{hc}", "xln"), evac_reads=(f"b1{sl}",),
                    evac_writes=("r1",), evac_eng="scalar")
        for ec in range(8):
            for tc in range(4):
                mms = [(lambda t, ps, s, e, hc=hc, ec=ec, tc=tc, w=hc_w(hc), sl=sl:
                        t.matmul(ps, w2[0:w, sl, hc, ec * 128:(ec + 1) * 128],
                                 r1[0:w, hc, tc * 512:(tc + 1) * 512], start=s, stop=e))
                       for hc in range(NHC)]
                slot = mr.g % 2
                mr.group(mms, lambda e, ps, ec=ec, slot=slot, sl=sl: e.tensor_scalar(
                    ost[:, slot, :], ps, b2[:, sl, ec:ec + 1], None, ALU.add),
                    reads=(f"w2{sl}", "r1"), evac_reads=(f"b2{sl}",),
                    evac_writes=(f"ost{slot}",), evac_eng="vector")
                p.dma("sync", outs[nm][ec * 128:(ec + 1) * 128, tc * 512:(tc + 1) * 512],
                      ost[:, slot, :], reads=(f"ost{slot}",), ch=3)
    return p.finish()


DECAY_C = 0.7408182206817179


def build_l4b():
    p = Prog()
    xlnT = p.din("xlnT", [C, TPC], BF16)
    kT = p.din("kT", [C, TPC], BF16)
    lr = {}
    for nm in ("a", "d", "g"):
        lr[nm] = (p.din(nm + "A", [C, 128], F32), p.din(nm + "B", [128, C], F32),
                  p.din(nm + "bc", [128, 8], F32), p.din(nm + "muc", [128, 8], F32))
    remc_d = p.din("remc_d", [128, 8], F32)
    mixc_d = p.din("mixc_d", [128, 8], F32)
    sel2d = p.din("sel2_d", [2, 128], F32)
    hsel_d = p.din("hsel_d", [128, 2], F32)
    outs = {nm: p.dout(nm, [C, TPC], BF16) for nm in ("kap_f", "ktil_f", "rho_f", "gate_f")}
    d_f = p.dout("d_f", [C, TPC], F32)

    xln = p.sb("xln", [128, 8, TPC], BF16)
    kf = p.sb("kf", [128, 8, TPC], BF16)
    wA = p.sb("wA", [128, 8, 128], BF16)
    wB = p.sb("wB", [128, C], BF16)
    bias = p.sb("bias", [128, 8], F32)
    muc = p.sb("muc", [128, 8], BF16)
    m1 = p.sb("m1", [128, 1], BF16)
    u1 = p.sb("u1", [128, TPC], BF16)
    af = p.sb("af", [128, 8, TPC], BF16)
    df = p.sb("df", [128, 2, TPC], F32)
    remc = p.sb("remc", [128, 8], F32)
    mixc = p.sb("mixc", [128, 8], F32)
    omix = p.sb("omix", [128, 8], F32)
    sel2 = p.sb("sel2", [2, 128], F32)
    hsel = p.sb("hsel", [128, 2], F32)
    ns = p.sb("ns", [2, 2, TPC], F32)
    scr = p.sb("scr", [128, 4, 512], F32)
    scr4 = p.sb("scr4", [128, 4, 512], F32)
    ost = p.sb("ost", [128, 2, 512], BF16)
    ost2 = p.sb("ost2", [128, 6, 512], BF16)
    cc = p.const_cols({"tiny": 1e-24})

    p.dma("sync", xln[:], xlnT.rearrange("(c p) t -> p c t", p=128), writes=("xln",))
    p.dma("sync", kf[:], kT.rearrange("(c p) t -> p c t", p=128), writes=("kf",))
    p.dma("sync", sel2[:], sel2d, writes=("sel2",))
    p.dma("sync", hsel[:], hsel_d, writes=("hsel",))
    p.dma("sync", remc[:], remc_d, writes=("remc",), ch=1)
    p.dma("sync", mixc[:], mixc_d, writes=("mixc",), ch=1)
    p.op("vector", lambda v: v.tensor_scalar(omix[:], mixc[:], -1.0, 1.0, ALU.mult, ALU.add),
         reads=("mixc",), writes=("omix",))

    mr = MMRunner(p, "l4b")
    psK = [p.ps(f"l4bk{i}", [128, 512], F32) for i in range(2)]

    for nm in ("a", "d", "g"):
        Ad, Bd, bd, mud = lr[nm]
        p.dma("gpsimd", wA[:], Ad.rearrange("(c p) r -> p c r", p=128), writes=("wA",))
        p.dma("gpsimd", wB[:], Bd, writes=("wB",))
        p.dma("sync", bias[:], bd, writes=("bias",), ch=1)
        p.dma("gpsimd", muc[:], mud, writes=("muc",))
        mms = [(lambda t, ps, s, e, dc=dc:
                t.matmul(ps[:, 0:1], wA[:, dc, :], muc[:, dc:dc + 1], start=s, stop=e))
               for dc in range(8)]
        mr.group(mms, lambda e, ps: copy_any(e, m1[:], ps[:, 0:1]),
                 reads=("wA", "muc"), evac_writes=("m1",), evac_eng="vector")
        for ec in range(8):
            mms = [(lambda t, ps, s, e, ec=ec:
                    t.matmul(ps[:, 0:1], wB[:, ec * 128:(ec + 1) * 128], m1[:],
                             start=s, stop=e))]
            mr.group(mms, lambda e, ps, ec=ec: e.tensor_tensor(
                bias[:, ec:ec + 1], bias[:, ec:ec + 1], ps[:, 0:1], ALU.add),
                reads=("wB", "m1"), evac_writes=("bias",), evac_eng="vector")
        for tc in range(4):
            mms = [(lambda t, ps, s, e, dc=dc, tc=tc:
                    t.matmul(ps, wA[:, dc, :], xln[:, dc, tc * 512:(tc + 1) * 512],
                             start=s, stop=e)) for dc in range(8)]
            mr.group(mms, lambda e, ps, tc=tc: copy_any(e, u1[:, tc * 512:(tc + 1) * 512], ps),
                     reads=("wA", "xln"), evac_writes=("u1",), evac_eng="vector")
        for ec in range(8):
            for tc in range(4):
                mms = [(lambda t, ps, s, e, ec=ec, tc=tc:
                        t.matmul(ps, wB[:, ec * 128:(ec + 1) * 128],
                                 u1[:, tc * 512:(tc + 1) * 512], start=s, stop=e))]
                if nm == "a":
                    ev = (lambda e, ps, ec=ec, tc=tc: e.activation(
                        af[:, ec, tc * 512:(tc + 1) * 512], ps, AF.Sigmoid,
                        bias=bias[:, ec:ec + 1]))
                    wkey = "af"
                elif nm == "g":
                    ev = (lambda e, ps, ec=ec, tc=tc: e.activation(
                        ost[:, 0, :], ps, AF.Sigmoid, bias=bias[:, ec:ec + 1]))
                    wkey = "ost0"
                else:
                    ev = (lambda e, ps, ec=ec, tc=tc: e.activation(
                        scr[:, 2, :], ps, AF.Tanh, bias=bias[:, ec:ec + 1]))
                    wkey = "scr2"
                mr.group(mms, ev, reads=("wB", "u1"), evac_reads=("bias",),
                         evac_writes=(wkey,), evac_eng="scalar")
                if nm == "g":
                    p.dma("sync", outs["gate_f"][ec * 128:(ec + 1) * 128, tc * 512:(tc + 1) * 512],
                          ost[:, 0, :], reads=("ost0",), ch=2)
                elif nm == "d":
                    p.op("scalar", lambda s, ec=ec, tc=tc: s.activation(
                        scr[:, 2, :], scr[:, 2, :], AF.Sigmoid),
                        reads=("scr2",), writes=("scr2",))
                    p.op("scalar", lambda s, ec=ec, tc=tc: s.activation(
                        df[:, ec % 2, tc * 512:(tc + 1) * 512], scr[:, 2, :], AF.Exp,
                        scale=-DECAY_C),
                        reads=("scr2",), writes=(f"df{ec % 2}",))
            if nm == "d":
                p.dma("sync", d_f[ec * 128:(ec + 1) * 128, :], df[:, ec % 2, :],
                      reads=(f"df{ec % 2}",), ch=2)

    p.begin_lanes(2)
    for ec in range(8):
        p.lane(ec % 2)
        LN_ = ec % 2
        for tc in range(4):
            p.op("vector", lambda v, ec=ec, tc=tc, LN_=LN_: v.tensor_scalar(
                scr[:, 2 * LN_, :], kf[:, ec, tc * 512:(tc + 1) * 512],
                remc[:, ec:ec + 1], None, ALU.mult),
                reads=("kf", "remc"), writes=(f"scr0_{LN_}",))
            p.op("scalar", lambda s, LN_=LN_: s.activation(scr[:, 2 * LN_ + 1, :], scr[:, 2 * LN_, :], AF.Square),
                 reads=(f"scr0_{LN_}",), writes=(f"scr1_{LN_}",))
            p.op("tensor", lambda t, LN_=LN_: t.matmul(
                psK[LN_][0:2, 0:512], hsel[:], scr[:, 2 * LN_ + 1, :],
                start=True, stop=True),
                reads=(f"scr1_{LN_}", "hsel"), writes=(f"l4bk{LN_}",))
            p.op("vector", lambda v, tc=tc, LN_=LN_: v.tensor_copy(
                ns[0:2, LN_, tc * 512:(tc + 1) * 512], psK[LN_][0:2, 0:512]),
                reads=(f"l4bk{LN_}",), writes=(f"ns{LN_}",))
        p.op("vector", lambda v, LN_=LN_: v.tensor_scalar_max(
            ns[:, LN_, :], ns[:, LN_, :], cc["tiny"][0:2, :]),
            reads=(f"ns{LN_}", "constc"), writes=(f"ns{LN_}",))
        p.op("scalar", lambda s, LN_=LN_: s.activation(ns[:, LN_, :], ns[:, LN_, :], AF.Ln),
             reads=(f"ns{LN_}",), writes=(f"ns{LN_}",))
        p.op("scalar", lambda s, LN_=LN_: s.activation(ns[:, LN_, :], ns[:, LN_, :],
                                                       AF.Exp, scale=-0.5),
             reads=(f"ns{LN_}",), writes=(f"ns{LN_}",))
        for tc in range(4):
            p.op("tensor", lambda t, tc=tc, LN_=LN_: t.matmul(
                psK[LN_][:, 0:512], sel2[:], ns[0:2, LN_, tc * 512:(tc + 1) * 512],
                start=True, stop=True),
                reads=(f"ns{LN_}", "sel2"), writes=(f"l4bk{LN_}",))
            p.op("vector", lambda v, LN_=LN_: v.tensor_copy(
                scr4[:, LN_, :], psK[LN_][:, 0:512]),
                reads=(f"l4bk{LN_}",), writes=(f"scr3_{LN_}",))
            p.op("vector", lambda v, ec=ec, tc=tc, LN_=LN_: v.tensor_scalar(
                scr[:, 2 * LN_, :], kf[:, ec, tc * 512:(tc + 1) * 512],
                remc[:, ec:ec + 1], None, ALU.mult),
                reads=("kf", "remc"), writes=(f"scr0_{LN_}",))
            p.op("vector", lambda v, LN_=LN_: v.tensor_tensor(scr[:, 2 * LN_ + 1, :], scr[:, 2 * LN_, :],
                                                     scr4[:, LN_, :], ALU.mult),
                 reads=(f"scr0_{LN_}", f"scr3_{LN_}"), writes=(f"scr1_{LN_}",))
            p.op("vector", lambda v, LN_=LN_: v.tensor_copy(ost2[:, LN_, :], scr[:, 2 * LN_ + 1, :]),
                 reads=(f"scr1_{LN_}",), writes=(f"ost2_{LN_}",))
            p.dma("sync", outs["kap_f"][ec * 128:(ec + 1) * 128, tc * 512:(tc + 1) * 512],
                  ost2[:, LN_, :], reads=(f"ost2_{LN_}",), ch=2 + LN_)
            p.op("vector", lambda v, ec=ec, tc=tc, LN_=LN_: v.tensor_tensor(
                ost2[:, 4 + LN_, :], scr[:, 2 * LN_ + 1, :], af[:, ec, tc * 512:(tc + 1) * 512], ALU.mult),
                reads=(f"scr1_{LN_}", "af"), writes=(f"ost2_{4 + LN_}",))
            p.dma("sync", outs["ktil_f"][ec * 128:(ec + 1) * 128, tc * 512:(tc + 1) * 512],
                  ost2[:, 4 + LN_, :], reads=(f"ost2_{4 + LN_}",), ch=2 + LN_)
            p.op("gpsimd", lambda v, ec=ec, tc=tc, LN_=LN_: v.tensor_scalar(
                scr4[:, 2 + LN_, :], af[:, ec, tc * 512:(tc + 1) * 512],
                mixc[:, ec:ec + 1], omix[:, ec:ec + 1], ALU.mult, ALU.add),
                reads=("af", "mixc", "omix"), writes=(f"scr2_{LN_}",))
            p.op("gpsimd", lambda v, ec=ec, tc=tc, LN_=LN_: v.tensor_tensor(
                ost2[:, 2 + LN_, :], scr4[:, 2 + LN_, :],
                kf[:, ec, tc * 512:(tc + 1) * 512], ALU.mult),
                reads=(f"scr2_{LN_}", "kf"), writes=(f"ost2_{2 + LN_}",))
            p.dma("sync", outs["rho_f"][ec * 128:(ec + 1) * 128, tc * 512:(tc + 1) * 512],
                  ost2[:, 2 + LN_, :], reads=(f"ost2_{2 + LN_}",), ch=2 + LN_)
    p.end_lanes()
    return p.finish()


def build_l5(Sv=S):
    p = Prog()
    CH = 128
    NCH = Sv // CH
    kapD = p.din("kapP", [256, Sv], BF16)
    ktlD = p.din("ktlP", [256, Sv], BF16)
    rhoD = p.din("rhoP", [256, Sv], BF16)
    vD = p.din("vP", [256, Sv], BF16)
    rD = p.din("rP", [256, Sv], BF16)
    dD = p.din("dP", [256, Sv], F32)
    mABd = p.din("mABd", [128, 256], F32)
    mSUd = p.din("mSUd", [128, 128], F32)
    mIUd = p.din("mIUd", [128, 128], F32)
    identd = p.din("identbf", [128, 128], BF16)
    yD = p.dout("yP", [256, Sv], F32)

    WIN = 512
    NW = Sv // WIN
    win = {nm: p.sb("w_" + nm, [128 if nm != "d" else 64, 2, 4, WIN],
                     BF16 if nm != "d" else F32)
           for nm in ("kap", "ktl", "rho", "v", "r", "d")}
    mAB = p.sb("mAB", [128, 256], F32)
    mSU = p.sb("mSU", [128, 128], F32)
    mIU = p.sb("mIU", [128, 128], F32)
    zz = p.sb("zz", [64, CH], F32)
    identf = p.sb("identf_s", [128, 128], BF16)

    def mk(nm, shape, dt=BF16):
        return [p.sb(f"{nm}{pr}", shape, dt) for pr in range(4)]
    def mk2(nm, shape, dt=BF16):
        return [[p.sb(f"{nm}{pr}c{cp}", shape, dt) for pr in range(4)]
                for cp in range(2)]
    SW = mk("SW", [128, 64]); SWT = mk("SWT", [128, 64])
    dp = mk2("dp", [64, CH], F32)
    dpp = mk("dpp", [64, CH], F32); rdp = mk("rdp", [64, CH], F32)
    Ktd = mk("Ktd", [128, CH]); KVd = mk("KVd", [128, 2 * CH])
    Kdt = mk2("Kdt", [CH, 64]); Vdt = mk2("Vdt", [CH, 64]); Pt = mk2("Pt", [CH, 64])
    AT = mk("AT", [CH, 2 * CH]); Bm = mk("Bm", [CH, CH]); Tt = mk("Tt", [CH, CH])
    Z = mk2("Z", [CH, 128])
    SQ = [p.sb(f"SQ{pr}", [CH, 11, CH], BF16) for pr in range(4)]
    Wmf = mk2("Wmf", [128, CH]); Mf = mk2("Mf", [128, CH])
    C1T = mk2("C1T", [CH, CH]); D2 = mk2("D2", [CH, CH]); M1s = mk2("M1s", [CH, CH])
    Q0 = mk("Q0", [128, CH]); Ht = mk("Ht", [CH, 64])
    T2 = mk("T2", [128, 64])
    YS = mk("YS", [64, CH], F32)

    ps5 = [p.ps(f"l5p{i}", [128, 512], F32) for i in range(4)]
    pk5 = [f"l5p{i}" for i in range(4)]
    ps_acc = [p.ps(f"l5a{i}", [128, 512], F32) for i in range(4)]
    pk_acc = [f"l5a{i}" for i in range(4)]

    def bank(pr):
        return ps5[pr], pk5[pr]

    p.dma("sync", mAB[:], mABd, writes=("mAB",))
    p.dma("sync", mSU[:], mSUd, writes=("mSU",))
    p.dma("sync", mIU[:], mIUd, writes=("mIU",))
    p.dma("sync", identf[:], identd, writes=("identf",))
    p.op("vector", lambda v: v.memset(zz[:], 0.0), writes=("zz",))
    for nm in ("kap", "ktl", "rho", "v", "r"):
        p.op("vector", lambda v, nm=nm: v.memset(win[nm][64:128], 0.0),
             writes=tuple(f"win_{nm}{pr}_{sl}" for pr in range(4) for sl in range(2)))
    for pr in range(4):
        P = f"p{pr}"
        p.op("vector", lambda v, pr=pr: v.memset(Ktd[pr][64:128, :], 0.0),
             writes=(P + "Ktd",))
        p.op("vector", lambda v, pr=pr: v.memset(KVd[pr][64:128, :], 0.0),
             writes=(P + "KVd",))
        p.op("vector", lambda v, pr=pr: v.memset(Q0[pr][64:128, :], 0.0),
             writes=(P + "Q0",))
        p.op("vector", lambda v, pr=pr: v.memset(T2[pr][64:128, :], 0.0),
             writes=(P + "T2",))
        for cp in range(2):
            P2 = f"p{pr}c{cp}"
            p.op("vector", lambda v, pr=pr, cp=cp: v.memset(
                Wmf[cp][pr][64:128, :], 0.0), writes=(P2 + "Wmf",))
            p.op("vector", lambda v, pr=pr, cp=cp: v.memset(
                Mf[cp][pr][64:128, :], 0.0), writes=(P2 + "Mf",))
    for pr in range(4):
        p.op("vector", lambda v, pr=pr: v.memset(SW[pr][:], 0.0), writes=(f"SW{pr}",))
        p.op("vector", lambda v, pr=pr: v.memset(SWT[pr][:], 0.0), writes=(f"SWT{pr}",))

    def ld_window(w):
        c0 = w * WIN
        sl = w % 2
        for nm, dram, eng in (("kap", kapD, "gpsimd"), ("ktl", ktlD, "gpsimd"),
                              ("rho", rhoD, "gpsimd"), ("v", vD, "gpsimd"),
                              ("r", rD, "gpsimd"), ("d", dD, "sync")):
            for pr in range(4):
                p.dma(eng, win[nm][0:64, sl, pr, :] if nm != "d"
                      else win[nm][:, sl, pr, :],
                      dram[pr * 64:(pr + 1) * 64, c0:c0 + WIN],
                      writes=(f"win_{nm}{pr}_{sl}",), ch=pr % 3)

    def mm(lhs, rhs, out_sl, reads, wkey, start=True, stop=True, bankref=None):
        if bankref is None:
            bk, key = bank(pr_cur[0])
        else:
            bk, key = bankref
        p.op("tensor", lambda t, lhs=lhs, rhs=rhs, bk=bk, sl=out_sl, s=start, e=stop:
             t.matmul(bk[sl], lhs, rhs, start=s, stop=e),
             reads=reads, writes=(key,))
        return bk, key

    def tr2(in_ap, reads, wkey_out, out_sb, scale_col=None, scale_key=None,
            ihr=None):
        bk, key = bank(pr_cur[0])
        ih, iw = in_ap.shape[0], in_ap.shape[1]
        ihr = ihr or ih
        p.op("tensor", lambda t, i=in_ap, bk=bk, ih=ih, iw=iw, ihr=ihr: t.matmul(
            bk[0:iw, 0:ihr], i, identf[0:ih, 0:ihr], start=True, stop=True),
            reads=reads + ("identf",), writes=(key,))
        if scale_col is None:
            p.op("scalar", lambda v, bk=bk, o=out_sb, ihr=ihr, iw=iw: v.copy(
                o[:], bk[0:iw, 0:ihr]), reads=(key,), writes=(wkey_out,))
        else:
            p.op("vector", lambda v, bk=bk, o=out_sb, sc=scale_col, ihr=ihr, iw=iw:
                 v.tensor_scalar(o[:], bk[0:iw, 0:ihr], sc, None, ALU.mult),
                 reads=(key, scale_key), writes=(wkey_out,))

    pr_cur = [0]
    ld_window(0)
    for w in range(NW):
        if w + 1 < NW:
            ld_window(w + 1)
        for ci in range(WIN // CH):
            cn = w * (WIN // CH) + ci
            cs = slice(ci * CH, (ci + 1) * CH)
            p.begin_lanes(8)
            cp = cn % 2
            for pr in range(4):
                p.lane(pr * 2 + cp)
                pr_cur[0] = pr
                P = f"p{pr}"
                P2 = f"p{pr}c{cp}"
                sl = w % 2
                wk = tuple(f"win_{nm}{pr}_{sl}" for nm in
                           ("kap", "ktl", "rho", "v", "r", "d"))
                Kf = win["kap"][0:64, sl, pr, cs]
                Ktf = win["ktl"][0:64, sl, pr, cs]
                Pf = win["rho"][:, sl, pr, cs]; Vf = win["v"][0:64, sl, pr, cs]
                Rf = win["r"][:, sl, pr, cs]; Df = win["d"][:, sl, pr, cs]
                p.op("vector", lambda v, pr=pr, Df=Df, cp=cp: v.tensor_tensor_scan(
                    dp[cp][pr][:], Df, zz[:], 1.0, ALU.mult, ALU.add),
                    reads=(wk[5], "zz"), writes=(P2 + "dp",))
                p.op("vector", lambda v, pr=pr: v.memset(dpp[pr][:, 0:1], 1.0),
                     writes=(P + "dpp",))
                p.op("gpsimd", lambda v, pr=pr, cp=cp: v.tensor_copy(
                    dpp[pr][:, 1:CH], dp[cp][pr][:, 0:CH - 1]),
                    reads=(P2 + "dp",), writes=(P + "dpp",))
                p.op("vector", lambda v, pr=pr, cp=cp: v.reciprocal(rdp[pr][:], dp[cp][pr][:]),
                     reads=(P2 + "dp",), writes=(P + "rdp",))
                p.op("gpsimd", lambda v, pr=pr, Ktf=Ktf: v.tensor_tensor(
                    Ktd[pr][0:64, :], Ktf, dpp[pr][:], ALU.mult),
                    reads=(wk[1], P + "dpp"), writes=(P + "Ktd",))
                p.op("gpsimd", lambda v, pr=pr, Kf=Kf: v.tensor_tensor(
                    KVd[pr][0:64, 0:CH], Kf, rdp[pr][:], ALU.mult),
                    reads=(wk[0], P + "rdp"), writes=(P + "KVd",))
                p.op("gpsimd", lambda v, pr=pr, Vf=Vf: v.tensor_tensor(
                    KVd[pr][0:64, CH:2 * CH], Vf, rdp[pr][:], ALU.mult),
                    reads=(wk[3], P + "rdp"), writes=(P + "KVd",))
                tr2(KVd[pr][:, 0:CH], (P + "KVd",), P2 + "Kdt", Kdt[cp][pr], ihr=64)
                p.op("vector", lambda v, pr=pr, cp=cp: v.tensor_scalar_mul(
                    Kdt[cp][pr][:], Kdt[cp][pr][:], -1.0),
                    reads=(P2 + "Kdt",), writes=(P2 + "Kdt",))
                tr2(KVd[pr][:, CH:2 * CH], (P + "KVd",), P2 + "Vdt", Vdt[cp][pr], ihr=64)
                bkp, keyp = bank(pr)
                p.op("tensor", lambda t, Pf=Pf, bk=bkp: t.matmul(
                    bk[0:CH, 0:64], Pf, identf[0:128, 0:64], start=True,
                    stop=True), reads=(wk[2], "identf"), writes=(keyp,))
                p.op("scalar", lambda v, bk=bkp, pr=pr, cp=cp: v.copy(
                    Pt[cp][pr][:], bk[0:CH, 0:64]), reads=(keyp,), writes=(P2 + "Pt",))
                bk1, key1 = mm(Ktd[pr][:], KVd[pr][:], (slice(0, CH), slice(0, 2 * CH)),
                               (P + "Ktd", P + "KVd"), None)
                p.op("vector", lambda v, bk=bk1, pr=pr: v.tensor_tensor(
                    AT[pr][:], bk[0:CH, 0:2 * CH], mAB[:], ALU.mult),
                    reads=(key1, "mAB"), writes=(P + "AT",))
                bk2, key2 = mm(KVd[pr][:, 0:CH], Ktd[pr][:], (slice(0, CH), slice(0, CH)),
                               (P + "KVd", P + "Ktd"), None)
                p.op("vector", lambda v, bk=bk2, pr=pr: v.tensor_tensor(
                    Bm[pr][:], bk[0:CH, 0:CH], mSU[:], ALU.mult),
                    reads=(key2, "mSU"), writes=(P + "Bm",))
                bk2b, key2b = mm(KVd[pr][:, CH:2 * CH], Ktd[pr][:],
                                 (slice(0, CH), slice(0, CH)),
                                 (P + "KVd", P + "Ktd"), None)
                p.op("vector", lambda v, bk=bk2b, pr=pr: v.tensor_tensor(
                    Tt[pr][:], bk[0:CH, 0:CH], mSU[:], ALU.mult),
                    reads=(key2b, "mSU"), writes=(P + "Tt",))
                bk3, key3 = bank(pr)
                p.op("tensor", lambda t, pr=pr, bk=bk3: t.matmul(
                    bk[0:CH, 0:64], Ktd[pr][:], identf[0:128, 0:64],
                    start=True, stop=True),
                    reads=(P + "Ktd", "identf"), writes=(key3,))
                p.op("scalar", lambda v, bk=bk3, pr=pr, cp=cp: v.copy(
                    Z[cp][pr][:, 0:64], bk[0:CH, 0:64]), reads=(key3,), writes=(P2 + "Z",))
                bk4, key4 = mm(Tt[pr][:], Pt[cp][pr][:], (slice(0, CH), slice(0, 64)),
                               (P + "Tt", P2 + "Pt"), None)
                p.op("scalar", lambda v, bk=bk4, pr=pr, cp=cp: v.copy(
                    Z[cp][pr][:, 64:128], bk[0:CH, 0:64]), reads=(key4,), writes=(P2 + "Z",))
                sq = SQ[pr]
                sqk = P + "SQ"
                Asl = AT[pr][:, 0:CH]
                Bsl = Bm[pr][:]

                def apply_z(lh, rkeys, sub=False):
                    bkx, keyx = mm(lh, Z[cp][pr][:], (slice(0, CH), slice(0, 128)),
                                   rkeys + (P2 + "Z",), None)
                    p.op("vector", lambda v, bk=bkx, pr=pr, sub=sub, cp=cp: v.tensor_tensor(
                        Z[cp][pr][:], Z[cp][pr][:], bk[0:CH, 0:128],
                        ALU.subtract if sub else ALU.add),
                        reads=(keyx, P2 + "Z"), writes=(P2 + "Z",))

                apply_z(Bsl, (P + "Bm",), sub=True)
                for (lh, rh, slot) in ((Bsl, Asl, 0), (Asl, Bsl, 1)):
                    bkx, keyx = mm(lh, rh, (slice(0, CH), slice(0, CH)),
                                   (P + "AT", P + "Bm"), None)
                    p.op("scalar", lambda v, bk=bkx, sq=sq, slot=slot: v.copy(
                        sq[:, slot, :], bk[0:CH, 0:CH]), reads=(keyx,), writes=(sqk,))
                apply_z(sq[:, 1, :], (sqk,))
                for lev in range(4):
                    a = 2 * lev; b = 2 * lev + 1
                    for (li, ri, slot) in ((b, a, a + 2), (a, b, b + 2)):
                        bkx, keyx = mm(sq[:, li, :], sq[:, ri, :],
                                       (slice(0, CH), slice(0, CH)), (sqk,), None)
                        p.op("scalar", lambda v, bk=bkx, sq=sq, slot=slot: v.copy(
                            sq[:, slot, :], bk[0:CH, 0:CH]), reads=(keyx,), writes=(sqk,))
                    apply_z(sq[:, 2 * lev + 3, :], (sqk,))
                bkx, keyx = mm(sq[:, 8, :], sq[:, 9, :], (slice(0, CH), slice(0, CH)),
                               (sqk,), None)
                p.op("scalar", lambda v, bk=bkx, sq=sq: v.copy(
                    sq[:, 10, :], bk[0:CH, 0:CH]), reads=(keyx,), writes=(sqk,))
                apply_z(sq[:, 10, :], (sqk,))
                tr2(Z[cp][pr][:, 0:64], (P2 + "Z",), P2 + "Wmf", Wmf[cp][pr][0:64, :])
                tr2(Z[cp][pr][:, 64:128], (P2 + "Z",), P2 + "Mf", Mf[cp][pr][0:64, :])
                bkx, keyx = mm(Pf, Rf, (slice(0, CH), slice(0, CH)),
                               (wk[2], wk[4]), None)
                p.op("vector", lambda v, bk=bkx, pr=pr, cp=cp: v.tensor_tensor(
                    C1T[cp][pr][:], bk[0:CH, 0:CH], mIU[:], ALU.mult),
                    reads=(keyx, "mIU"), writes=(P2 + "C1T",))
                bkx, keyx = mm(Mf[cp][pr][:], Rf, (slice(0, CH), slice(0, CH)),
                               (P2 + "Mf", wk[4]), None)
                p.op("vector", lambda v, bk=bkx, pr=pr, cp=cp: v.tensor_tensor(
                    D2[cp][pr][:], bk[0:CH, 0:CH], mIU[:], ALU.mult),
                    reads=(keyx, "mIU"), writes=(P2 + "D2",))
                bkq, keyq = mm(SW[pr][:], Rf, (slice(0, 64), slice(0, CH)),
                               (f"SW{pr}", wk[4]), None)
                p.op("scalar", lambda s, bk=bkq, pr=pr: s.copy(
                    Q0[pr][0:64, :], bk[0:64, 0:CH]), reads=(keyq,), writes=(P + "Q0",))
                bkx, keyx = mm(Wmf[cp][pr][:], Q0[pr][:], (slice(0, CH), slice(0, CH)),
                               (P2 + "Wmf", P + "Q0"), None)
                p.op("vector", lambda v, bk=bkx, pr=pr, cp=cp: v.tensor_tensor(
                    M1s[cp][pr][:], bk[0:CH, 0:CH], mIU[:], ALU.mult),
                    reads=(keyx, "mIU"), writes=(P2 + "M1s",))
                p.op("vector", lambda v, pr=pr, cp=cp: v.tensor_tensor(
                    D2[cp][pr][:], D2[cp][pr][:], M1s[cp][pr][:], ALU.add),
                    reads=(P2 + "M1s", P2 + "D2"), writes=(P2 + "D2",))
                bky, keyy = mm(Vdt[cp][pr][:], C1T[cp][pr][:], (slice(0, 64), slice(0, CH)),
                               (P2 + "Vdt", P2 + "C1T"), None, start=True, stop=False,
                               bankref=(ps_acc[pr], pk_acc[pr]))
                mm(Kdt[cp][pr][:], D2[cp][pr][:], (slice(0, 64), slice(0, CH)),
                   (P2 + "Kdt", P2 + "D2"), None, start=False, stop=True,
                   bankref=(bky, keyy))
                p.op("vector", lambda v, bk=bky, pr=pr: v.tensor_tensor(
                    YS[pr][:], bk[0:64, 0:CH], Q0[pr][0:64, :], ALU.add),
                    reads=(keyy, P + "Q0"), writes=(P + "YS",))
                p.op("vector", lambda v, pr=pr, cp=cp: v.tensor_tensor(
                    YS[pr][:], YS[pr][:], dp[cp][pr][:], ALU.mult),
                    reads=(P + "YS", P2 + "dp"), writes=(P + "YS",))
                p.dma("sync", yD[pr * 64:(pr + 1) * 64, cn * CH:(cn + 1) * CH],
                      YS[pr][:], reads=(P + "YS",), ch=pr)
                bkh, keyh = mm(Wmf[cp][pr][:], SWT[pr][:], (slice(0, CH), slice(0, 64)),
                               (P2 + "Wmf", f"SWT{pr}"), None)
                p.op("scalar", lambda s, bk=bkh, pr=pr: s.copy(
                    Ht[pr][:], bk[0:CH, 0:64]), reads=(keyh,), writes=(P + "Ht",))
                p.op("vector", lambda v, pr=pr, cp=cp: v.tensor_tensor(
                    Ht[pr][:], Ht[pr][:], Z[cp][pr][:, 64:128], ALU.add),
                    reads=(P + "Ht", P2 + "Z"), writes=(P + "Ht",))
                bk6, key6 = mm(Pt[cp][pr][:], Vdt[cp][pr][:], (slice(0, 64), slice(0, 64)),
                               (P2 + "Pt", P2 + "Vdt"), None, start=True, stop=False,
                               bankref=(ps_acc[pr], pk_acc[pr]))
                mm(Ht[pr][:], Kdt[cp][pr][:], (slice(0, 64), slice(0, 64)),
                   (P + "Ht", P2 + "Kdt"), None, start=False, stop=True,
                   bankref=(bk6, key6))
                p.op("vector", lambda v, bk=bk6, pr=pr: v.tensor_tensor(
                    T2[pr][0:64, :], SW[pr][0:64, :], bk[0:64, 0:64], ALU.add),
                    reads=(key6, f"SW{pr}"), writes=(P + "T2",))
                tr2(T2[pr][:], (P + "T2",), f"SWT{pr}", SWT[pr][0:64, :],
                    scale_col=dp[cp][pr][:, CH - 1:CH], scale_key=P2 + "dp",
                    ihr=64)
                tr2(SWT[pr][:], (f"SWT{pr}",), f"SW{pr}", SW[pr][0:64, :],
                    ihr=64)
            p.end_lanes()
    return p.finish()


def build_l6a():
    p = Prog()
    yT = p.din("yT", [C, TPC], F32)
    rT = p.din("rT", [C, TPC], BF16)
    rhoT = p.din("rhoT", [C, TPC], BF16)
    vT = p.din("vT", [C, TPC], BF16)
    gT = p.din("gT", [C, TPC], BF16)
    lngc = p.din("lngc", [128, 8], F32)
    lnbc = p.din("lnbc", [128, 8], F32)
    bmc = p.din("bmc", [128, 8], F32)
    hsel_d = p.din("hsel_d", [128, 2], F32)
    sel2d = p.din("sel2_d", [2, 128], F32)
    zT = p.dout("zT", [C, TPC], BF16)

    rc = p.sb("rc", [128, TPC], BF16)
    rhoc = p.sb("rhoc", [128, TPC], BF16)
    vc = p.sb("vc", [128, TPC], BF16)
    gc = p.sb("gc", [128, TPC], BF16)
    yc = p.sb("yc", [128, TPC], F32)
    ybb = p.sb("ybb", [128, 8, TPC], BF16)
    sq = ybb
    sqb = p.sb("sqq", [128, 8, TPC], BF16)
    lg = p.sb("lg", [128, 8], F32)
    lb = p.sb("lb", [128, 8], F32)
    bm02 = p.sb("bm02", [128, 8], F32)
    hsel = p.sb("hsel", [128, 2], F32)
    sel2 = p.sb("sel2", [2, 128], F32)
    ns = p.sb("ns", [2, TPC], F32)
    scr = p.sb("scr", [128, 2, TPC // 2], F32)
    scrq = p.sb("scrq", [128, 512], F32)
    Ab = p.sb("Ab", [128, TPC], F32)
    Bb = p.sb("Bb", [128, TPC], F32)
    rows = p.sb("rows", [1, 3, TPC], F32)
    zst = p.sb("zst", [128, 2, TPC], BF16)
    onesb = p.sb("onesb", [128, 1], BF16)
    ones1 = p.sb("ones1", [1, 128], F32)
    cc = p.const_cols({"eps6": 1e-6})
    p.op("gpsimd", lambda g: g.memset(onesb[:], 1.0), writes=("onesb",))
    p.op("gpsimd", lambda g: g.memset(ones1[:], 1.0), writes=("ones1",))
    p.dma("sync", lg[:], lngc, writes=("gcols",), ch=1)
    p.dma("sync", lb[:], lnbc, writes=("gcols",), ch=1)
    p.dma("sync", bm02[:], bmc, writes=("bm02",), ch=1)
    p.dma("sync", hsel[:], hsel_d, writes=("hsel",), ch=1)
    p.dma("sync", sel2[:], sel2d, writes=("sel2",), ch=1)
    p.op("vector", lambda v: v.tensor_scalar_mul(bm02[:], bm02[:], 0.2),
         reads=("bm02",), writes=("bm02",))

    mr = MMRunner(p, "l6a")
    yTr = yT.rearrange("(c p) t -> p c t", p=128)
    for ec in range(8):
        p.dma("sync", rc[:], rT.rearrange("(c p) t -> p c t", p=128)[:, ec], writes=("rc",))
        p.dma("sync", rhoc[:], rhoT.rearrange("(c p) t -> p c t", p=128)[:, ec],
              writes=("rhoc",), ch=1)
        p.dma("sync", vc[:], vT.rearrange("(c p) t -> p c t", p=128)[:, ec],
              writes=("vc",), ch=2)
        p.dma("gpsimd", yc[:], yTr[:, ec], writes=("yc",))
        for tc in range(4):
            cs = slice(tc * 512, (tc + 1) * 512)
            p.op("vector", lambda v, ec=ec, cs=cs: v.tensor_tensor(
                scrq[:], rc[:, cs], rhoc[:, cs], ALU.mult),
                reads=("rc", "rhoc"), writes=("scrq",))
            mms = [(lambda t, ps, s, e: t.matmul(ps[0:2, :], hsel[:], scrq[:],
                                                 start=s, stop=e))]
            mr.group(mms, lambda e, ps, cs=cs: copy_any(e, ns[0:2, cs], ps[0:2, :]),
                     reads=("scrq", "hsel"), evac_writes=("ns",), evac_eng="vector")
        for tc in range(4):
            cs = slice(tc * 512, (tc + 1) * 512)
            mms = [(lambda t, ps, s, e, cs=cs: t.matmul(ps, sel2[:], ns[0:2, cs],
                                                        start=s, stop=e))]

            def bonus_ev(e, ps, ec=ec, cs=cs):
                return e.tensor_tensor(scrq[:], ps, vc[:, cs], ALU.mult)
            mr.group(mms, bonus_ev, reads=("ns", "sel2"), evac_reads=("vc",),
                     evac_writes=("scrq",), evac_eng="vector")
            p.op("vector", lambda v, ec=ec: v.tensor_scalar(
                scrq[:], scrq[:], bm02[:, ec:ec + 1], None, ALU.mult),
                reads=("scrq", "bm02"), writes=("scrq",))
            p.op("vector", lambda v, ec=ec, cs=cs: v.tensor_tensor(
                ybb[:, ec, cs], scrq[:], yc[:, cs], ALU.add),
                reads=("scrq", "yc"), writes=("ybb",))
    feat_ln(p, mr, "ln6", ybb, "ybb", sqb, rows, Ab, Bb, onesb, ones1, cc, "eps6")
    for ec in range(8):
        p.dma("sync", gc[:], gT.rearrange("(c p) t -> p c t", p=128)[:, ec], writes=("gc",))
        for half in range(2):
            cs = slice(half * 1024, (half + 1) * 1024)
            sb2 = half % 2
            p.op("vector", lambda v, ec=ec, cs=cs, sb2=sb2: v.tensor_tensor(
                scr[:, sb2, :], ybb[:, ec, cs], Ab[:, cs], ALU.mult),
                reads=("ybb", "ln6A"), writes=(f"scr{sb2}",))
            p.op("vector", lambda v, ec=ec, cs=cs, sb2=sb2: v.tensor_tensor(
                scr[:, sb2, :], scr[:, sb2, :], Bb[:, cs], ALU.add),
                reads=(f"scr{sb2}", "ln6B"), writes=(f"scr{sb2}",))
            p.op("vector", lambda v, ec=ec, cs=cs, sb2=sb2: v.tensor_scalar(
                scr[:, sb2, :], scr[:, sb2, :], lg[:, ec:ec + 1], lb[:, ec:ec + 1],
                ALU.mult, ALU.add),
                reads=(f"scr{sb2}", "gcols"), writes=(f"scr{sb2}",))
            p.op("vector", lambda v, ec=ec, cs=cs, sb2=sb2: v.tensor_tensor(
                zst[:, ec % 2, cs], scr[:, sb2, :], gc[:, cs], ALU.mult),
                reads=(f"scr{sb2}", "gc"), writes=(f"zst{ec % 2}",))
        p.dma("sync", zT[ec * 128:(ec + 1) * 128, :], zst[:, ec % 2, :],
              reads=(f"zst{ec % 2}",), ch=2)
    return p.finish()


def build_l6b():
    p = Prog()
    zTd = p.din("zT", [C, TPC], BF16)
    x2T = p.din("x2T", [C, TPC], F32)
    oW1 = p.din("oW1", [C, E1], F32)
    ob1c = p.din("ob1c", [128, 9], F32)
    ob1t = p.din("ob1t", [76, 1], F32)
    oW2 = p.din("oW2", [E1, C], F32)
    ob2c = p.din("ob2c", [128, 8], F32)
    x3T = p.dout("x3T", [C, TPC], F32)
    xln3T = p.dout("xln3T", [C, TPC], BF16)

    zs = p.sb("zs", [128, 8, TPC], BF16)
    w1 = p.sb("w1", [128, 8, E1], BF16)
    w2 = p.sb("w2", [128, NHC, C], BF16)
    b1 = p.sb("b1", [128, NHC], F32)
    b2 = p.sb("b2", [128, 8], F32)
    r1 = p.sb("r1", [128, NHC, TPC], BF16)
    x2t = p.sb("x2t", [128, 2, 512], F32)
    x3b = zs
    xln = p.sb("xlnb", [128, 8, TPC], BF16)
    sq = xln
    Ab = p.sb("Ab", [128, TPC], F32)
    Bb = p.sb("Bb", [128, TPC], F32)
    rows = p.sb("rows", [1, 3, TPC], F32)
    st1 = p.sb("st1", [128, 2, 512], F32)
    scr = p.sb("scr", [128, 2, 512], F32)
    onesb = p.sb("onesb", [128, 1], BF16)
    ones1 = p.sb("ones1", [1, 128], F32)
    cc = p.const_cols({"eps5": 1e-5})
    p.op("gpsimd", lambda g: g.memset(onesb[:], 1.0), writes=("onesb",))
    p.op("gpsimd", lambda g: g.memset(ones1[:], 1.0), writes=("ones1",))

    p.dma("sync", zs[:], zTd.rearrange("(c p) t -> p c t", p=128), writes=("zs",))
    w1R = oW1.rearrange("(c p) h -> p c h", p=128)
    for hcw in range(NHC):
        ww = hc_w(hcw)
        p.dma("gpsimd", w1[:, :, hcw * 128:hcw * 128 + ww],
              w1R[:, :, hcw * 128:hcw * 128 + ww], writes=(f"w1{hcw}",), ch=hcw % 2)
    p.dma("gpsimd", w2[:, 0:9], oW2[0:9 * 128, :].rearrange("(c p) e -> p c e", p=128),
          writes=("w2",))
    p.dma("gpsimd", w2[0:76, 9], oW2[9 * 128:E1, :], writes=("w2",))
    p.dma("sync", b1[:, 0:9], ob1c, writes=("b1",), ch=1)
    p.dma("sync", b1[0:76, 9:10], ob1t, writes=("b1",), ch=1)
    p.dma("sync", b2[:], ob2c, writes=("b2",), ch=1)

    mr = MMRunner(p, "l6b")
    for hc in range(NHC):
        w = hc_w(hc)
        for tc in range(4):
            mms = [(lambda t, ps, s, e, dc=dc, hc=hc, tc=tc, w=w:
                    t.matmul(ps[0:w, :], w1[:, dc, hc * 128:hc * 128 + w],
                             zs[:, dc, tc * 512:(tc + 1) * 512], start=s, stop=e))
                   for dc in range(8)]
            mr.group(mms, lambda e, ps, hc=hc, tc=tc, w=w: e.activation(
                r1[0:w, hc, tc * 512:(tc + 1) * 512], ps[0:w, :], AF.Gelu,
                bias=b1[0:w, hc:hc + 1]),
                reads=(f"w1{hc}", "zs"), evac_reads=("b1",),
                evac_writes=("r1",), evac_eng="scalar")
    x2Tr = x2T.rearrange("(c p) t -> p c t", p=128)
    for tc in range(4):
        for ec in range(8):
            slot = mr.g % 2
            p.dma("sync", x2t[:, slot, :], x2Tr[:, ec, tc * 512:(tc + 1) * 512],
                  writes=(f"x2t{slot}",), ch=2)
            mms = [(lambda t, ps, s, e, hc=hc, ec=ec, tc=tc, w=hc_w(hc):
                    t.matmul(ps, w2[0:w, hc, ec * 128:(ec + 1) * 128],
                             r1[0:w, hc, tc * 512:(tc + 1) * 512], start=s, stop=e))
                   for hc in range(NHC)]

            def ev(e, ps, ec=ec, slot=slot):
                return e.scalar_tensor_tensor(st1[:, slot, :], ps, b2[:, ec:ec + 1],
                                              x2t[:, slot, :], ALU.add, ALU.add)
            mr.group(mms, ev, reads=("w2", "r1"), evac_reads=("b2", f"x2t{slot}"),
                     evac_writes=(f"st1{slot}",), evac_eng="vector")
            p.dma("sync", x3T[ec * 128:(ec + 1) * 128, tc * 512:(tc + 1) * 512],
                  st1[:, slot, :], reads=(f"st1{slot}",), ch=3)
            p.op("scalar", lambda s, ec=ec, tc=tc, slot=slot: s.copy(
                x3b[:, ec, tc * 512:(tc + 1) * 512], st1[:, slot, :]),
                reads=(f"st1{slot}",), writes=("zs",))
    feat_ln(p, mr, "ln3", x3b, "zs", sq, rows, Ab, Bb, onesb, ones1, cc, "eps5")
    feat_ln_apply(p, "ln3", x3b, "zs", Ab, Bb, scr, xln, "ln3sq")
    for ec in range(8):
        p.dma("sync", xln3T[ec * 128:(ec + 1) * 128, :], xln[:, ec, :],
              reads=("ln3sq",), ch=2)
    return p.finish()


def build_l7():
    p = Prog()
    xlnT = p.din("xlnT", [C, TPC], BF16)
    x3T = p.din("x3T", [C, TPC], F32)
    gwD = p.din("gwD", [C, 8], F32)
    w1D = p.din("w1D", [8, C, 512], F32)
    w3D = p.din("w3D", [8, C, 512], F32)
    w2D = p.din("w2D", [8, 512, C], F32)
    identd = p.din("identf", [128, 128], F32)
    outT = p.dout("outT", [C, TPC], F32)

    xln = p.sb("xln", [128, 8, TPC], BF16)
    gw = p.sb("gw", [128, 8, 8], BF16)
    w1 = p.sb("w1e", [128, 2, 8, 512], BF16)
    w3 = p.sb("w3e", [128, 2, 8, 512], BF16)
    w2 = p.sb("w2e", [128, 2, 4, C], BF16)
    identf = p.sb("identf_s", [128, 128], F32)
    lgt = p.sb("lgt", [128, 16, 8], F32)
    eqs = p.sb("eqs", [128, 8], F32)
    p2s = p.sb("p2s", [128, 8], F32)
    scs = p.sb("scs", [128, 8], F32)
    sfx = p.sb("sfx", [1, 8, TPC], BF16)
    Sb = p.sb("Sbc", [128, TPC], BF16)
    hb = p.sb("hb", [128, 4, TPC], BF16)
    sg = p.sb("sg", [128, 2, 512], BF16)
    scr2 = p.sb("scr2", [128, 2, 512], F32)
    yac = p.sb("yac", [128, 8, TPC], F32)
    x3t = p.sb("x3t", [128, 2, 512], F32)
    st1 = scr2
    stt = p.sb("sttl", [128, 6], F32)
    ones1 = p.sb("ones1", [1, 128], BF16)
    p.op("gpsimd", lambda g: g.memset(ones1[:], 1.0), writes=("ones1",))

    xlR = xlnT.rearrange("(c p) t -> p c t", p=128)
    for tq4 in range(4):
        p.dma("sync", xln[:, :, tq4 * 512:(tq4 + 1) * 512],
              xlR[:, :, tq4 * 512:(tq4 + 1) * 512], writes=(f"xln{tq4}",))
    p.dma("gpsimd", gw[:], gwD.rearrange("(c p) e -> p c e", p=128), writes=("gw",))
    p.dma("sync", identf[:], identd, writes=("identf",), ch=1)

    mrs = [MMRunner(p, "l7A", nbank=2), MMRunner(p, "l7B", nbank=2)]
    mr = mrs[0]
    for ti in range(NTILE):
        mms = [(lambda t, ps, s, e, dc=dc, ti=ti:
                t.matmul(ps[:, 0:8], xln[:, dc, ti * 128:(ti + 1) * 128],
                         gw[:, dc, :], start=s, stop=e)) for dc in range(8)]
        mr.group(mms, lambda e, ps, ti=ti: copy_any(e, lgt[:, ti, :], ps[:, 0:8]),
                 reads=(f"xln{ti // 4}", "gw"), evac_writes=("lgt",),
                 evac_eng="vector")
    for ti in range(NTILE):
        p.op("vector", lambda v, ti=ti: v.reduce_max(stt[:, 0:1], lgt[:, ti, :], axis=AX),
             reads=("lgt",), writes=("sm",))
        p.op("vector", lambda v: v.tensor_scalar_mul(stt[:, 1:2], stt[:, 0:1], -1.0),
             reads=("sm",), writes=("snm",))
        p.op("scalar", lambda s, ti=ti: s.activation(
            lgt[:, ti, :], lgt[:, ti, :], AF.Exp, bias=stt[:, 1:2],
            accum_out=stt[:, 2:3]),
            reads=("lgt", "snm"), writes=("lgt", "ssum"))
        p.op("vector", lambda v: v.reciprocal(stt[:, 3:4], stt[:, 2:3]),
             reads=("ssum",), writes=("srl",))
        p.op("vector", lambda v, ti=ti: v.tensor_scalar(
            lgt[:, ti, :], lgt[:, ti, :], stt[:, 3:4], None, ALU.mult),
            reads=("lgt", "srl"), writes=("lgt",))
        p.op("vector", lambda v, ti=ti: v.reduce_max(stt[:, 0:1], lgt[:, ti, :], axis=AX),
             reads=("lgt",), writes=("sm",))
        p.op("vector", lambda v, ti=ti: v.tensor_scalar(
            eqs[:], lgt[:, ti, :], stt[:, 0:1], None, ALU.is_ge),
            reads=("lgt", "sm"), writes=("eqs",))
        p.op("vector", lambda v, ti=ti: v.scalar_tensor_tensor(
            p2s[:], eqs[:], -2.0, lgt[:, ti, :], ALU.mult, ALU.add),
            reads=("eqs", "lgt"), writes=("p2s",))
        p.op("vector", lambda v: v.reduce_max(stt[:, 4:5], p2s[:], axis=AX),
             reads=("p2s",), writes=("top2",))
        p.op("vector", lambda v, ti=ti: v.tensor_scalar(
            eqs[:], lgt[:, ti, :], stt[:, 4:5], None, ALU.is_ge),
            reads=("lgt", "top2"), writes=("eqs",))
        p.op("vector", lambda v, ti=ti: v.tensor_tensor(
            scs[:], lgt[:, ti, :], eqs[:], ALU.mult),
            reads=("lgt", "eqs"), writes=("scs",))
        for e8 in range(8):
            bkx = mr.banks[mr.g % mr.nbank]
            keyx = mr.bkeys[mr.g % mr.nbank]
            mr.g += 1
            p.op("tensor", lambda t, bk=bkx, e8=e8: t.transpose(
                bk[0:1, 0:128], scs[:, e8:e8 + 1], identf[:]),
                reads=("scs", "identf"), writes=(keyx,))
            p.op("vector", lambda v, bk=bkx, ti=ti, e8=e8: v.tensor_copy(
                sfx[0:1, e8, ti * 128:(ti + 1) * 128], bk[0:1, 0:128]),
                reads=(keyx,), writes=("sfx",))

    def ld_exp(e8):
        sl = e8 % 2
        p.dma("gpsimd", w1[:, sl], w1D[e8].rearrange("(c p) h -> p c h", p=128),
              writes=(f"w1e{sl}",), ch=0)
        p.dma("gpsimd", w3[:, sl], w3D[e8].rearrange("(c p) h -> p c h", p=128),
              writes=(f"w3e{sl}",), ch=1)
        p.dma("gpsimd", w2[:, sl], w2D[e8].rearrange("(c p) v -> p c v", p=128),
              writes=(f"w2e{sl}",), ch=2)

    ld_exp(0)
    for e8 in range(8):
        sl = e8 % 2
        if e8 + 1 < 8:
            ld_exp(e8 + 1)
        for tc in range(4):
            cs = slice(tc * 512, (tc + 1) * 512)
            mms = [(lambda t, ps, s, e, e8=e8, cs=cs:
                    t.matmul(ps[:, 0:512], ones1[:], sfx[0:1, e8, cs], start=s, stop=e))]
            mr.group(mms, lambda e, ps, cs=cs: copy_any(e, Sb[:, cs], ps[:, 0:512]),
                     reads=("sfx", "ones1"), evac_writes=("Sbc",), evac_eng="vector")
        p.begin_lanes(2)
        for hc in range(4):
            slot = hc % 2
            p.lane(slot)
            mr2 = mrs[slot]
            for tc in range(4):
                cs = slice(tc * 512, (tc + 1) * 512)
                gm = [(lambda t, ps, s, e, dc=dc, hc=hc, cs=cs, sl=sl:
                       t.matmul(ps, w1[:, sl, dc, hc * 128:(hc + 1) * 128],
                                xln[:, dc, cs], start=s, stop=e))
                      for dc in range(8)]

                def ev1(e, ps, slot=slot, cs=cs):
                    return e.tensor_tensor(scr2[:, slot, :], ps, Sb[:, cs], ALU.mult)
                mr2.group(gm, ev1, reads=(f"w1e{sl}", f"xln{tc}"), evac_reads=("Sbc",),
                          evac_writes=(f"scr2{slot}",), evac_eng="vector")
                p.op("scalar", lambda s, slot=slot: s.activation(
                    sg[:, slot, :], scr2[:, slot, :], AF.Silu),
                    reads=(f"scr2{slot}",), writes=(f"sg{slot}",))
                um = [(lambda t, ps, s, e, dc=dc, hc=hc, cs=cs, sl=sl:
                       t.matmul(ps, w3[:, sl, dc, hc * 128:(hc + 1) * 128],
                                xln[:, dc, cs], start=s, stop=e))
                      for dc in range(8)]

                def ev3(e, ps, slot=slot, cs=cs):
                    return e.tensor_tensor(scr2[:, slot, :], ps, Sb[:, cs], ALU.mult)
                mr2.group(um, ev3, reads=(f"w3e{sl}", f"xln{tc}"), evac_reads=("Sbc",),
                          evac_writes=(f"scr2{slot}",), evac_eng="vector")
                p.op("vector", lambda v, slot=slot, hc=hc, cs=cs: v.tensor_tensor(
                    hb[:, hc, cs], scr2[:, slot, :], sg[:, slot, :], ALU.mult),
                    reads=(f"scr2{slot}", f"sg{slot}"), writes=(f"hb{slot}",))
        p.end_lanes()
        p.begin_lanes(2)
        for ec in range(8):
            p.lane(ec % 2)
            mr2 = mrs[ec % 2]
            for tc in range(4):
                mms = [(lambda t, ps, s, e, hc=hc, ec=ec, tc=tc, sl=sl:
                        t.matmul(ps, w2[:, sl, hc, ec * 128:(ec + 1) * 128],
                                 hb[:, hc, tc * 512:(tc + 1) * 512], start=s, stop=e))
                       for hc in range(4)]
                if e8 == 0:
                    ev = (lambda e, ps, ec=ec, tc=tc: copy_any(
                        e, yac[:, ec, tc * 512:(tc + 1) * 512], ps))
                else:
                    ev = (lambda e, ps, ec=ec, tc=tc: e.tensor_tensor(
                        yac[:, ec, tc * 512:(tc + 1) * 512],
                        yac[:, ec, tc * 512:(tc + 1) * 512], ps, ALU.add))
                mr2.group(mms, ev, reads=(f"w2e{sl}", "hb0", "hb1"),
                          evac_writes=(f"yac{ec % 2}",), evac_eng="vector")
        p.end_lanes()
    x3Tr = x3T.rearrange("(c p) t -> p c t", p=128)
    for tc in range(4):
        for ec in range(8):
            slot = ec % 2
            p.dma("sync", x3t[:, slot, :], x3Tr[:, ec, tc * 512:(tc + 1) * 512],
                  writes=(f"x3t{slot}",), ch=2)
            p.op("vector", lambda v, ec=ec, tc=tc, slot=slot: v.tensor_tensor(
                st1[:, slot, :], yac[:, ec, tc * 512:(tc + 1) * 512],
                x3t[:, slot, :], ALU.add),
                reads=("yac0", "yac1", f"x3t{slot}"), writes=(f"scr2{slot}",))
            p.dma("sync", outT[ec * 128:(ec + 1) * 128, tc * 512:(tc + 1) * 512],
                  st1[:, slot, :], reads=(f"scr2{slot}",), ch=1)
    return p.finish()


_NC_CACHE = {}


def _get_nc(name, builder):
    if name not in _NC_CACHE:
        _NC_CACHE[name] = builder()
    return _NC_CACHE[name]


def _run(name, builder, in_maps):
    nc = _get_nc(name, builder)
    res = run_bass_kernel_spmd(nc, in_maps, core_ids=list(range(NCORE)))
    return res.results


def _split_tokens(arr):
    out = []
    for c in range(NCORE):
        b, qq = c // 4, c % 4
        out.append(np.ascontiguousarray(arr[b, qq * TPC:(qq + 1) * TPC]))
    return out


def _colsof(vec):
    return np.ascontiguousarray(np.asarray(vec).reshape(8, 128).T)


def kernel(**inputs):
    I = {k: np.asarray(v) for k, v in inputs.items()}
    x = I["L0_attn_x"]
    ident_bf = np.eye(128, dtype=BF)
    ident_f = np.eye(128, dtype=np.float32)

    wqT = np.ascontiguousarray(I["L0_attn_wq"].T)
    wkT = np.ascontiguousarray(I["L0_attn_wk"].T)
    wvT = np.ascontiguousarray(I["L0_attn_wv"].T)
    r1 = _run("l1", build_l1,
              [{"x": xs, "wqT": wqT, "wkT": wkT, "wvT": wvT, "identbf": ident_bf}
               for xs in _split_tokens(x)])

    qf = [np.concatenate([r1[4 * b + q]["q_f"] for q in range(4)], axis=1) for b in range(2)]
    kf = [np.concatenate([r1[4 * b + q]["k_f"] for q in range(4)], axis=1) for b in range(2)]
    vt = [np.concatenate([r1[4 * b + q]["v_t"] for q in range(4)], axis=0) for b in range(2)]

    tri = np.triu(np.ones((128, 128), np.float32))
    m4 = np.zeros((128, 4, 512), np.float32)
    for t in range(4):
        m4[:, t, t * 128:(t + 1) * 128] = tri
        m4[:, t, (t + 1) * 128:] = 1.0
    masks4 = np.ascontiguousarray(m4.reshape(128, 2048)).astype(BF)
    in2 = []
    for c in range(NCORE):
        b, hg = c // 4, c % 4
        rs = slice(hg * 256, (hg + 1) * 256)
        in2.append({"qh": np.ascontiguousarray(qf[b][rs]),
                    "kh": np.ascontiguousarray(kf[b][rs]),
                    "vh": np.ascontiguousarray(vt[b][:, rs]),
                    "masks4": masks4, "identf": ident_f})
    r2 = _run("l2", build_l2, in2)
    atf = [np.concatenate([r2[4 * b + hg]["attn"] for hg in range(4)], axis=1)
           for b in range(2)]

    woT = np.ascontiguousarray(I["L0_attn_wo"].T)
    wgT = np.ascontiguousarray(I["L0_ffn_w_gate"].T)
    wuT = np.ascontiguousarray(I["L0_ffn_w_up"].T)
    in3 = []
    for c in range(NCORE):
        b, qq = c // 4, c % 4
        ts_ = slice(qq * TPC, (qq + 1) * TPC)
        in3.append({"atT": np.ascontiguousarray(atf[b][ts_].T),
                    "xT": np.ascontiguousarray(x[b, ts_].T),
                    "woT": woT, "wgT": wgT, "wuT": wuT})
    r3ab = _run("l3ab", build_l3ab, in3)

    wdT = np.ascontiguousarray(I["L0_ffn_w_down"].T)
    r3c = _run("l3c", build_l3c,
               [{"hT": r3ab[c]["hT"], "x1T": r3ab[c]["x1T"], "wdT": wdT}
                for c in range(NCORE)])

    in4 = []
    w4 = {}
    for nm in ("r", "k", "v"):
        b1 = I[f"L1_attn_{nm}b1"]
        w4[nm + "W1"] = np.ascontiguousarray(I[f"L1_attn_{nm}W1"])
        w4[nm + "b1c"] = np.ascontiguousarray(b1[:9 * 128].reshape(9, 128).T)
        w4[nm + "b1t"] = np.ascontiguousarray(b1[9 * 128:].reshape(76, 1))
        w4[nm + "W2"] = np.ascontiguousarray(I[f"L1_attn_{nm}W2"])
        w4[nm + "b2c"] = _colsof(I[f"L1_attn_{nm}b2"])
        w4[nm + "muc"] = _colsof(I[f"L1_attn_mu_{nm}"])
    r4a = _run("l4a", build_l4a,
               [{"xlnT": r3c[c]["xln2T"], **w4} for c in range(NCORE)])

    hsel = np.stack([np.r_[np.ones(64), np.zeros(64)],
                     np.r_[np.zeros(64), np.ones(64)]], axis=1).astype(np.float32)
    sel2 = np.ascontiguousarray(hsel.T)
    w4b = {"remc_d": _colsof(I["L1_attn_rem_mult"]),
           "mixc_d": _colsof(I["L1_attn_iclr_mix"]),
           "sel2_d": sel2, "hsel_d": hsel}
    for nm in ("a", "d", "g"):
        w4b[nm + "A"] = np.ascontiguousarray(I[f"L1_attn_{nm}A"])
        w4b[nm + "B"] = np.ascontiguousarray(I[f"L1_attn_{nm}B"])
        bkey = {"a": "L1_attn_ab", "d": "L1_attn_db", "g": None}[nm]
        bvec = I[bkey] if bkey else np.zeros(C, np.float32)
        w4b[nm + "bc"] = _colsof(bvec)
        w4b[nm + "muc"] = _colsof(I[f"L1_attn_mu_{nm}"])
    r4b = _run("l4b", build_l4b,
               [{"xlnT": r3c[c]["xln2T"], "kT": r4a[c]["k_f"], **w4b}
                for c in range(NCORE)])

    def _gather_feat(key, res, dt=None):
        out = []
        for b in range(2):
            full = np.concatenate([res[4 * b + q][key] for q in range(4)], axis=1)
            out.append(full)
        return out

    kapf = _gather_feat("kap_f", r4b)
    ktlf = _gather_feat("ktil_f", r4b)
    rhof = _gather_feat("rho_f", r4b)
    vf = _gather_feat("v_f", r4a)
    rf = _gather_feat("r_f", r4a)
    df = _gather_feat("d_f", r4b)
    mAB = np.concatenate([np.tril(np.ones((128, 128), np.float32), -1)] * 2, axis=1)
    mSU = np.triu(np.ones((128, 128), np.float32), 1)
    mIU = np.triu(np.ones((128, 128), np.float32), 0)
    in5 = []
    for c in range(NCORE):
        b, hg = c // 4, c % 4
        rs = slice(hg * 256, (hg + 1) * 256)
        in5.append({"kapP": np.ascontiguousarray(kapf[b][rs]),
                    "ktlP": np.ascontiguousarray(ktlf[b][rs]),
                    "rhoP": np.ascontiguousarray(rhof[b][rs]),
                    "vP": np.ascontiguousarray(vf[b][rs]),
                    "rP": np.ascontiguousarray(rf[b][rs]),
                    "dP": np.ascontiguousarray(df[b][rs]),
                    "mABd": mAB, "mSUd": mSU, "mIUd": mIU,
                    "identbf": ident_bf})
    r5 = _run("l5", build_l5, in5)
    yfull = [np.concatenate([r5[4 * b + hg]["yP"] for hg in range(4)], axis=0)
             for b in range(2)]

    in6 = []
    for c in range(NCORE):
        b, qq = c // 4, c % 4
        cs = slice(qq * TPC, (qq + 1) * TPC)
        in6.append({"yT": np.ascontiguousarray(yfull[b][:, cs]),
                    "rT": r4a[c]["r_f"], "rhoT": r4b[c]["rho_f"],
                    "vT": r4a[c]["v_f"], "gT": r4b[c]["gate_f"],
                    "lngc": _colsof(I["L1_attn_ln_g"]),
                    "lnbc": _colsof(I["L1_attn_ln_b"]),
                    "bmc": _colsof(I["L1_attn_bonus_mult"]),
                    "hsel_d": hsel, "sel2_d": sel2})
    r6a = _run("l6a", build_l6a, in6)

    ob1 = I["L1_attn_ob1"]
    w6 = {"oW1": np.ascontiguousarray(I["L1_attn_oW1"]),
          "ob1c": np.ascontiguousarray(ob1[:9 * 128].reshape(9, 128).T),
          "ob1t": np.ascontiguousarray(ob1[9 * 128:].reshape(76, 1)),
          "oW2": np.ascontiguousarray(I["L1_attn_oW2"]),
          "ob2c": _colsof(I["L1_attn_ob2"])}
    r6b = _run("l6b", build_l6b,
               [{"zT": r6a[c]["zT"], "x2T": r3c[c]["x2T"], **w6} for c in range(NCORE)])

    w7 = {"gwD": np.ascontiguousarray(I["L1_ffn_gate_w"]),
          "w1D": np.ascontiguousarray(I["L1_ffn_w1"]),
          "w3D": np.ascontiguousarray(I["L1_ffn_w3"]),
          "w2D": np.ascontiguousarray(I["L1_ffn_w2"]),
          "identf": ident_f}
    r7 = _run("l7", build_l7,
              [{"xlnT": r6b[c]["xln3T"], "x3T": r6b[c]["x3T"], **w7}
               for c in range(NCORE)])

    out = np.empty((2, 8192, C), np.float32)
    for c in range(NCORE):
        b, qq = c // 4, c % 4
        out[b, qq * TPC:(qq + 1) * TPC] = r7[c]["outT"].T
    return out


# version 22
# speedup vs baseline: 1.1722x; 1.0072x over previous
import numpy as np
import ml_dtypes
from contextlib import ExitStack

import concourse.bass as bass
import concourse.mybir as mybir
from concourse.bass_utils import run_bass_kernel_spmd

F32 = mybir.dt.float32
BF16 = mybir.dt.bfloat16
I32 = mybir.dt.int32
AF = mybir.ActivationFunctionType
ALU = mybir.AluOpType
AX = mybir.AxisListType.X
BF = ml_dtypes.bfloat16

NCORE = 8
TPC = 2048
C = 1024
NTILE = TPC // 128

ENGS = ("sync", "tensor", "vector", "scalar", "gpsimd")


ELIDE_SAME_ENG = False


class Prog:
    def __init__(self):
        self.nc = bass.Bass("TRN2", target_bir_lowering=False)
        self.ctx = ExitStack()
        self.ops = {e: [] for e in ENGS}
        self.es = {}
        self.cnt = {e: 0 for e in ENGS}
        self.waited = {e: {} for e in ENGS}
        self.lastw = {}
        self._ch = 0
        self.chlast = {}
        self.readers = {}

    def _esem(self, key):
        if key not in self.es:
            self.es[key] = self.ctx.enter_context(self.nc.semaphore("es_" + key.replace(".", "_")))
        return self.es[key]

    def sb(self, name, shape, dt):
        return self.ctx.enter_context(self.nc.sbuf_tensor(name, list(shape), dt))

    def ps(self, name, shape, dt=F32):
        return self.ctx.enter_context(self.nc.psum_tensor(name, list(shape), dt))

    def din(self, name, shape, dt):
        return self.nc.dram_tensor(name, list(shape), dt, kind="ExternalInput").ap()

    def dout(self, name, shape, dt):
        return self.nc.dram_tensor(name, list(shape), dt, kind="ExternalOutput").ap()

    def begin_lanes(self, n):
        self._lanes = [[] for _ in range(n)]
        self._lane = 0

    def lane(self, k):
        self._lane = k

    def end_lanes(self, gran=1):
        lanes = self._lanes
        self._lanes = None
        idx = [0] * len(lanes)
        while True:
            done = True
            for k in range(len(lanes)):
                for _ in range(gran):
                    if idx[k] < len(lanes[k]):
                        done = False
                        lanes[k][idx[k]]()
                        idx[k] += 1
            if done:
                break

    def op(self, eng, fn, inc=1, deps=(), reads=(), writes=()):
        if getattr(self, "_lanes", None) is not None:
            self._lanes[self._lane].append(
                lambda eng=eng, fn=fn, inc=inc, deps=deps, reads=reads, writes=writes:
                self._op_now(eng, fn, inc, deps, reads, writes))
            return None
        return self._op_now(eng, fn, inc, deps, reads, writes)

    def _op_now(self, eng, fn, inc=1, deps=(), reads=(), writes=()):
        alldeps = list(deps)
        for r in reads:
            alldeps.append(self.lastw.get(r))
        for w in writes:
            alldeps.append(self.lastw.get(w))
            alldeps.extend(self.readers.get(w, ()))
        self.dep(eng, *alldeps)
        key = eng + (f".d{self._ch}" if inc == 16 else ".c")
        h = self._esem(key)
        self.cnt[key] = self.cnt.get(key, 0) + inc
        tok = (key, self.cnt[key])

        def g(e, fn=fn, h=h, k=inc):
            fn(e).then_inc(h, k)
        self.ops[eng].append(g)
        for r in reads:
            self.readers.setdefault(r, []).append(tok)
        for w in writes:
            self.lastw[w] = tok
            self.readers[w] = []
        return tok

    def dma(self, eng, out, in_, deps=(), reads=(), writes=(), ch=0):
        if getattr(self, "_lanes", None) is not None:
            self._lanes[self._lane].append(
                lambda eng=eng, out=out, in_=in_, deps=deps, reads=reads,
                writes=writes, ch=ch: self._dma_now(eng, out, in_, deps, reads, writes, ch))
            return None
        return self._dma_now(eng, out, in_, deps, reads, writes, ch)

    def _dma_now(self, eng, out, in_, deps=(), reads=(), writes=(), ch=0):
        self._ch = ch
        key = f"{eng}.d{ch}"
        prev = self.chlast.get(key)
        if prev is not None:
            self.dep(eng, prev)
        tok = self._op_now(eng, lambda e, o=out, i=in_: e.dma_start(out=o, in_=i),
                           inc=16, deps=deps, reads=reads, writes=writes)
        self.chlast[key] = tok
        return tok

    def dep(self, eng, *toks):
        best = {}
        for t in toks:
            if t is None:
                continue
            tk, n = t
            if tk == "tensor.c" and eng == "tensor":
                continue
            if ELIDE_SAME_ENG and tk == eng + ".c":
                continue
            if n > best.get(tk, 0):
                best[tk] = n
        for tk, n in best.items():
            if self.waited[eng].get(tk, 0) >= n:
                continue
            h = self._esem(tk)
            self.ops[eng].append(lambda e, h=h, v=n: e.wait_ge(h, v))
            self.waited[eng][tk] = n

    def finish(self, final_eng="sync"):
        for key, n in list(self.cnt.items()):
            self.dep(final_eng, (key, n))
        with self.nc.Block() as block:
            for eng in ENGS:
                if self.ops[eng]:
                    def runner(e, _ops=tuple(self.ops[eng])):
                        for f in _ops:
                            f(e)
                    getattr(block, eng)(runner)
        self.ctx.close()
        return self.nc

    def const_cols(self, values: dict):
        tile = self.sb("constc", [128, max(1, len(values))], F32)
        aps = {}
        for i, (k, v) in enumerate(values.items()):
            self.op("gpsimd",
                    lambda g, i=i, v=v, t=tile: g.memset(t[:, i:i + 1], float(v)),
                    writes=("constc",))
            aps[k] = tile[:, i:i + 1]
        return aps


def copy_any(e, dst, src):
    if hasattr(e, "tensor_copy"):
        return e.tensor_copy(dst, src)
    return e.copy(dst, src)


class MMRunner:

    def __init__(self, p: Prog, name, width=512, nbank=4, psum_dt=F32):
        self.p = p
        self.banks = [p.ps(f"{name}_pm{i}", [128, width], psum_dt) for i in range(nbank)]
        self.bkeys = [f"{name}_pm{i}" for i in range(nbank)]
        self.nbank = nbank
        self.g = 0

    def group(self, mms, evac, reads=(), evac_eng=None, evac_writes=(), evac_reads=()):
        p = self.p
        g = self.g
        b = g % self.nbank
        bank = self.banks[b]
        bkey = self.bkeys[b]
        eng = evac_eng or ("vector" if g % 2 == 0 else "scalar")
        n = len(mms)
        for i, mm in enumerate(mms):
            p.op("tensor",
                 (lambda t, mm=mm, bank=bank, s=(i == 0), e=(i == n - 1):
                  mm(t, bank[:], s, e)),
                 reads=reads, writes=(bkey,))
        ev = p.op(eng, lambda e, bank=bank, evac=evac: evac(e, bank[:]),
                  reads=(bkey,) + tuple(evac_reads), writes=tuple(evac_writes))
        self.g += 1
        return ev


def feat_ln(p, mr, name, xb, xbkey, sq, rows, Ab, Bb, onesb, ones1, cc, eps_key,
            T=TPC, nchunk=8):
    nt = T // 512
    sqkey = name + "sq"
    for ec in range(nchunk):
        p.op("scalar", lambda s, ec=ec: s.activation(sq[:, ec, :], xb[:, ec, :], AF.Square),
             reads=(xbkey,), writes=(sqkey,))
    for tc in range(nt):
        mms = [(lambda t, ps, s, e, ec=ec, tc=tc:
                t.matmul(ps[0:1, :], onesb[:], xb[:, ec, tc * 512:(tc + 1) * 512],
                         start=s, stop=e)) for ec in range(nchunk)]
        mr.group(mms, lambda e, ps, tc=tc: copy_any(e, rows[0:1, 0, tc * 512:(tc + 1) * 512], ps[0:1, :]),
                 reads=(xbkey, "onesb"), evac_writes=(name + "r0",))
        mms = [(lambda t, ps, s, e, ec=ec, tc=tc:
                t.matmul(ps[0:1, :], onesb[:], sq[:, ec, tc * 512:(tc + 1) * 512],
                         start=s, stop=e)) for ec in range(nchunk)]
        mr.group(mms, lambda e, ps, tc=tc: copy_any(e, rows[0:1, 1, tc * 512:(tc + 1) * 512], ps[0:1, :]),
                 reads=(name + "sq", "onesb"), evac_writes=(name + "r1",))
    nd = float(nchunk * 128)
    p.op("vector", lambda v: v.tensor_scalar_mul(rows[0:1, 0, :], rows[0:1, 0, :], 1.0 / nd),
         reads=(name + "r0",), writes=(name + "r0",))
    p.op("scalar", lambda s: s.activation(rows[0:1, 2, :], rows[0:1, 0, :], AF.Square),
         reads=(name + "r0",), writes=(name + "r2",))
    p.op("vector", lambda v: v.tensor_scalar_mul(rows[0:1, 1, :], rows[0:1, 1, :], 1.0 / nd),
         reads=(name + "r1",), writes=(name + "r1",))
    p.op("vector", lambda v: v.tensor_sub(rows[0:1, 1, :], rows[0:1, 1, :], rows[0:1, 2, :]),
         reads=(name + "r1", name + "r2"), writes=(name + "r1",))
    p.op("scalar", lambda s: s.activation(rows[0:1, 1, :], rows[0:1, 1, :], AF.Ln,
                                          bias=cc[eps_key][0:1, :]),
         reads=(name + "r1", "constc"), writes=(name + "r1",))
    p.op("scalar", lambda s: s.activation(rows[0:1, 1, :], rows[0:1, 1, :], AF.Exp,
                                          scale=-0.5),
         reads=(name + "r1",), writes=(name + "r1",))
    p.op("vector", lambda v: v.tensor_mul(rows[0:1, 2, :], rows[0:1, 0, :], rows[0:1, 1, :]),
         reads=(name + "r0", name + "r1"), writes=(name + "r2",))
    p.op("vector", lambda v: v.tensor_scalar_mul(rows[0:1, 2, :], rows[0:1, 2, :], -1.0),
         reads=(name + "r2",), writes=(name + "r2",))
    for tc in range(nt):
        mms = [(lambda t, ps, s, e, tc=tc:
                t.matmul(ps, ones1[:], rows[0:1, 1, tc * 512:(tc + 1) * 512],
                         start=True, stop=True))]
        mr.group(mms, lambda e, ps, tc=tc: copy_any(e, Ab[:, tc * 512:(tc + 1) * 512], ps),
                 reads=(name + "r1", "ones1"), evac_writes=(name + "A",))
        mms = [(lambda t, ps, s, e, tc=tc:
                t.matmul(ps, ones1[:], rows[0:1, 2, tc * 512:(tc + 1) * 512],
                         start=True, stop=True))]
        mr.group(mms, lambda e, ps, tc=tc: copy_any(e, Bb[:, tc * 512:(tc + 1) * 512], ps),
                 reads=(name + "r2", "ones1"), evac_writes=(name + "B",))


def feat_ln_apply(p, name, xb, xbkey, Ab, Bb, scr, dst, dstkey, T=TPC, nchunk=8,
                  gcol=None, bcol=None, dst2=None, dst2key=None):
    W = scr.shape[2]
    for ec in range(nchunk):
        eng = "vector" if ec % 2 == 0 else "gpsimd"
        for c0 in range(0, T, W):
            cs = slice(c0, c0 + W)
            p.op(eng, lambda v, ec=ec, cs=cs: v.tensor_tensor(
                scr[:, ec % 2, :], xb[:, ec, cs], Ab[:, cs], ALU.mult),
                reads=(xbkey, name + "A"), writes=(f"{name}scr{ec % 2}",))
            if gcol is None:
                p.op(eng, lambda v, ec=ec, cs=cs: v.tensor_tensor(
                    dst[:, ec, cs], scr[:, ec % 2, :], Bb[:, cs], ALU.add),
                    reads=(f"{name}scr{ec % 2}", name + "B"), writes=(dstkey,))
            else:
                p.op(eng, lambda v, ec=ec, cs=cs: v.tensor_tensor(
                    scr[:, ec % 2, :], scr[:, ec % 2, :], Bb[:, cs], ALU.add),
                    reads=(f"{name}scr{ec % 2}", name + "B"), writes=(f"{name}scr{ec % 2}",))
                p.op(eng, lambda v, ec=ec, cs=cs: v.tensor_scalar(
                    dst[:, ec, cs], scr[:, ec % 2, :], gcol(ec), bcol(ec), ALU.mult, ALU.add),
                    reads=(f"{name}scr{ec % 2}", "gcols"), writes=(dstkey,))
            if dst2 is not None:
                p.op(eng, lambda v, ec=ec, cs=cs: v.tensor_copy(dst2[:, ec, cs], dst[:, ec, cs]),
                     reads=(dstkey,), writes=(dst2key,))



def build_l1():
    p = Prog()
    x = p.din("x", [TPC, C], F32)
    wqT = p.din("wqT", [C, C], F32)
    wkT = p.din("wkT", [C, C], F32)
    wvT = p.din("wvT", [C, C], F32)
    identd = p.din("identbf", [128, 128], BF16)
    q_f = p.dout("q_f", [C, TPC], BF16)
    k_f = p.dout("k_f", [C, TPC], BF16)
    v_t = p.dout("v_t", [TPC, C], BF16)

    xs = p.sb("xs", [128, NTILE, C], F32)
    w3 = p.sb("w3", [128, 3, 8, C], BF16)
    ident = p.sb("ident", [128, 128], BF16)
    xln = p.sb("xln", [128, 2, C], BF16)
    xT = p.sb("xT", [128, 8, TPC], BF16)
    ost = p.sb("ost", [128, 4, 512], BF16)
    stats = p.sb("stats", [128, 8], F32)
    xc = p.sb("xc", [128, 2, C], F32)
    cc = p.const_cols({"eps": 1e-5})

    ptr_full = [p.ps(f"ptr{i}", [128, 1024], BF16) for i in range(2)]
    ptr = [t[:, 0:128] for t in ptr_full]

    p.dma("sync", xs[:], x.rearrange("(t p) d -> p t d", p=128), writes=("xs",))
    p.dma("sync", ident[:], identd, writes=("ident",))
    for i, w in enumerate((wqT, wkT, wvT)):
        p.dma("gpsimd", w3[:, i], w.rearrange("(c p) e -> p c e", p=128), writes=("w3",))

    for i in range(NTILE):
        b = i % 2
        p.op("vector", lambda v, i=i: v.reduce_sum(stats[:, 7:8], xs[:, i, :], axis=AX),
             reads=("xs",), writes=("st_m",))
        p.op("vector", lambda v: v.tensor_scalar_mul(stats[:, 7:8], stats[:, 7:8], 1.0 / C),
             reads=("st_m",), writes=("st_m",))
        p.op("vector", lambda v, i=i, b=b: v.tensor_scalar(
            xc[:, b, :], xs[:, i, :], stats[:, 7:8], None, ALU.subtract),
            reads=("xs", "st_m"), writes=(f"xc{b}",))
        p.op("scalar", lambda s, b=b: s.activation(xc[:, b, :], xc[:, b, :], AF.Square,
                                                   accum_out=stats[:, 6:7]),
             reads=(f"xc{b}",), writes=(f"xc{b}", "st_v"))
        p.op("scalar", lambda s: s.activation(stats[:, 6:7], stats[:, 6:7], AF.Ln,
                                              scale=1.0 / C, bias=cc["eps"]),
             reads=("st_v", "constc"), writes=("st_v",))
        p.op("scalar", lambda s: s.activation(stats[:, 6:7], stats[:, 6:7], AF.Exp,
                                              scale=-0.5),
             reads=("st_v",), writes=("st_v",))
        p.op("vector", lambda v, i=i, b=b: v.tensor_scalar(
            xln[:, b, :], xs[:, i, :], stats[:, 7:8], stats[:, 6:7],
            ALU.subtract, ALU.mult),
            reads=("xs", "st_m", "st_v"), writes=(f"xln{b}",))
        for c8 in range(8):
            k = i * 8 + c8
            p.op("tensor", lambda t, b=b, c8=c8, k=k: t.transpose(
                ptr[k % 2], xln[:, b, c8 * 128:(c8 + 1) * 128], ident[:]),
                reads=(f"xln{b}", "ident"), writes=(f"ptr{k % 2}",))
            p.op("vector", lambda v, i=i, c8=c8, k=k: v.tensor_copy(
                xT[:, c8, i * 128:(i + 1) * 128], ptr[k % 2]),
                reads=(f"ptr{k % 2}",), writes=("xT",))

    mr = MMRunner(p, "l1")

    def emit_group(mms, dram_ap):
        slot = mr.g % 4
        dst = ost[:, slot, :]
        ev = mr.group(mms, lambda e, ps, dst=dst: copy_any(e, dst, ps),
                      reads=("xT", "w3"), evac_writes=(f"ost{slot}",))
        p.dma("sync", dram_ap, dst, reads=(f"ost{slot}",), ch=1)

    for tc in range(4):
        for w_idx, outap in ((0, q_f), (1, k_f)):
            for ec in range(8):
                mms = [
                    (lambda t, ps, s, e, dc=dc, w_idx=w_idx, ec=ec, tc=tc:
                     t.matmul(ps, w3[:, w_idx, dc, ec * 128:(ec + 1) * 128],
                              xT[:, dc, tc * 512:(tc + 1) * 512], start=s, stop=e))
                    for dc in range(8)
                ]
                emit_group(mms, outap[ec * 128:(ec + 1) * 128, tc * 512:(tc + 1) * 512])
    for ti in range(NTILE):
        for eh in range(2):
            mms = [
                (lambda t, ps, s, e, dc=dc, ti=ti, eh=eh:
                 t.matmul(ps, xT[:, dc, ti * 128:(ti + 1) * 128],
                          w3[:, 2, dc, eh * 512:(eh + 1) * 512], start=s, stop=e))
                for dc in range(8)
            ]
            emit_group(mms, v_t[ti * 128:(ti + 1) * 128, eh * 512:(eh + 1) * 512])

    return p.finish()


S = 8192
NQT = S // 128


def build_l2(Sv=S):
    p = Prog()
    S_ = Sv
    qh = p.din("qh", [256, S_], BF16)
    kh = p.din("kh", [256, S_], BF16)
    vh = p.din("vh", [S_, 256], BF16)
    masks4d = p.din("masks4", [128, 4 * 512], BF16)
    identfd = p.din("identf", [128, 128], F32)
    attn = p.dout("attn", [S_, 256], BF16)

    M0 = 20.0
    NBLK = S_ // 512
    NSL = 16
    qs2 = p.sb("qs2", [128, 4, S_], BF16)
    ks2 = p.sb("ks2", [128, 4, S_], BF16)
    vse = p.sb("vse", [128, S_ // 128, 260], BF16)
    masks4 = p.sb("masks4s", [128, 4, 512], BF16)
    identf = p.sb("identfs", [128, 128], F32)
    probsT = p.sb("probsT", [128, 2, NSL, 512], BF16)
    oev = p.sb("oev", [65, 2, 128], F32)
    rlc = p.sb("rlc", [128, 2, 1], F32)
    ostage = p.sb("ostage", [128, 2, 64], BF16)
    cc = p.const_cols({"negm": -M0})

    ps_s = [[p.ps(f"l2s{ln}_{j}", [128, 512], F32) for j in range(3)]
            for ln in range(2)]
    ps_a = [p.ps(f"l2a{ln}", [128, 512], F32) for ln in range(2)]

    qhR = qh.rearrange("(c p) t -> p c t", p=64)
    khR = kh.rearrange("(c p) t -> p c t", p=64)
    p.op("vector", lambda v: v.memset(qs2[64:128, :, :], 0.0), writes=("qs2z",))
    for hi in range(4):
        p.dma("sync", qs2[0:64, hi, :], qhR[:, hi, :], writes=(f"qs2{hi}",))
        p.dma("sync", ks2[0:64, hi, :], khR[:, hi, :], writes=(f"ks2{hi}",), ch=3)
        p.dma("gpsimd", ks2[64:128, hi, :], khR[:, hi, :],
              writes=(f"ks2{hi}",), ch=1)
    for h4 in range(4):
        p.dma("sync", vse[:, :, h4 * 65:h4 * 65 + 64],
              vh.rearrange("(t p) e -> p t e", p=128)[:, :, h4 * 64:(h4 + 1) * 64],
              writes=(f"vse{h4}",), ch=1)
    p.op("vector", lambda v: v.memset(
        vse[:].rearrange("p t (h c) -> p t h c", c=65)[:, :, :, 64:65], 1.0),
        writes=("vse1",))
    p.dma("sync", masks4[:], masks4d.rearrange("p (t q) -> p t q", q=512),
          writes=("masks4",), ch=2)
    p.dma("sync", identf[:], identfd, writes=("identf",), ch=2)

    SCALE = 64.0 ** -0.5

    def finalize_tile(h, i, ln, PA, tq):
        p.op("vector", lambda v, ln=ln, tq=tq: v.tensor_copy(
            oev[0:65, ln, :], ps_a[ln][0:65, tq * 128:(tq + 1) * 128]),
            reads=(PA,), writes=(f"oev{ln}",))
        PS0 = f"l2s{ln}_0"
        p.op("tensor", lambda t, ln=ln: t.transpose(
            ps_s[ln][0][0:128, 0:65], oev[0:65, ln, :], identf[0:65, 0:65]),
            reads=(f"oev{ln}", "identf"), writes=(PS0,))
        p.op("vector", lambda v, ln=ln: v.reciprocal(
            rlc[:, ln, :], ps_s[ln][0][0:128, 64:65]),
            reads=(PS0,), writes=(f"rlc{ln}",))
        p.op("vector", lambda v, ln=ln: v.tensor_scalar(
            ostage[:, ln, :], ps_s[ln][0][0:128, 0:64], rlc[:, ln, :],
            None, ALU.mult),
            reads=(PS0, f"rlc{ln}"), writes=(f"ostage{ln}",))
        p.dma("sync", attn[i * 128:(i + 1) * 128, h * 64:(h + 1) * 64],
              ostage[:, ln, :], reads=(f"ostage{ln}",), ch=ln)

    def emit_qk_batch(h, ln, B, kb, n):
        q0 = B * 512
        for j in range(n):
            ksb = kb + j
            sb_i = ksb % 3
            PS = f"l2s{ln}_{sb_i}"
            sl = ksb % NSL
            PT = f"pT{ln}_{sl}"
            p.op("tensor", lambda t, h=h, ksb=ksb, q0=q0, ln=ln, sb_i=sb_i:
                 t.matmul(ps_s[ln][sb_i][:, 0:512],
                          ks2[0:128, h, ksb * 128:ksb * 128 + 128],
                          qs2[0:128, h, q0:q0 + 512],
                          start=True, stop=True),
                 reads=(f"ks2{h}", f"qs2{h}", "qs2z"), writes=(PS,))
            p.op("scalar", lambda s, ln=ln, sb_i=sb_i, sl=sl: s.activation(
                probsT[:, ln, sl, :], ps_s[ln][sb_i][:, 0:512], AF.Exp,
                scale=SCALE, bias=cc["negm"]),
                reads=(PS, "constc"), writes=(PT,))
            t4 = ksb - 4 * B
            if t4 >= 0:
                p.op("vector", lambda g, ln=ln, sl=sl, t4=t4: g.tensor_tensor(
                    probsT[:, ln, sl, :], probsT[:, ln, sl, :],
                    masks4[:, t4, :], ALU.mult),
                    reads=(PT, "masks4"), writes=(PT,))

    def emit_pv_batch(h, ln, B, kb, n, nk):
        for j in range(n):
            ksb = kb + j
            sl = ksb % NSL
            PT = f"pT{ln}_{sl}"
            PA = f"l2pa{ln}"
            p.op("tensor", lambda t, ksb=ksb, h=h, ln=ln, sl=sl,
                 first=(ksb == 0), last=(ksb == nk - 1):
                 t.matmul(ps_a[ln][0:65, 0:512],
                          vse[:, ksb, h * 65:(h + 1) * 65],
                          probsT[:, ln, sl, :],
                          start=first, stop=last),
                 reads=(f"vse{h}", "vse1", PT), writes=(PA,))

    for hp in range(2):
        p.begin_lanes(2)
        pend = [None, None]
        for B in range(NBLK):
            for ln in range(2):
                h = 2 * hp + ln
                p.lane(ln)
                nk = 4 * B + 4
                batches = [(kb, min(8, nk - kb)) for kb in range(0, nk, 8)]
                PA = f"l2pa{ln}"
                emit_qk_batch(h, ln, B, *batches[0])
                if pend[ln] is not None:
                    ph, pB = pend[ln]
                    for tq in range(4):
                        finalize_tile(ph, 4 * pB + tq, ln, PA, tq)
                for bi in range(1, len(batches)):
                    emit_qk_batch(h, ln, B, *batches[bi])
                    emit_pv_batch(h, ln, B, *batches[bi - 1], nk)
                emit_pv_batch(h, ln, B, *batches[-1], nk)
                pend[ln] = (h, B)
        for ln in range(2):
            p.lane(ln)
            ph, pB = pend[ln]
            PA = f"l2pa{ln}"
            for tq in range(4):
                finalize_tile(ph, 4 * pB + tq, ln, PA, tq)
        p.end_lanes()
    return p.finish()


def build_l3a():
    p = Prog()
    atT = p.din("atT", [C, TPC], BF16)
    xT = p.din("xT", [C, TPC], F32)
    woT = p.din("woT", [C, C], F32)
    x1T = p.dout("x1T", [C, TPC], F32)
    xln1T = p.dout("xln1T", [C, TPC], BF16)

    wos = p.sb("wos", [128, 8, C], BF16)
    att = p.sb("att", [128, 2, 8, 512], BF16)
    xtt = p.sb("xtt", [128, 2, 8, 512], F32)
    x1b = p.sb("x1b", [128, 8, TPC], BF16)
    xln = p.sb("xlnb", [128, 8, TPC], BF16)
    sq = xln
    Ab = p.sb("Ab", [128, TPC], F32)
    Bb = p.sb("Bb", [128, TPC], F32)
    rows = p.sb("rows", [1, 3, TPC], F32)
    st1 = p.sb("st1", [128, 2, 512], F32)
    scr = p.sb("scr", [128, 2, TPC], F32)
    onesb = p.sb("onesb", [128, 1], BF16)
    ones1 = p.sb("ones1", [1, 128], F32)
    cc = p.const_cols({"eps5": 1e-5})
    p.op("gpsimd", lambda g: g.memset(onesb[:], 1.0), writes=("onesb",))
    p.op("gpsimd", lambda g: g.memset(ones1[:], 1.0), writes=("ones1",))

    p.dma("gpsimd", wos[:], woT.rearrange("(c p) e -> p c e", p=128), writes=("wos",))
    mr = MMRunner(p, "l3a")
    for tc in range(4):
        tb = tc % 2
        p.dma("sync", att[:, tb], atT.rearrange("(c p) t -> p c t", p=128)[:, :, tc * 512:(tc + 1) * 512],
              writes=(f"att{tb}",))
        p.dma("sync", xtt[:, tb], xT.rearrange("(c p) t -> p c t", p=128)[:, :, tc * 512:(tc + 1) * 512],
              writes=(f"xtt{tb}",), ch=1)
        for hc in range(8):
            mms = [(lambda t, ps, s, e, ec=ec, hc=hc, tb=tb:
                    t.matmul(ps, wos[:, ec, hc * 128:(hc + 1) * 128], att[:, tb, ec, :],
                             start=s, stop=e)) for ec in range(8)]
            sslot = mr.g % 2
            ev = mr.group(mms, lambda e, ps, hc=hc, tb=tb, sslot=sslot:
                          e.tensor_tensor(st1[:, sslot, :], ps, xtt[:, tb, hc, :], ALU.add),
                          reads=("wos", f"att{tb}"),
                          evac_reads=(f"xtt{tb}",),
                          evac_writes=(f"st1{sslot}",), evac_eng="vector")
            p.dma("sync", x1T[hc * 128:(hc + 1) * 128, tc * 512:(tc + 1) * 512],
                  st1[:, sslot, :], reads=(f"st1{sslot}",), ch=2)
            p.op("scalar", lambda s, hc=hc, tc=tc, sslot=sslot: s.copy(
                x1b[:, hc, tc * 512:(tc + 1) * 512], st1[:, sslot, :]),
                reads=(f"st1{sslot}",), writes=("x1b",))
    feat_ln(p, mr, "ln1", x1b, "x1b", sq, rows, Ab, Bb, onesb, ones1, cc, "eps5")
    feat_ln_apply(p, "ln1", x1b, "x1b", Ab, Bb, scr, xln, "ln1sq")
    for ec in range(8):
        p.dma("sync", xln1T[ec * 128:(ec + 1) * 128, :], xln[:, ec, :],
              reads=("ln1sq",), ch=2)
    return p.finish()


FH = 3072


def build_l3b():
    p = Prog()
    xlnT = p.din("xlnT", [C, TPC], BF16)
    wgT = p.din("wgT", [C, FH], I32)
    wuT = p.din("wuT", [C, FH], I32)
    hT = p.dout("hT", [FH, TPC], BF16)

    xln = p.sb("xln", [128, 8, TPC], BF16)
    wg = p.sb("wg", [128, 8, FH], BF16)
    wu = p.sb("wu", [128, 8, FH], BF16)
    sg = p.sb("sg", [128, 2, 512], BF16)
    hst = p.sb("hst", [128, 2, 512], BF16)

    p.dma("sync", xln[:], xlnT.rearrange("(c p) t -> p c t", p=128), writes=("xln",))
    p.dma("gpsimd", wg[:], wgT.rearrange("(c p) h -> p c h", p=128), writes=("wg",))
    p.dma("gpsimd", wu[:], wuT.rearrange("(c p) h -> p c h", p=128), writes=("wu",))

    mrs = [MMRunner(p, "l3bA", nbank=2), MMRunner(p, "l3bB", nbank=2)]
    SC = 1.0 / 127.0
    p.begin_lanes(2)
    for hc in range(24):
        slot = hc % 2
        p.lane(slot)
        mr = mrs[slot]
        for tc in range(4):
            gm = [(lambda t, ps, s, e, ec=ec, hc=hc, tc=tc:
                   t.matmul(ps, wg[:, ec, hc * 128:(hc + 1) * 128],
                            xln[:, ec, tc * 512:(tc + 1) * 512], start=s, stop=e))
                  for ec in range(8)]
            mr.group(gm, lambda e, ps, slot=slot: e.activation(
                sg[:, slot, :], ps, AF.Silu, scale=SC),
                reads=("wg", "xln"), evac_writes=(f"sg{slot}",), evac_eng="scalar")
            um = [(lambda t, ps, s, e, ec=ec, hc=hc, tc=tc:
                   t.matmul(ps, wu[:, ec, hc * 128:(hc + 1) * 128],
                            xln[:, ec, tc * 512:(tc + 1) * 512], start=s, stop=e))
                  for ec in range(8)]
            mr.group(um, lambda e, ps, slot=slot: e.scalar_tensor_tensor(
                hst[:, slot, :], ps, SC, sg[:, slot, :], ALU.mult, ALU.mult),
                reads=("wu", "xln"), evac_reads=(f"sg{slot}",),
                evac_writes=(f"hst{slot}",), evac_eng="vector")
            p.dma("sync", hT[hc * 128:(hc + 1) * 128, tc * 512:(tc + 1) * 512],
                  hst[:, slot, :], reads=(f"hst{slot}",), ch=1 + slot)
    p.end_lanes()
    return p.finish()


def build_l3ab():
    p = Prog()
    atT = p.din("atT", [C, TPC], BF16)
    xT = p.din("xT", [C, TPC], F32)
    woT = p.din("woT", [C, C], F32)
    wgT = p.din("wgT", [C, FH], I32)
    wuT = p.din("wuT", [C, FH], I32)
    x1T = p.dout("x1T", [C, TPC], F32)
    hT = p.dout("hT", [FH, TPC], BF16)

    wos = p.sb("wos", [128, 8, C], BF16)
    att = p.sb("att", [128, 2, 8, 512], BF16)
    xtt = p.sb("xtt", [128, 2, 8, 512], F32)
    x1b = p.sb("x1b", [128, 8, TPC], BF16)
    xln = p.sb("xlnb", [128, 8, TPC], BF16)
    sq = xln
    Ab = p.sb("Ab", [128, TPC], F32)
    Bb = p.sb("Bb", [128, TPC], F32)
    rows = p.sb("rows", [1, 3, TPC], F32)
    st1 = p.sb("st1", [128, 2, 512], F32)
    scr = xtt[:].rearrange("p a e t -> p (a e) t")[:, 0:8, :].rearrange(
        "p (a u) t -> p a (u t)", a=2)
    onesb = p.sb("onesb", [128, 1], BF16)
    ones1 = p.sb("ones1", [1, 128], F32)
    wg = p.sb("wg", [128, 2, 2, 8, 128], BF16)
    wu = p.sb("wu", [128, 2, 2, 8, 128], BF16)
    sg = p.sb("sg", [128, 2, 512], BF16)
    hst = p.sb("hst", [128, 2, 512], BF16)
    cc = p.const_cols({"eps5": 1e-5})
    p.op("gpsimd", lambda g: g.memset(onesb[:], 1.0), writes=("onesb",))
    p.op("gpsimd", lambda g: g.memset(ones1[:], 1.0), writes=("ones1",))

    wgR = wgT.rearrange("(c p) h -> p c h", p=128)
    wuR = wuT.rearrange("(c p) h -> p c h", p=128)

    def ld_w(hc):
        ln_, bf = hc % 2, (hc // 2) % 2
        p.dma("gpsimd", wg[:, ln_, bf], wgR[:, :, hc * 128:(hc + 1) * 128],
              writes=(f"wg{ln_}{bf}",), ch=1)
        p.dma("gpsimd", wu[:, ln_, bf], wuR[:, :, hc * 128:(hc + 1) * 128],
              writes=(f"wu{ln_}{bf}",), ch=2)

    p.dma("gpsimd", wos[:], woT.rearrange("(c p) e -> p c e", p=128), writes=("wos",))
    for hc0 in range(4):
        ld_w(hc0)

    mr = MMRunner(p, "l3a")
    for tc in range(4):
        tb = tc % 2
        p.dma("sync", att[:, tb], atT.rearrange("(c p) t -> p c t", p=128)[:, :, tc * 512:(tc + 1) * 512],
              writes=(f"att{tb}",))
        p.dma("sync", xtt[:, tb], xT.rearrange("(c p) t -> p c t", p=128)[:, :, tc * 512:(tc + 1) * 512],
              writes=(f"xtt{tb}",), ch=1)
        for hc in range(8):
            mms = [(lambda t, ps, s, e, ec=ec, hc=hc, tb=tb:
                    t.matmul(ps, wos[:, ec, hc * 128:(hc + 1) * 128], att[:, tb, ec, :],
                             start=s, stop=e)) for ec in range(8)]
            sslot = mr.g % 2
            ev = mr.group(mms, lambda e, ps, hc=hc, tb=tb, sslot=sslot:
                          e.tensor_tensor(st1[:, sslot, :], ps, xtt[:, tb, hc, :], ALU.add),
                          reads=("wos", f"att{tb}"),
                          evac_reads=(f"xtt{tb}",),
                          evac_writes=(f"st1{sslot}",), evac_eng="vector")
            p.dma("sync", x1T[hc * 128:(hc + 1) * 128, tc * 512:(tc + 1) * 512],
                  st1[:, sslot, :], reads=(f"st1{sslot}",), ch=2)
            p.op("scalar", lambda s, hc=hc, tc=tc, sslot=sslot: s.copy(
                x1b[:, hc, tc * 512:(tc + 1) * 512], st1[:, sslot, :]),
                reads=(f"st1{sslot}",), writes=("x1b",))
    feat_ln(p, mr, "ln1", x1b, "x1b", sq, rows, Ab, Bb, onesb, ones1, cc, "eps5")
    feat_ln_apply(p, "ln1", x1b, "x1b", Ab, Bb, scr, xln, "ln1sq")

    mrs = [MMRunner(p, "l3bA", nbank=2), MMRunner(p, "l3bB", nbank=2)]
    SC = 1.0 / 127.0
    p.begin_lanes(2)
    for hc in range(24):
        slot = hc % 2
        p.lane(slot)
        mr2 = mrs[slot]
        bf = (hc // 2) % 2
        for tc in range(4):
            gm = [(lambda t, ps, s, e, ec=ec, tc=tc, slot=slot, bf=bf:
                   t.matmul(ps, wg[:, slot, bf, ec, :],
                            xln[:, ec, tc * 512:(tc + 1) * 512], start=s, stop=e))
                  for ec in range(8)]
            mr2.group(gm, lambda e, ps, slot=slot: e.activation(
                sg[:, slot, :], ps, AF.Silu, scale=SC),
                reads=(f"wg{slot}{bf}", "ln1sq"),
                evac_writes=(f"sg{slot}",), evac_eng="scalar")
            um = [(lambda t, ps, s, e, ec=ec, tc=tc, slot=slot, bf=bf:
                   t.matmul(ps, wu[:, slot, bf, ec, :],
                            xln[:, ec, tc * 512:(tc + 1) * 512], start=s, stop=e))
                  for ec in range(8)]
            mr2.group(um, lambda e, ps, slot=slot: e.scalar_tensor_tensor(
                hst[:, slot, :], ps, SC, sg[:, slot, :], ALU.mult, ALU.mult),
                reads=(f"wu{slot}{bf}", "ln1sq"), evac_reads=(f"sg{slot}",),
                evac_writes=(f"hst{slot}",), evac_eng="vector")
            p.dma("sync", hT[hc * 128:(hc + 1) * 128, tc * 512:(tc + 1) * 512],
                  hst[:, slot, :], reads=(f"hst{slot}",), ch=3 + slot)
        if hc + 4 < 24:
            ld_w(hc + 4)
    p.end_lanes()
    return p.finish()


def build_l3c():
    p = Prog()
    hTd = p.din("hT", [FH, TPC], BF16)
    x1T = p.din("x1T", [C, TPC], F32)
    wdT = p.din("wdT", [FH, C], I32)
    x2T = p.dout("x2T", [C, TPC], F32)
    xln2T = p.dout("xln2T", [C, TPC], BF16)

    wd = p.sb("wd", [128, 24, C], BF16)
    ht = p.sb("ht", [128, 1, 24, 512], BF16)
    x1t = p.sb("x1t", [128, 1, 8, 512], F32)
    x2b = p.sb("x2b", [128, 8, TPC], BF16)
    xln = p.sb("xlnb", [128, 8, TPC], BF16)
    sq = xln
    Ab = p.sb("Ab", [128, TPC], F32)
    Bb = p.sb("Bb", [128, TPC], F32)
    rows = p.sb("rows", [1, 3, TPC], F32)
    st1 = p.sb("st1", [128, 2, 512], F32)
    scr = p.sb("scr", [128, 2, TPC // 2], F32)
    onesb = p.sb("onesb", [128, 1], BF16)
    ones1 = p.sb("ones1", [1, 128], F32)
    cc = p.const_cols({"eps5": 1e-5})
    p.op("gpsimd", lambda g: g.memset(onesb[:], 1.0), writes=("onesb",))
    p.op("gpsimd", lambda g: g.memset(ones1[:], 1.0), writes=("ones1",))

    wdR = wdT.rearrange("(c p) e -> p c e", p=128)
    for ecw in range(8):
        p.dma("gpsimd", wd[:, :, ecw * 128:(ecw + 1) * 128],
              wdR[:, :, ecw * 128:(ecw + 1) * 128],
              writes=(f"wd{ecw}",), ch=ecw % 3)
    mr = MMRunner(p, "l3c")
    SC = 1.0 / 127.0
    for tc in range(4):
        tb = 0
        p.dma("sync", ht[:, tb], hTd.rearrange("(c p) t -> p c t", p=128)[:, :, tc * 512:(tc + 1) * 512],
              writes=(f"ht{tb}",))
        p.dma("sync", x1t[:, tb], x1T.rearrange("(c p) t -> p c t", p=128)[:, :, tc * 512:(tc + 1) * 512],
              writes=(f"x1t{tb}",), ch=1)
        for ec in range(8):
            mms = [(lambda t, ps, s, e, hc=hc, ec=ec, tb=tb:
                    t.matmul(ps, wd[:, hc, ec * 128:(ec + 1) * 128], ht[:, tb, hc, :],
                             start=s, stop=e)) for hc in range(24)]
            sslot = mr.g % 2
            mr.group(mms, lambda e, ps, ec=ec, tb=tb, sslot=sslot:
                     e.scalar_tensor_tensor(st1[:, sslot, :], ps, SC, x1t[:, tb, ec, :],
                                            ALU.mult, ALU.add),
                     reads=(f"wd{ec}", f"ht{tb}"), evac_reads=(f"x1t{tb}",),
                     evac_writes=(f"st1{sslot}",), evac_eng="vector")
            p.dma("sync", x2T[ec * 128:(ec + 1) * 128, tc * 512:(tc + 1) * 512],
                  st1[:, sslot, :], reads=(f"st1{sslot}",), ch=2)
            p.op("scalar", lambda s, ec=ec, tc=tc, sslot=sslot: s.copy(
                x2b[:, ec, tc * 512:(tc + 1) * 512], st1[:, sslot, :]),
                reads=(f"st1{sslot}",), writes=("x2b",))
    feat_ln(p, mr, "ln2", x2b, "x2b", sq, rows, Ab, Bb, onesb, ones1, cc, "eps5")
    feat_ln_apply(p, "ln2", x2b, "x2b", Ab, Bb, scr, xln, "ln2sq")
    for ec in range(8):
        p.dma("sync", xln2T[ec * 128:(ec + 1) * 128, :], xln[:, ec, :],
              reads=("ln2sq",), ch=2)
    return p.finish()


E1 = 1228
NHC = 10


def hc_w(hc):
    return 128 if hc < 9 else E1 - 9 * 128


def build_l4a():
    p = Prog()
    xlnT = p.din("xlnT", [C, TPC], BF16)
    ins = {}
    for nm in ("r", "k", "v"):
        ins[nm] = (p.din(nm + "W1", [C, E1], F32),
                   p.din(nm + "b1c", [128, 9], F32),
                   p.din(nm + "b1t", [76, 1], F32),
                   p.din(nm + "W2", [E1, C], F32),
                   p.din(nm + "b2c", [128, 8], F32),
                   p.din(nm + "muc", [128, 8], F32))
    outs = {nm: p.dout(nm + "_f", [C, TPC], BF16) for nm in ("r", "k", "v")}

    xln = p.sb("xln", [128, 8, TPC], BF16)
    w1 = p.sb("w1", [128, 2, 8, E1], BF16)
    w2 = p.sb("w2", [128, 2, NHC, C], BF16)
    b1 = p.sb("b1", [128, 2, NHC], F32)
    b2 = p.sb("b2", [128, 2, 8], F32)
    muc = p.sb("muc", [128, 2, 8], BF16)
    r1 = p.sb("r1", [128, NHC, TPC], BF16)
    ost = p.sb("ost", [128, 2, 512], BF16)

    p.dma("sync", xln[:], xlnT.rearrange("(c p) t -> p c t", p=128), writes=("xln",))
    mr = MMRunner(p, "l4a")

    def ld_mlp(mi):
        nm = ("r", "k", "v")[mi]
        sl = mi % 2
        W1d, b1d, b1td, W2d, b2d, mud = ins[nm]
        W1R = W1d.rearrange("(c p) h -> p c h", p=128)
        for hcw in range(NHC):
            ww = hc_w(hcw)
            p.dma("gpsimd", w1[:, sl, :, hcw * 128:hcw * 128 + ww],
                  W1R[:, :, hcw * 128:hcw * 128 + ww],
                  writes=(f"w1{sl}_{hcw}",), ch=0)
        p.dma("gpsimd", w2[0:76, sl, 9], W2d[9 * 128:E1, :], writes=(f"w2{sl}",), ch=1)
        p.dma("gpsimd", w2[:, sl, 0:9], W2d[0:9 * 128, :].rearrange("(c p) e -> p c e", p=128),
              writes=(f"w2{sl}",), ch=1)
        p.dma("sync", b1[:, sl, 0:9], b1d, writes=(f"b1{sl}",), ch=1)
        p.dma("sync", b1[0:76, sl, 9:10], b1td, writes=(f"b1{sl}",), ch=1)
        p.dma("sync", b2[:, sl], b2d, writes=(f"b2{sl}",), ch=1)
        p.dma("gpsimd", muc[:, sl], mud, writes=(f"muc{sl}",), ch=2)

    ld_mlp(0)
    for mi, nm in enumerate(("r", "k", "v")):
        sl = mi % 2
        if mi + 1 < 3:
            ld_mlp(mi + 1)
        for hc in range(NHC):
            w = hc_w(hc)
            mms = [(lambda t, ps, s, e, dc=dc, hc=hc, w=w, sl=sl:
                    t.matmul(ps[0:w, 0:1], w1[:, sl, dc, hc * 128:hc * 128 + w],
                             muc[:, sl, dc:dc + 1], start=s, stop=e)) for dc in range(8)]
            mr.group(mms, lambda e, ps, hc=hc, w=w, sl=sl: e.tensor_tensor(
                b1[0:w, sl, hc:hc + 1], b1[0:w, sl, hc:hc + 1], ps[0:w, 0:1], ALU.add),
                reads=(f"w1{sl}_{hc}", f"muc{sl}"), evac_writes=(f"b1{sl}",),
                evac_eng="vector")
        for hc in range(NHC):
            w = hc_w(hc)
            for tc in range(4):
                mms = [(lambda t, ps, s, e, dc=dc, hc=hc, tc=tc, w=w, sl=sl:
                        t.matmul(ps[0:w, :], w1[:, sl, dc, hc * 128:hc * 128 + w],
                                 xln[:, dc, tc * 512:(tc + 1) * 512], start=s, stop=e))
                       for dc in range(8)]
                mr.group(mms, lambda e, ps, hc=hc, tc=tc, w=w, sl=sl: e.activation(
                    r1[0:w, hc, tc * 512:(tc + 1) * 512], ps[0:w, :], AF.Gelu,
                    bias=b1[0:w, sl, hc:hc + 1]),
                    reads=(f"w1{sl}_{hc}", "xln"), evac_reads=(f"b1{sl}",),
                    evac_writes=("r1",), evac_eng="scalar")
        for ec in range(8):
            for tc in range(4):
                mms = [(lambda t, ps, s, e, hc=hc, ec=ec, tc=tc, w=hc_w(hc), sl=sl:
                        t.matmul(ps, w2[0:w, sl, hc, ec * 128:(ec + 1) * 128],
                                 r1[0:w, hc, tc * 512:(tc + 1) * 512], start=s, stop=e))
                       for hc in range(NHC)]
                slot = mr.g % 2
                mr.group(mms, lambda e, ps, ec=ec, slot=slot, sl=sl: e.tensor_scalar(
                    ost[:, slot, :], ps, b2[:, sl, ec:ec + 1], None, ALU.add),
                    reads=(f"w2{sl}", "r1"), evac_reads=(f"b2{sl}",),
                    evac_writes=(f"ost{slot}",), evac_eng="vector")
                p.dma("sync", outs[nm][ec * 128:(ec + 1) * 128, tc * 512:(tc + 1) * 512],
                      ost[:, slot, :], reads=(f"ost{slot}",), ch=3)
    return p.finish()


def build_l4ab():
    p = Prog()
    xlnT = p.din("xlnT", [C, TPC], BF16)
    ins = {}
    for nm in ("r", "k", "v"):
        ins[nm] = (p.din(nm + "W1", [C, E1], F32),
                   p.din(nm + "b1c", [128, 9], F32),
                   p.din(nm + "b1t", [76, 1], F32),
                   p.din(nm + "W2", [E1, C], F32),
                   p.din(nm + "b2c", [128, 8], F32),
                   p.din(nm + "muc", [128, 8], F32))
    outs = {nm: p.dout(nm + "_f", [C, TPC], BF16) for nm in ("r", "k", "v")}

    xln = p.sb("xln", [128, 8, TPC], BF16)
    w1 = p.sb("w1", [128, 2, 8, E1], BF16)
    w2 = p.sb("w2", [128, 2, NHC, C], BF16)
    b1 = p.sb("b1", [128, 2, NHC], F32)
    b2 = p.sb("b2", [128, 2, 8], F32)
    muc = p.sb("muc", [128, 2, 8], BF16)
    r1 = p.sb("r1", [128, NHC, TPC], BF16)
    ost = p.sb("ost", [128, 2, 512], BF16)

    p.dma("sync", xln[:], xlnT.rearrange("(c p) t -> p c t", p=128), writes=("xln",))
    mr = MMRunner(p, "l4a")

    def ld_mlp(mi):
        nm = ("r", "k", "v")[mi]
        sl = mi % 2
        W1d, b1d, b1td, W2d, b2d, mud = ins[nm]
        W1R = W1d.rearrange("(c p) h -> p c h", p=128)
        for hcw in range(NHC):
            ww = hc_w(hcw)
            p.dma("gpsimd", w1[:, sl, :, hcw * 128:hcw * 128 + ww],
                  W1R[:, :, hcw * 128:hcw * 128 + ww],
                  writes=(f"w1{sl}_{hcw}",), ch=0)
        p.dma("gpsimd", w2[0:76, sl, 9], W2d[9 * 128:E1, :], writes=(f"w2{sl}",), ch=1)
        p.dma("gpsimd", w2[:, sl, 0:9], W2d[0:9 * 128, :].rearrange("(c p) e -> p c e", p=128),
              writes=(f"w2{sl}",), ch=1)
        p.dma("sync", b1[:, sl, 0:9], b1d, writes=(f"b1{sl}",), ch=1)
        p.dma("sync", b1[0:76, sl, 9:10], b1td, writes=(f"b1{sl}",), ch=1)
        p.dma("sync", b2[:, sl], b2d, writes=(f"b2{sl}",), ch=1)
        p.dma("gpsimd", muc[:, sl], mud, writes=(f"muc{sl}",), ch=2)

    ld_mlp(0)
    for mi, nm in enumerate(("r", "k", "v")):
        sl = mi % 2
        if mi + 1 < 3:
            ld_mlp(mi + 1)
        for hc in range(NHC):
            w = hc_w(hc)
            mms = [(lambda t, ps, s, e, dc=dc, hc=hc, w=w, sl=sl:
                    t.matmul(ps[0:w, 0:1], w1[:, sl, dc, hc * 128:hc * 128 + w],
                             muc[:, sl, dc:dc + 1], start=s, stop=e)) for dc in range(8)]
            mr.group(mms, lambda e, ps, hc=hc, w=w, sl=sl: e.tensor_tensor(
                b1[0:w, sl, hc:hc + 1], b1[0:w, sl, hc:hc + 1], ps[0:w, 0:1], ALU.add),
                reads=(f"w1{sl}_{hc}", f"muc{sl}"), evac_writes=(f"b1{sl}",),
                evac_eng="vector")
        for hc in range(NHC):
            w = hc_w(hc)
            for tc in range(4):
                mms = [(lambda t, ps, s, e, dc=dc, hc=hc, tc=tc, w=w, sl=sl:
                        t.matmul(ps[0:w, :], w1[:, sl, dc, hc * 128:hc * 128 + w],
                                 xln[:, dc, tc * 512:(tc + 1) * 512], start=s, stop=e))
                       for dc in range(8)]
                mr.group(mms, lambda e, ps, hc=hc, tc=tc, w=w, sl=sl: e.activation(
                    r1[0:w, hc, tc * 512:(tc + 1) * 512], ps[0:w, :], AF.Gelu,
                    bias=b1[0:w, sl, hc:hc + 1]),
                    reads=(f"w1{sl}_{hc}", "xln"), evac_reads=(f"b1{sl}",),
                    evac_writes=("r1",), evac_eng="scalar")
        for ec in range(8):
            for tc in range(4):
                mms = [(lambda t, ps, s, e, hc=hc, ec=ec, tc=tc, w=hc_w(hc), sl=sl:
                        t.matmul(ps, w2[0:w, sl, hc, ec * 128:(ec + 1) * 128],
                                 r1[0:w, hc, tc * 512:(tc + 1) * 512], start=s, stop=e))
                       for hc in range(NHC)]
                slot = mr.g % 2
                mr.group(mms, lambda e, ps, ec=ec, slot=slot, sl=sl: e.tensor_scalar(
                    ost[:, slot, :], ps, b2[:, sl, ec:ec + 1], None, ALU.add),
                    reads=(f"w2{sl}", "r1"), evac_reads=(f"b2{sl}",),
                    evac_writes=(f"ost{slot}",), evac_eng="vector")
                p.dma("sync", outs[nm][ec * 128:(ec + 1) * 128, tc * 512:(tc + 1) * 512],
                      ost[:, slot, :], reads=(f"ost{slot}",), ch=3)
                if nm == "k":
                    p.op("gpsimd", lambda g, ec=ec, tc=tc, slot=slot:
                         g.tensor_copy(kf[:, ec, tc * 512:(tc + 1) * 512],
                                       ost[:, slot, :]),
                         reads=(f"ost{slot}",), writes=("kf",))

    lr = {}
    for nm in ("a", "d", "g"):
        lr[nm] = (p.din(nm + "A", [C, 128], F32), p.din(nm + "B", [128, C], F32),
                  p.din(nm + "bc", [128, 8], F32), p.din(nm + "muc", [128, 8], F32))
    remc_d = p.din("remc_d", [128, 8], F32)
    mixc_d = p.din("mixc_d", [128, 8], F32)
    sel2d = p.din("sel2_d", [2, 128], F32)
    hsel_d = p.din("hsel_d", [128, 2], F32)
    outs = {nm: p.dout(nm, [C, TPC], BF16) for nm in ("kap_f", "ktil_f", "rho_f", "gate_f")}
    d_f = p.dout("d_f", [C, TPC], F32)

    wA = p.sb("wA", [128, 8, 128], BF16)
    wB = p.sb("wB", [128, C], BF16)
    bias = p.sb("bias", [128, 8], F32)
    kf = p.sb("kf", [128, 8, TPC], BF16)
    mucB = p.sb("mucB", [128, 8], BF16)
    m1 = p.sb("m1", [128, 1], BF16)
    df = p.sb("df", [128, 2, TPC], F32)
    af = r1[:].rearrange("p h t -> p (h t)")[:, 0:8 * TPC].rearrange(
        "p (e t) -> p e t", e=8)
    w2f = w2[:].rearrange("p s h e -> p (s h e)")
    u1 = w2f[:, 10 * C:10 * C + TPC]
    ost2 = w2f[:, 10 * C + TPC:10 * C + TPC + 3072].rearrange(
        "p (a t) -> p a t", a=6)
    ns = df[0:2, :, :]
    remc = p.sb("remc", [128, 8], F32)
    mixc = p.sb("mixc", [128, 8], F32)
    omix = p.sb("omix", [128, 8], F32)
    sel2 = p.sb("sel2", [2, 128], F32)
    hsel = p.sb("hsel", [128, 2], F32)
    w1f = w1[:].rearrange("p s h e -> p (s h e)").bitcast(F32)
    scr = w1f[:, 0:2048].rearrange("p (a t) -> p a t", a=4)
    scr4 = w1f[:, 2048:4096].rearrange("p (a t) -> p a t", a=4)
    cc = p.const_cols({"tiny": 1e-24})

    p.dma("sync", sel2[:], sel2d, writes=("sel2",))
    p.dma("sync", hsel[:], hsel_d, writes=("hsel",))
    p.dma("sync", remc[:], remc_d, writes=("remc",), ch=1)
    p.dma("sync", mixc[:], mixc_d, writes=("mixc",), ch=1)
    p.op("vector", lambda v: v.tensor_scalar(omix[:], mixc[:], -1.0, 1.0, ALU.mult, ALU.add),
         reads=("mixc",), writes=("omix",))

    psK = [p.ps(f"l4bk{i}", [128, 512], F32) for i in range(2)]

    for nm in ("a", "d", "g"):
        Ad, Bd, bd, mud = lr[nm]
        p.dma("gpsimd", wA[:], Ad.rearrange("(c p) r -> p c r", p=128), writes=("wA",))
        p.dma("gpsimd", wB[:], Bd, writes=("wB",))
        p.dma("sync", bias[:], bd, writes=("bias",), ch=1)
        p.dma("gpsimd", mucB[:], mud, writes=("muc",))
        mms = [(lambda t, ps, s, e, dc=dc:
                t.matmul(ps[:, 0:1], wA[:, dc, :], mucB[:, dc:dc + 1], start=s, stop=e))
               for dc in range(8)]
        mr.group(mms, lambda e, ps: copy_any(e, m1[:], ps[:, 0:1]),
                 reads=("wA", "muc"), evac_writes=("m1",), evac_eng="vector")
        for ec in range(8):
            mms = [(lambda t, ps, s, e, ec=ec:
                    t.matmul(ps[:, 0:1], wB[:, ec * 128:(ec + 1) * 128], m1[:],
                             start=s, stop=e))]
            mr.group(mms, lambda e, ps, ec=ec: e.tensor_tensor(
                bias[:, ec:ec + 1], bias[:, ec:ec + 1], ps[:, 0:1], ALU.add),
                reads=("wB", "m1"), evac_writes=("bias",), evac_eng="vector")
        for tc in range(4):
            mms = [(lambda t, ps, s, e, dc=dc, tc=tc:
                    t.matmul(ps, wA[:, dc, :], xln[:, dc, tc * 512:(tc + 1) * 512],
                             start=s, stop=e)) for dc in range(8)]
            mr.group(mms, lambda e, ps, tc=tc: copy_any(e, u1[:, tc * 512:(tc + 1) * 512], ps),
                     reads=("wA", "xln"), evac_writes=("u1",), evac_eng="vector")
        for ec in range(8):
            for tc in range(4):
                mms = [(lambda t, ps, s, e, ec=ec, tc=tc:
                        t.matmul(ps, wB[:, ec * 128:(ec + 1) * 128],
                                 u1[:, tc * 512:(tc + 1) * 512], start=s, stop=e))]
                if nm == "a":
                    ev = (lambda e, ps, ec=ec, tc=tc: e.activation(
                        af[:, ec, tc * 512:(tc + 1) * 512], ps, AF.Sigmoid,
                        bias=bias[:, ec:ec + 1]))
                    wkey = "af"
                elif nm == "g":
                    ev = (lambda e, ps, ec=ec, tc=tc: e.activation(
                        ost[:, 0, :], ps, AF.Sigmoid, bias=bias[:, ec:ec + 1]))
                    wkey = "ost0"
                else:
                    ev = (lambda e, ps, ec=ec, tc=tc: e.activation(
                        scr[:, 2, :], ps, AF.Tanh, bias=bias[:, ec:ec + 1]))
                    wkey = "scr2"
                mr.group(mms, ev, reads=("wB", "u1"), evac_reads=("bias",),
                         evac_writes=(wkey,), evac_eng="scalar")
                if nm == "g":
                    p.dma("sync", outs["gate_f"][ec * 128:(ec + 1) * 128, tc * 512:(tc + 1) * 512],
                          ost[:, 0, :], reads=("ost0",), ch=2)
                elif nm == "d":
                    p.op("scalar", lambda s, ec=ec, tc=tc: s.activation(
                        scr[:, 2, :], scr[:, 2, :], AF.Sigmoid),
                        reads=("scr2",), writes=("scr2",))
                    p.op("scalar", lambda s, ec=ec, tc=tc: s.activation(
                        df[:, ec % 2, tc * 512:(tc + 1) * 512], scr[:, 2, :], AF.Exp,
                        scale=-DECAY_C),
                        reads=("scr2",), writes=(f"df{ec % 2}",))
            if nm == "d":
                p.dma("sync", d_f[ec * 128:(ec + 1) * 128, :], df[:, ec % 2, :],
                      reads=(f"df{ec % 2}",), ch=2)

    p.op("gpsimd", lambda g: g.memset(m1[:], 0.0),
         writes=("df0", "df1", "m1", "nsfence"))
    p.begin_lanes(2)
    for ec in range(8):
        p.lane(ec % 2)
        LN_ = ec % 2
        for tc in range(4):
            p.op("vector", lambda v, ec=ec, tc=tc, LN_=LN_: v.tensor_scalar(
                scr[:, 2 * LN_, :], kf[:, ec, tc * 512:(tc + 1) * 512],
                remc[:, ec:ec + 1], None, ALU.mult),
                reads=("kf", "remc"), writes=(f"scr0_{LN_}",))
            p.op("scalar", lambda s, LN_=LN_: s.activation(scr[:, 2 * LN_ + 1, :], scr[:, 2 * LN_, :], AF.Square),
                 reads=(f"scr0_{LN_}",), writes=(f"scr1_{LN_}",))
            p.op("tensor", lambda t, LN_=LN_: t.matmul(
                psK[LN_][0:2, 0:512], hsel[:], scr[:, 2 * LN_ + 1, :],
                start=True, stop=True),
                reads=(f"scr1_{LN_}", "hsel"), writes=(f"l4bk{LN_}",))
            p.op("vector", lambda v, tc=tc, LN_=LN_: v.tensor_copy(
                ns[0:2, LN_, tc * 512:(tc + 1) * 512], psK[LN_][0:2, 0:512]),
                reads=(f"l4bk{LN_}", "nsfence"), writes=(f"ns{LN_}",))
        p.op("vector", lambda v, LN_=LN_: v.tensor_scalar_max(
            ns[:, LN_, :], ns[:, LN_, :], cc["tiny"][0:2, :]),
            reads=(f"ns{LN_}", "constc"), writes=(f"ns{LN_}",))
        p.op("scalar", lambda s, LN_=LN_: s.activation(ns[:, LN_, :], ns[:, LN_, :], AF.Ln),
             reads=(f"ns{LN_}",), writes=(f"ns{LN_}",))
        p.op("scalar", lambda s, LN_=LN_: s.activation(ns[:, LN_, :], ns[:, LN_, :],
                                                       AF.Exp, scale=-0.5),
             reads=(f"ns{LN_}",), writes=(f"ns{LN_}",))
        for tc in range(4):
            p.op("tensor", lambda t, tc=tc, LN_=LN_: t.matmul(
                psK[LN_][:, 0:512], sel2[:], ns[0:2, LN_, tc * 512:(tc + 1) * 512],
                start=True, stop=True),
                reads=(f"ns{LN_}", "sel2"), writes=(f"l4bk{LN_}",))
            p.op("vector", lambda v, LN_=LN_: v.tensor_copy(
                scr4[:, LN_, :], psK[LN_][:, 0:512]),
                reads=(f"l4bk{LN_}",), writes=(f"scr3_{LN_}",))
            p.op("vector", lambda v, ec=ec, tc=tc, LN_=LN_: v.tensor_scalar(
                scr[:, 2 * LN_, :], kf[:, ec, tc * 512:(tc + 1) * 512],
                remc[:, ec:ec + 1], None, ALU.mult),
                reads=("kf", "remc"), writes=(f"scr0_{LN_}",))
            p.op("vector", lambda v, LN_=LN_: v.tensor_tensor(scr[:, 2 * LN_ + 1, :], scr[:, 2 * LN_, :],
                                                     scr4[:, LN_, :], ALU.mult),
                 reads=(f"scr0_{LN_}", f"scr3_{LN_}"), writes=(f"scr1_{LN_}",))
            p.op("vector", lambda v, LN_=LN_: v.tensor_copy(ost2[:, LN_, :], scr[:, 2 * LN_ + 1, :]),
                 reads=(f"scr1_{LN_}",), writes=(f"ost2_{LN_}",))
            p.dma("sync", outs["kap_f"][ec * 128:(ec + 1) * 128, tc * 512:(tc + 1) * 512],
                  ost2[:, LN_, :], reads=(f"ost2_{LN_}",), ch=2 + LN_)
            p.op("vector", lambda v, ec=ec, tc=tc, LN_=LN_: v.tensor_tensor(
                ost2[:, 4 + LN_, :], scr[:, 2 * LN_ + 1, :], af[:, ec, tc * 512:(tc + 1) * 512], ALU.mult),
                reads=(f"scr1_{LN_}", "af"), writes=(f"ost2_{4 + LN_}",))
            p.dma("sync", outs["ktil_f"][ec * 128:(ec + 1) * 128, tc * 512:(tc + 1) * 512],
                  ost2[:, 4 + LN_, :], reads=(f"ost2_{4 + LN_}",), ch=2 + LN_)
            p.op("gpsimd", lambda v, ec=ec, tc=tc, LN_=LN_: v.tensor_scalar(
                scr4[:, 2 + LN_, :], af[:, ec, tc * 512:(tc + 1) * 512],
                mixc[:, ec:ec + 1], omix[:, ec:ec + 1], ALU.mult, ALU.add),
                reads=("af", "mixc", "omix"), writes=(f"scr2_{LN_}",))
            p.op("gpsimd", lambda v, ec=ec, tc=tc, LN_=LN_: v.tensor_tensor(
                ost2[:, 2 + LN_, :], scr4[:, 2 + LN_, :],
                kf[:, ec, tc * 512:(tc + 1) * 512], ALU.mult),
                reads=(f"scr2_{LN_}", "kf"), writes=(f"ost2_{2 + LN_}",))
            p.dma("sync", outs["rho_f"][ec * 128:(ec + 1) * 128, tc * 512:(tc + 1) * 512],
                  ost2[:, 2 + LN_, :], reads=(f"ost2_{2 + LN_}",), ch=2 + LN_)
    p.end_lanes()
    return p.finish()


DECAY_C = 0.7408182206817179


def build_l4b():
    p = Prog()
    xlnT = p.din("xlnT", [C, TPC], BF16)
    kT = p.din("kT", [C, TPC], BF16)
    lr = {}
    for nm in ("a", "d", "g"):
        lr[nm] = (p.din(nm + "A", [C, 128], F32), p.din(nm + "B", [128, C], F32),
                  p.din(nm + "bc", [128, 8], F32), p.din(nm + "muc", [128, 8], F32))
    remc_d = p.din("remc_d", [128, 8], F32)
    mixc_d = p.din("mixc_d", [128, 8], F32)
    sel2d = p.din("sel2_d", [2, 128], F32)
    hsel_d = p.din("hsel_d", [128, 2], F32)
    outs = {nm: p.dout(nm, [C, TPC], BF16) for nm in ("kap_f", "ktil_f", "rho_f", "gate_f")}
    d_f = p.dout("d_f", [C, TPC], F32)

    xln = p.sb("xln", [128, 8, TPC], BF16)
    kf = p.sb("kf", [128, 8, TPC], BF16)
    wA = p.sb("wA", [128, 8, 128], BF16)
    wB = p.sb("wB", [128, C], BF16)
    bias = p.sb("bias", [128, 8], F32)
    muc = p.sb("muc", [128, 8], BF16)
    m1 = p.sb("m1", [128, 1], BF16)
    u1 = p.sb("u1", [128, TPC], BF16)
    af = p.sb("af", [128, 8, TPC], BF16)
    df = p.sb("df", [128, 2, TPC], F32)
    remc = p.sb("remc", [128, 8], F32)
    mixc = p.sb("mixc", [128, 8], F32)
    omix = p.sb("omix", [128, 8], F32)
    sel2 = p.sb("sel2", [2, 128], F32)
    hsel = p.sb("hsel", [128, 2], F32)
    ns = p.sb("ns", [2, 2, TPC], F32)
    scr = p.sb("scr", [128, 4, 512], F32)
    scr4 = p.sb("scr4", [128, 4, 512], F32)
    ost = p.sb("ost", [128, 2, 512], BF16)
    ost2 = p.sb("ost2", [128, 6, 512], BF16)
    cc = p.const_cols({"tiny": 1e-24})

    p.dma("sync", xln[:], xlnT.rearrange("(c p) t -> p c t", p=128), writes=("xln",))
    p.dma("sync", kf[:], kT.rearrange("(c p) t -> p c t", p=128), writes=("kf",))
    p.dma("sync", sel2[:], sel2d, writes=("sel2",))
    p.dma("sync", hsel[:], hsel_d, writes=("hsel",))
    p.dma("sync", remc[:], remc_d, writes=("remc",), ch=1)
    p.dma("sync", mixc[:], mixc_d, writes=("mixc",), ch=1)
    p.op("vector", lambda v: v.tensor_scalar(omix[:], mixc[:], -1.0, 1.0, ALU.mult, ALU.add),
         reads=("mixc",), writes=("omix",))

    mr = MMRunner(p, "l4b")
    psK = [p.ps(f"l4bk{i}", [128, 512], F32) for i in range(2)]

    for nm in ("a", "d", "g"):
        Ad, Bd, bd, mud = lr[nm]
        p.dma("gpsimd", wA[:], Ad.rearrange("(c p) r -> p c r", p=128), writes=("wA",))
        p.dma("gpsimd", wB[:], Bd, writes=("wB",))
        p.dma("sync", bias[:], bd, writes=("bias",), ch=1)
        p.dma("gpsimd", muc[:], mud, writes=("muc",))
        mms = [(lambda t, ps, s, e, dc=dc:
                t.matmul(ps[:, 0:1], wA[:, dc, :], muc[:, dc:dc + 1], start=s, stop=e))
               for dc in range(8)]
        mr.group(mms, lambda e, ps: copy_any(e, m1[:], ps[:, 0:1]),
                 reads=("wA", "muc"), evac_writes=("m1",), evac_eng="vector")
        for ec in range(8):
            mms = [(lambda t, ps, s, e, ec=ec:
                    t.matmul(ps[:, 0:1], wB[:, ec * 128:(ec + 1) * 128], m1[:],
                             start=s, stop=e))]
            mr.group(mms, lambda e, ps, ec=ec: e.tensor_tensor(
                bias[:, ec:ec + 1], bias[:, ec:ec + 1], ps[:, 0:1], ALU.add),
                reads=("wB", "m1"), evac_writes=("bias",), evac_eng="vector")
        for tc in range(4):
            mms = [(lambda t, ps, s, e, dc=dc, tc=tc:
                    t.matmul(ps, wA[:, dc, :], xln[:, dc, tc * 512:(tc + 1) * 512],
                             start=s, stop=e)) for dc in range(8)]
            mr.group(mms, lambda e, ps, tc=tc: copy_any(e, u1[:, tc * 512:(tc + 1) * 512], ps),
                     reads=("wA", "xln"), evac_writes=("u1",), evac_eng="vector")
        for ec in range(8):
            for tc in range(4):
                mms = [(lambda t, ps, s, e, ec=ec, tc=tc:
                        t.matmul(ps, wB[:, ec * 128:(ec + 1) * 128],
                                 u1[:, tc * 512:(tc + 1) * 512], start=s, stop=e))]
                if nm == "a":
                    ev = (lambda e, ps, ec=ec, tc=tc: e.activation(
                        af[:, ec, tc * 512:(tc + 1) * 512], ps, AF.Sigmoid,
                        bias=bias[:, ec:ec + 1]))
                    wkey = "af"
                elif nm == "g":
                    ev = (lambda e, ps, ec=ec, tc=tc: e.activation(
                        ost[:, 0, :], ps, AF.Sigmoid, bias=bias[:, ec:ec + 1]))
                    wkey = "ost0"
                else:
                    ev = (lambda e, ps, ec=ec, tc=tc: e.activation(
                        scr[:, 2, :], ps, AF.Tanh, bias=bias[:, ec:ec + 1]))
                    wkey = "scr2"
                mr.group(mms, ev, reads=("wB", "u1"), evac_reads=("bias",),
                         evac_writes=(wkey,), evac_eng="scalar")
                if nm == "g":
                    p.dma("sync", outs["gate_f"][ec * 128:(ec + 1) * 128, tc * 512:(tc + 1) * 512],
                          ost[:, 0, :], reads=("ost0",), ch=2)
                elif nm == "d":
                    p.op("scalar", lambda s, ec=ec, tc=tc: s.activation(
                        scr[:, 2, :], scr[:, 2, :], AF.Sigmoid),
                        reads=("scr2",), writes=("scr2",))
                    p.op("scalar", lambda s, ec=ec, tc=tc: s.activation(
                        df[:, ec % 2, tc * 512:(tc + 1) * 512], scr[:, 2, :], AF.Exp,
                        scale=-DECAY_C),
                        reads=("scr2",), writes=(f"df{ec % 2}",))
            if nm == "d":
                p.dma("sync", d_f[ec * 128:(ec + 1) * 128, :], df[:, ec % 2, :],
                      reads=(f"df{ec % 2}",), ch=2)

    p.begin_lanes(2)
    for ec in range(8):
        p.lane(ec % 2)
        LN_ = ec % 2
        for tc in range(4):
            p.op("vector", lambda v, ec=ec, tc=tc, LN_=LN_: v.tensor_scalar(
                scr[:, 2 * LN_, :], kf[:, ec, tc * 512:(tc + 1) * 512],
                remc[:, ec:ec + 1], None, ALU.mult),
                reads=("kf", "remc"), writes=(f"scr0_{LN_}",))
            p.op("scalar", lambda s, LN_=LN_: s.activation(scr[:, 2 * LN_ + 1, :], scr[:, 2 * LN_, :], AF.Square),
                 reads=(f"scr0_{LN_}",), writes=(f"scr1_{LN_}",))
            p.op("tensor", lambda t, LN_=LN_: t.matmul(
                psK[LN_][0:2, 0:512], hsel[:], scr[:, 2 * LN_ + 1, :],
                start=True, stop=True),
                reads=(f"scr1_{LN_}", "hsel"), writes=(f"l4bk{LN_}",))
            p.op("vector", lambda v, tc=tc, LN_=LN_: v.tensor_copy(
                ns[0:2, LN_, tc * 512:(tc + 1) * 512], psK[LN_][0:2, 0:512]),
                reads=(f"l4bk{LN_}",), writes=(f"ns{LN_}",))
        p.op("vector", lambda v, LN_=LN_: v.tensor_scalar_max(
            ns[:, LN_, :], ns[:, LN_, :], cc["tiny"][0:2, :]),
            reads=(f"ns{LN_}", "constc"), writes=(f"ns{LN_}",))
        p.op("scalar", lambda s, LN_=LN_: s.activation(ns[:, LN_, :], ns[:, LN_, :], AF.Ln),
             reads=(f"ns{LN_}",), writes=(f"ns{LN_}",))
        p.op("scalar", lambda s, LN_=LN_: s.activation(ns[:, LN_, :], ns[:, LN_, :],
                                                       AF.Exp, scale=-0.5),
             reads=(f"ns{LN_}",), writes=(f"ns{LN_}",))
        for tc in range(4):
            p.op("tensor", lambda t, tc=tc, LN_=LN_: t.matmul(
                psK[LN_][:, 0:512], sel2[:], ns[0:2, LN_, tc * 512:(tc + 1) * 512],
                start=True, stop=True),
                reads=(f"ns{LN_}", "sel2"), writes=(f"l4bk{LN_}",))
            p.op("vector", lambda v, LN_=LN_: v.tensor_copy(
                scr4[:, LN_, :], psK[LN_][:, 0:512]),
                reads=(f"l4bk{LN_}",), writes=(f"scr3_{LN_}",))
            p.op("vector", lambda v, ec=ec, tc=tc, LN_=LN_: v.tensor_scalar(
                scr[:, 2 * LN_, :], kf[:, ec, tc * 512:(tc + 1) * 512],
                remc[:, ec:ec + 1], None, ALU.mult),
                reads=("kf", "remc"), writes=(f"scr0_{LN_}",))
            p.op("vector", lambda v, LN_=LN_: v.tensor_tensor(scr[:, 2 * LN_ + 1, :], scr[:, 2 * LN_, :],
                                                     scr4[:, LN_, :], ALU.mult),
                 reads=(f"scr0_{LN_}", f"scr3_{LN_}"), writes=(f"scr1_{LN_}",))
            p.op("vector", lambda v, LN_=LN_: v.tensor_copy(ost2[:, LN_, :], scr[:, 2 * LN_ + 1, :]),
                 reads=(f"scr1_{LN_}",), writes=(f"ost2_{LN_}",))
            p.dma("sync", outs["kap_f"][ec * 128:(ec + 1) * 128, tc * 512:(tc + 1) * 512],
                  ost2[:, LN_, :], reads=(f"ost2_{LN_}",), ch=2 + LN_)
            p.op("vector", lambda v, ec=ec, tc=tc, LN_=LN_: v.tensor_tensor(
                ost2[:, 4 + LN_, :], scr[:, 2 * LN_ + 1, :], af[:, ec, tc * 512:(tc + 1) * 512], ALU.mult),
                reads=(f"scr1_{LN_}", "af"), writes=(f"ost2_{4 + LN_}",))
            p.dma("sync", outs["ktil_f"][ec * 128:(ec + 1) * 128, tc * 512:(tc + 1) * 512],
                  ost2[:, 4 + LN_, :], reads=(f"ost2_{4 + LN_}",), ch=2 + LN_)
            p.op("gpsimd", lambda v, ec=ec, tc=tc, LN_=LN_: v.tensor_scalar(
                scr4[:, 2 + LN_, :], af[:, ec, tc * 512:(tc + 1) * 512],
                mixc[:, ec:ec + 1], omix[:, ec:ec + 1], ALU.mult, ALU.add),
                reads=("af", "mixc", "omix"), writes=(f"scr2_{LN_}",))
            p.op("gpsimd", lambda v, ec=ec, tc=tc, LN_=LN_: v.tensor_tensor(
                ost2[:, 2 + LN_, :], scr4[:, 2 + LN_, :],
                kf[:, ec, tc * 512:(tc + 1) * 512], ALU.mult),
                reads=(f"scr2_{LN_}", "kf"), writes=(f"ost2_{2 + LN_}",))
            p.dma("sync", outs["rho_f"][ec * 128:(ec + 1) * 128, tc * 512:(tc + 1) * 512],
                  ost2[:, 2 + LN_, :], reads=(f"ost2_{2 + LN_}",), ch=2 + LN_)
    p.end_lanes()
    return p.finish()


def build_l5(Sv=S):
    p = Prog()
    CH = 128
    NCH = Sv // CH
    kapD = p.din("kapP", [256, Sv], BF16)
    ktlD = p.din("ktlP", [256, Sv], BF16)
    rhoD = p.din("rhoP", [256, Sv], BF16)
    vD = p.din("vP", [256, Sv], BF16)
    rD = p.din("rP", [256, Sv], BF16)
    dD = p.din("dP", [256, Sv], F32)
    mABd = p.din("mABd", [128, 256], F32)
    mSUd = p.din("mSUd", [128, 128], F32)
    mIUd = p.din("mIUd", [128, 128], F32)
    identd = p.din("identbf", [128, 128], BF16)
    yD = p.dout("yP", [256, Sv], F32)

    WIN = 512
    NW = Sv // WIN
    win = {nm: p.sb("w_" + nm, [128 if nm != "d" else 64, 2, 4, WIN],
                     BF16 if nm != "d" else F32)
           for nm in ("kap", "ktl", "rho", "v", "r", "d")}
    mAB = p.sb("mAB", [128, 256], F32)
    mSU = p.sb("mSU", [128, 128], F32)
    mIU = p.sb("mIU", [128, 128], F32)
    zz = p.sb("zz", [64, CH], F32)
    identf = p.sb("identf_s", [128, 128], BF16)

    def mk(nm, shape, dt=BF16):
        return [p.sb(f"{nm}{pr}", shape, dt) for pr in range(4)]
    def mk2(nm, shape, dt=BF16):
        return [[p.sb(f"{nm}{pr}c{cp}", shape, dt) for pr in range(4)]
                for cp in range(2)]
    SW = mk("SW", [128, 64]); SWT = mk("SWT", [128, 64])
    dp = mk2("dp", [64, CH], F32)
    dpp = mk("dpp", [64, CH], F32); rdp = mk("rdp", [64, CH], F32)
    Ktd = mk("Ktd", [128, CH]); KVd = mk("KVd", [128, 2 * CH])
    Kdt = mk2("Kdt", [CH, 64]); Vdt = mk2("Vdt", [CH, 64]); Pt = mk2("Pt", [CH, 64])
    AT = mk("AT", [CH, 2 * CH]); Bm = mk("Bm", [CH, CH]); Tt = mk("Tt", [CH, CH])
    Z = mk2("Z", [CH, 128])
    SQ = [p.sb(f"SQ{pr}", [CH, 11, CH], BF16) for pr in range(4)]
    Wmf = mk2("Wmf", [128, CH]); Mf = mk2("Mf", [128, CH])
    C1T = mk2("C1T", [CH, CH]); D2 = mk2("D2", [CH, CH]); M1s = mk2("M1s", [CH, CH])
    Q0 = mk("Q0", [128, CH]); Ht = mk("Ht", [CH, 64])
    T2 = mk("T2", [128, 64])
    YS = mk("YS", [64, CH], F32)

    ps5 = [p.ps(f"l5p{i}", [128, 512], F32) for i in range(4)]
    pk5 = [f"l5p{i}" for i in range(4)]
    ps_acc = [p.ps(f"l5a{i}", [128, 512], F32) for i in range(4)]
    pk_acc = [f"l5a{i}" for i in range(4)]

    def bank(pr):
        return ps5[pr], pk5[pr]

    p.dma("sync", mAB[:], mABd, writes=("mAB",))
    p.dma("sync", mSU[:], mSUd, writes=("mSU",))
    p.dma("sync", mIU[:], mIUd, writes=("mIU",))
    p.dma("sync", identf[:], identd, writes=("identf",))
    p.op("vector", lambda v: v.memset(zz[:], 0.0), writes=("zz",))
    for nm in ("kap", "ktl", "rho", "v", "r"):
        p.op("vector", lambda v, nm=nm: v.memset(win[nm][64:128], 0.0),
             writes=tuple(f"win_{nm}{pr}_{sl}" for pr in range(4) for sl in range(2)))
    for pr in range(4):
        P = f"p{pr}"
        p.op("vector", lambda v, pr=pr: v.memset(Ktd[pr][64:128, :], 0.0),
             writes=(P + "Ktd",))
        p.op("vector", lambda v, pr=pr: v.memset(KVd[pr][64:128, :], 0.0),
             writes=(P + "KVd",))
        p.op("vector", lambda v, pr=pr: v.memset(Q0[pr][64:128, :], 0.0),
             writes=(P + "Q0",))
        p.op("vector", lambda v, pr=pr: v.memset(T2[pr][64:128, :], 0.0),
             writes=(P + "T2",))
        for cp in range(2):
            P2 = f"p{pr}c{cp}"
            p.op("vector", lambda v, pr=pr, cp=cp: v.memset(
                Wmf[cp][pr][64:128, :], 0.0), writes=(P2 + "Wmf",))
            p.op("vector", lambda v, pr=pr, cp=cp: v.memset(
                Mf[cp][pr][64:128, :], 0.0), writes=(P2 + "Mf",))
    for pr in range(4):
        p.op("vector", lambda v, pr=pr: v.memset(SW[pr][:], 0.0), writes=(f"SW{pr}",))
        p.op("vector", lambda v, pr=pr: v.memset(SWT[pr][:], 0.0), writes=(f"SWT{pr}",))

    def ld_window(w):
        c0 = w * WIN
        sl = w % 2
        for nm, dram, eng in (("kap", kapD, "gpsimd"), ("ktl", ktlD, "gpsimd"),
                              ("rho", rhoD, "gpsimd"), ("v", vD, "gpsimd"),
                              ("r", rD, "gpsimd"), ("d", dD, "sync")):
            for pr in range(4):
                p.dma(eng, win[nm][0:64, sl, pr, :] if nm != "d"
                      else win[nm][:, sl, pr, :],
                      dram[pr * 64:(pr + 1) * 64, c0:c0 + WIN],
                      writes=(f"win_{nm}{pr}_{sl}",), ch=pr % 3)

    def mm(lhs, rhs, out_sl, reads, wkey, start=True, stop=True, bankref=None):
        if bankref is None:
            bk, key = bank(pr_cur[0])
        else:
            bk, key = bankref
        p.op("tensor", lambda t, lhs=lhs, rhs=rhs, bk=bk, sl=out_sl, s=start, e=stop:
             t.matmul(bk[sl], lhs, rhs, start=s, stop=e),
             reads=reads, writes=(key,))
        return bk, key

    def tr2(in_ap, reads, wkey_out, out_sb, scale_col=None, scale_key=None,
            ihr=None):
        bk, key = bank(pr_cur[0])
        ih, iw = in_ap.shape[0], in_ap.shape[1]
        ihr = ihr or ih
        p.op("tensor", lambda t, i=in_ap, bk=bk, ih=ih, iw=iw, ihr=ihr: t.matmul(
            bk[0:iw, 0:ihr], i, identf[0:ih, 0:ihr], start=True, stop=True),
            reads=reads + ("identf",), writes=(key,))
        if scale_col is None:
            p.op("scalar", lambda v, bk=bk, o=out_sb, ihr=ihr, iw=iw: v.copy(
                o[:], bk[0:iw, 0:ihr]), reads=(key,), writes=(wkey_out,))
        else:
            p.op("vector", lambda v, bk=bk, o=out_sb, sc=scale_col, ihr=ihr, iw=iw:
                 v.tensor_scalar(o[:], bk[0:iw, 0:ihr], sc, None, ALU.mult),
                 reads=(key, scale_key), writes=(wkey_out,))

    pr_cur = [0]
    ld_window(0)
    for w in range(NW):
        if w + 1 < NW:
            ld_window(w + 1)
        for ci in range(WIN // CH):
            cn = w * (WIN // CH) + ci
            cs = slice(ci * CH, (ci + 1) * CH)
            p.begin_lanes(8)
            cp = cn % 2
            for pr in range(4):
                p.lane(pr * 2 + cp)
                pr_cur[0] = pr
                P = f"p{pr}"
                P2 = f"p{pr}c{cp}"
                sl = w % 2
                wk = tuple(f"win_{nm}{pr}_{sl}" for nm in
                           ("kap", "ktl", "rho", "v", "r", "d"))
                Kf = win["kap"][0:64, sl, pr, cs]
                Ktf = win["ktl"][0:64, sl, pr, cs]
                Pf = win["rho"][:, sl, pr, cs]; Vf = win["v"][0:64, sl, pr, cs]
                Rf = win["r"][:, sl, pr, cs]; Df = win["d"][:, sl, pr, cs]
                p.op("vector", lambda v, pr=pr, Df=Df, cp=cp: v.tensor_tensor_scan(
                    dp[cp][pr][:], Df, zz[:], 1.0, ALU.mult, ALU.add),
                    reads=(wk[5], "zz"), writes=(P2 + "dp",))
                p.op("vector", lambda v, pr=pr: v.memset(dpp[pr][:, 0:1], 1.0),
                     writes=(P + "dpp",))
                p.op("gpsimd", lambda v, pr=pr, cp=cp: v.tensor_copy(
                    dpp[pr][:, 1:CH], dp[cp][pr][:, 0:CH - 1]),
                    reads=(P2 + "dp",), writes=(P + "dpp",))
                p.op("vector", lambda v, pr=pr, cp=cp: v.reciprocal(rdp[pr][:], dp[cp][pr][:]),
                     reads=(P2 + "dp",), writes=(P + "rdp",))
                p.op("gpsimd", lambda v, pr=pr, Ktf=Ktf: v.tensor_tensor(
                    Ktd[pr][0:64, :], Ktf, dpp[pr][:], ALU.mult),
                    reads=(wk[1], P + "dpp"), writes=(P + "Ktd",))
                p.op("gpsimd", lambda v, pr=pr, Kf=Kf: v.tensor_tensor(
                    KVd[pr][0:64, 0:CH], Kf, rdp[pr][:], ALU.mult),
                    reads=(wk[0], P + "rdp"), writes=(P + "KVd",))
                p.op("gpsimd", lambda v, pr=pr, Vf=Vf: v.tensor_tensor(
                    KVd[pr][0:64, CH:2 * CH], Vf, rdp[pr][:], ALU.mult),
                    reads=(wk[3], P + "rdp"), writes=(P + "KVd",))
                tr2(KVd[pr][:, 0:CH], (P + "KVd",), P2 + "Kdt", Kdt[cp][pr], ihr=64)
                p.op("vector", lambda v, pr=pr, cp=cp: v.tensor_scalar_mul(
                    Kdt[cp][pr][:], Kdt[cp][pr][:], -1.0),
                    reads=(P2 + "Kdt",), writes=(P2 + "Kdt",))
                tr2(KVd[pr][:, CH:2 * CH], (P + "KVd",), P2 + "Vdt", Vdt[cp][pr], ihr=64)
                bkp, keyp = bank(pr)
                p.op("tensor", lambda t, Pf=Pf, bk=bkp: t.matmul(
                    bk[0:CH, 0:64], Pf, identf[0:128, 0:64], start=True,
                    stop=True), reads=(wk[2], "identf"), writes=(keyp,))
                p.op("scalar", lambda v, bk=bkp, pr=pr, cp=cp: v.copy(
                    Pt[cp][pr][:], bk[0:CH, 0:64]), reads=(keyp,), writes=(P2 + "Pt",))
                bk1, key1 = mm(Ktd[pr][:], KVd[pr][:], (slice(0, CH), slice(0, 2 * CH)),
                               (P + "Ktd", P + "KVd"), None)
                p.op("vector", lambda v, bk=bk1, pr=pr: v.tensor_tensor(
                    AT[pr][:], bk[0:CH, 0:2 * CH], mAB[:], ALU.mult),
                    reads=(key1, "mAB"), writes=(P + "AT",))
                bk2, key2 = mm(KVd[pr][:, 0:CH], Ktd[pr][:], (slice(0, CH), slice(0, CH)),
                               (P + "KVd", P + "Ktd"), None)
                p.op("vector", lambda v, bk=bk2, pr=pr: v.tensor_tensor(
                    Bm[pr][:], bk[0:CH, 0:CH], mSU[:], ALU.mult),
                    reads=(key2, "mSU"), writes=(P + "Bm",))
                bk2b, key2b = mm(KVd[pr][:, CH:2 * CH], Ktd[pr][:],
                                 (slice(0, CH), slice(0, CH)),
                                 (P + "KVd", P + "Ktd"), None)
                p.op("vector", lambda v, bk=bk2b, pr=pr: v.tensor_tensor(
                    Tt[pr][:], bk[0:CH, 0:CH], mSU[:], ALU.mult),
                    reads=(key2b, "mSU"), writes=(P + "Tt",))
                bk3, key3 = bank(pr)
                p.op("tensor", lambda t, pr=pr, bk=bk3: t.matmul(
                    bk[0:CH, 0:64], Ktd[pr][:], identf[0:128, 0:64],
                    start=True, stop=True),
                    reads=(P + "Ktd", "identf"), writes=(key3,))
                p.op("scalar", lambda v, bk=bk3, pr=pr, cp=cp: v.copy(
                    Z[cp][pr][:, 0:64], bk[0:CH, 0:64]), reads=(key3,), writes=(P2 + "Z",))
                bk4, key4 = mm(Tt[pr][:], Pt[cp][pr][:], (slice(0, CH), slice(0, 64)),
                               (P + "Tt", P2 + "Pt"), None)
                p.op("scalar", lambda v, bk=bk4, pr=pr, cp=cp: v.copy(
                    Z[cp][pr][:, 64:128], bk[0:CH, 0:64]), reads=(key4,), writes=(P2 + "Z",))
                sq = SQ[pr]
                sqk = P + "SQ"
                Asl = AT[pr][:, 0:CH]
                Bsl = Bm[pr][:]

                def apply_z(lh, rkeys, sub=False):
                    bkx, keyx = mm(lh, Z[cp][pr][:], (slice(0, CH), slice(0, 128)),
                                   rkeys + (P2 + "Z",), None)
                    p.op("vector", lambda v, bk=bkx, pr=pr, sub=sub, cp=cp: v.tensor_tensor(
                        Z[cp][pr][:], Z[cp][pr][:], bk[0:CH, 0:128],
                        ALU.subtract if sub else ALU.add),
                        reads=(keyx, P2 + "Z"), writes=(P2 + "Z",))

                apply_z(Bsl, (P + "Bm",), sub=True)
                for (lh, rh, slot) in ((Bsl, Asl, 0), (Asl, Bsl, 1)):
                    bkx, keyx = mm(lh, rh, (slice(0, CH), slice(0, CH)),
                                   (P + "AT", P + "Bm"), None)
                    p.op("scalar", lambda v, bk=bkx, sq=sq, slot=slot: v.copy(
                        sq[:, slot, :], bk[0:CH, 0:CH]), reads=(keyx,), writes=(sqk,))
                apply_z(sq[:, 1, :], (sqk,))
                for lev in range(4):
                    a = 2 * lev; b = 2 * lev + 1
                    for (li, ri, slot) in ((b, a, a + 2), (a, b, b + 2)):
                        bkx, keyx = mm(sq[:, li, :], sq[:, ri, :],
                                       (slice(0, CH), slice(0, CH)), (sqk,), None)
                        p.op("scalar", lambda v, bk=bkx, sq=sq, slot=slot: v.copy(
                            sq[:, slot, :], bk[0:CH, 0:CH]), reads=(keyx,), writes=(sqk,))
                    apply_z(sq[:, 2 * lev + 3, :], (sqk,))
                bkx, keyx = mm(sq[:, 8, :], sq[:, 9, :], (slice(0, CH), slice(0, CH)),
                               (sqk,), None)
                p.op("scalar", lambda v, bk=bkx, sq=sq: v.copy(
                    sq[:, 10, :], bk[0:CH, 0:CH]), reads=(keyx,), writes=(sqk,))
                apply_z(sq[:, 10, :], (sqk,))
                tr2(Z[cp][pr][:, 0:64], (P2 + "Z",), P2 + "Wmf", Wmf[cp][pr][0:64, :])
                tr2(Z[cp][pr][:, 64:128], (P2 + "Z",), P2 + "Mf", Mf[cp][pr][0:64, :])
                bkx, keyx = mm(Pf, Rf, (slice(0, CH), slice(0, CH)),
                               (wk[2], wk[4]), None)
                p.op("vector", lambda v, bk=bkx, pr=pr, cp=cp: v.tensor_tensor(
                    C1T[cp][pr][:], bk[0:CH, 0:CH], mIU[:], ALU.mult),
                    reads=(keyx, "mIU"), writes=(P2 + "C1T",))
                bkx, keyx = mm(Mf[cp][pr][:], Rf, (slice(0, CH), slice(0, CH)),
                               (P2 + "Mf", wk[4]), None)
                p.op("vector", lambda v, bk=bkx, pr=pr, cp=cp: v.tensor_tensor(
                    D2[cp][pr][:], bk[0:CH, 0:CH], mIU[:], ALU.mult),
                    reads=(keyx, "mIU"), writes=(P2 + "D2",))
                bkq, keyq = mm(SW[pr][:], Rf, (slice(0, 64), slice(0, CH)),
                               (f"SW{pr}", wk[4]), None)
                p.op("scalar", lambda s, bk=bkq, pr=pr: s.copy(
                    Q0[pr][0:64, :], bk[0:64, 0:CH]), reads=(keyq,), writes=(P + "Q0",))
                bkx, keyx = mm(Wmf[cp][pr][:], Q0[pr][:], (slice(0, CH), slice(0, CH)),
                               (P2 + "Wmf", P + "Q0"), None)
                p.op("vector", lambda v, bk=bkx, pr=pr, cp=cp: v.tensor_tensor(
                    M1s[cp][pr][:], bk[0:CH, 0:CH], mIU[:], ALU.mult),
                    reads=(keyx, "mIU"), writes=(P2 + "M1s",))
                p.op("vector", lambda v, pr=pr, cp=cp: v.tensor_tensor(
                    D2[cp][pr][:], D2[cp][pr][:], M1s[cp][pr][:], ALU.add),
                    reads=(P2 + "M1s", P2 + "D2"), writes=(P2 + "D2",))
                bky, keyy = mm(Vdt[cp][pr][:], C1T[cp][pr][:], (slice(0, 64), slice(0, CH)),
                               (P2 + "Vdt", P2 + "C1T"), None, start=True, stop=False,
                               bankref=(ps_acc[pr], pk_acc[pr]))
                mm(Kdt[cp][pr][:], D2[cp][pr][:], (slice(0, 64), slice(0, CH)),
                   (P2 + "Kdt", P2 + "D2"), None, start=False, stop=True,
                   bankref=(bky, keyy))
                p.op("vector", lambda v, bk=bky, pr=pr: v.tensor_tensor(
                    YS[pr][:], bk[0:64, 0:CH], Q0[pr][0:64, :], ALU.add),
                    reads=(keyy, P + "Q0"), writes=(P + "YS",))
                p.op("vector", lambda v, pr=pr, cp=cp: v.tensor_tensor(
                    YS[pr][:], YS[pr][:], dp[cp][pr][:], ALU.mult),
                    reads=(P + "YS", P2 + "dp"), writes=(P + "YS",))
                p.dma("sync", yD[pr * 64:(pr + 1) * 64, cn * CH:(cn + 1) * CH],
                      YS[pr][:], reads=(P + "YS",), ch=pr)
                bkh, keyh = mm(Wmf[cp][pr][:], SWT[pr][:], (slice(0, CH), slice(0, 64)),
                               (P2 + "Wmf", f"SWT{pr}"), None)
                p.op("scalar", lambda s, bk=bkh, pr=pr: s.copy(
                    Ht[pr][:], bk[0:CH, 0:64]), reads=(keyh,), writes=(P + "Ht",))
                p.op("vector", lambda v, pr=pr, cp=cp: v.tensor_tensor(
                    Ht[pr][:], Ht[pr][:], Z[cp][pr][:, 64:128], ALU.add),
                    reads=(P + "Ht", P2 + "Z"), writes=(P + "Ht",))
                bk6, key6 = mm(Pt[cp][pr][:], Vdt[cp][pr][:], (slice(0, 64), slice(0, 64)),
                               (P2 + "Pt", P2 + "Vdt"), None, start=True, stop=False,
                               bankref=(ps_acc[pr], pk_acc[pr]))
                mm(Ht[pr][:], Kdt[cp][pr][:], (slice(0, 64), slice(0, 64)),
                   (P + "Ht", P2 + "Kdt"), None, start=False, stop=True,
                   bankref=(bk6, key6))
                p.op("vector", lambda v, bk=bk6, pr=pr: v.tensor_tensor(
                    T2[pr][0:64, :], SW[pr][0:64, :], bk[0:64, 0:64], ALU.add),
                    reads=(key6, f"SW{pr}"), writes=(P + "T2",))
                tr2(T2[pr][:], (P + "T2",), f"SWT{pr}", SWT[pr][0:64, :],
                    scale_col=dp[cp][pr][:, CH - 1:CH], scale_key=P2 + "dp",
                    ihr=64)
                tr2(SWT[pr][:], (f"SWT{pr}",), f"SW{pr}", SW[pr][0:64, :],
                    ihr=64)
            p.end_lanes()
    return p.finish()


def build_l6a():
    p = Prog()
    yT = p.din("yT", [C, TPC], F32)
    rT = p.din("rT", [C, TPC], BF16)
    rhoT = p.din("rhoT", [C, TPC], BF16)
    vT = p.din("vT", [C, TPC], BF16)
    gT = p.din("gT", [C, TPC], BF16)
    lngc = p.din("lngc", [128, 8], F32)
    lnbc = p.din("lnbc", [128, 8], F32)
    bmc = p.din("bmc", [128, 8], F32)
    hsel_d = p.din("hsel_d", [128, 2], F32)
    sel2d = p.din("sel2_d", [2, 128], F32)
    zT = p.dout("zT", [C, TPC], BF16)

    rc = p.sb("rc", [128, TPC], BF16)
    rhoc = p.sb("rhoc", [128, TPC], BF16)
    vc = p.sb("vc", [128, TPC], BF16)
    gc = p.sb("gc", [128, TPC], BF16)
    yc = p.sb("yc", [128, TPC], F32)
    ybb = p.sb("ybb", [128, 8, TPC], BF16)
    sq = ybb
    sqb = p.sb("sqq", [128, 8, TPC], BF16)
    lg = p.sb("lg", [128, 8], F32)
    lb = p.sb("lb", [128, 8], F32)
    bm02 = p.sb("bm02", [128, 8], F32)
    hsel = p.sb("hsel", [128, 2], F32)
    sel2 = p.sb("sel2", [2, 128], F32)
    ns = p.sb("ns", [2, TPC], F32)
    scr = p.sb("scr", [128, 2, TPC // 2], F32)
    scrq = p.sb("scrq", [128, 512], F32)
    Ab = p.sb("Ab", [128, TPC], F32)
    Bb = p.sb("Bb", [128, TPC], F32)
    rows = p.sb("rows", [1, 3, TPC], F32)
    zst = p.sb("zst", [128, 2, TPC], BF16)
    onesb = p.sb("onesb", [128, 1], BF16)
    ones1 = p.sb("ones1", [1, 128], F32)
    cc = p.const_cols({"eps6": 1e-6})
    p.op("gpsimd", lambda g: g.memset(onesb[:], 1.0), writes=("onesb",))
    p.op("gpsimd", lambda g: g.memset(ones1[:], 1.0), writes=("ones1",))
    p.dma("sync", lg[:], lngc, writes=("gcols",), ch=1)
    p.dma("sync", lb[:], lnbc, writes=("gcols",), ch=1)
    p.dma("sync", bm02[:], bmc, writes=("bm02",), ch=1)
    p.dma("sync", hsel[:], hsel_d, writes=("hsel",), ch=1)
    p.dma("sync", sel2[:], sel2d, writes=("sel2",), ch=1)
    p.op("vector", lambda v: v.tensor_scalar_mul(bm02[:], bm02[:], 0.2),
         reads=("bm02",), writes=("bm02",))

    mr = MMRunner(p, "l6a")
    yTr = yT.rearrange("(c p) t -> p c t", p=128)
    for ec in range(8):
        p.dma("sync", rc[:], rT.rearrange("(c p) t -> p c t", p=128)[:, ec], writes=("rc",))
        p.dma("sync", rhoc[:], rhoT.rearrange("(c p) t -> p c t", p=128)[:, ec],
              writes=("rhoc",), ch=1)
        p.dma("sync", vc[:], vT.rearrange("(c p) t -> p c t", p=128)[:, ec],
              writes=("vc",), ch=2)
        p.dma("gpsimd", yc[:], yTr[:, ec], writes=("yc",))
        for tc in range(4):
            cs = slice(tc * 512, (tc + 1) * 512)
            p.op("vector", lambda v, ec=ec, cs=cs: v.tensor_tensor(
                scrq[:], rc[:, cs], rhoc[:, cs], ALU.mult),
                reads=("rc", "rhoc"), writes=("scrq",))
            mms = [(lambda t, ps, s, e: t.matmul(ps[0:2, :], hsel[:], scrq[:],
                                                 start=s, stop=e))]
            mr.group(mms, lambda e, ps, cs=cs: copy_any(e, ns[0:2, cs], ps[0:2, :]),
                     reads=("scrq", "hsel"), evac_writes=("ns",), evac_eng="vector")
        for tc in range(4):
            cs = slice(tc * 512, (tc + 1) * 512)
            mms = [(lambda t, ps, s, e, cs=cs: t.matmul(ps, sel2[:], ns[0:2, cs],
                                                        start=s, stop=e))]

            def bonus_ev(e, ps, ec=ec, cs=cs):
                return e.tensor_tensor(scrq[:], ps, vc[:, cs], ALU.mult)
            mr.group(mms, bonus_ev, reads=("ns", "sel2"), evac_reads=("vc",),
                     evac_writes=("scrq",), evac_eng="vector")
            p.op("vector", lambda v, ec=ec: v.tensor_scalar(
                scrq[:], scrq[:], bm02[:, ec:ec + 1], None, ALU.mult),
                reads=("scrq", "bm02"), writes=("scrq",))
            p.op("vector", lambda v, ec=ec, cs=cs: v.tensor_tensor(
                ybb[:, ec, cs], scrq[:], yc[:, cs], ALU.add),
                reads=("scrq", "yc"), writes=("ybb",))
    feat_ln(p, mr, "ln6", ybb, "ybb", sqb, rows, Ab, Bb, onesb, ones1, cc, "eps6")
    for ec in range(8):
        p.dma("sync", gc[:], gT.rearrange("(c p) t -> p c t", p=128)[:, ec], writes=("gc",))
        for half in range(2):
            cs = slice(half * 1024, (half + 1) * 1024)
            sb2 = half % 2
            p.op("vector", lambda v, ec=ec, cs=cs, sb2=sb2: v.tensor_tensor(
                scr[:, sb2, :], ybb[:, ec, cs], Ab[:, cs], ALU.mult),
                reads=("ybb", "ln6A"), writes=(f"scr{sb2}",))
            p.op("vector", lambda v, ec=ec, cs=cs, sb2=sb2: v.tensor_tensor(
                scr[:, sb2, :], scr[:, sb2, :], Bb[:, cs], ALU.add),
                reads=(f"scr{sb2}", "ln6B"), writes=(f"scr{sb2}",))
            p.op("vector", lambda v, ec=ec, cs=cs, sb2=sb2: v.tensor_scalar(
                scr[:, sb2, :], scr[:, sb2, :], lg[:, ec:ec + 1], lb[:, ec:ec + 1],
                ALU.mult, ALU.add),
                reads=(f"scr{sb2}", "gcols"), writes=(f"scr{sb2}",))
            p.op("vector", lambda v, ec=ec, cs=cs, sb2=sb2: v.tensor_tensor(
                zst[:, ec % 2, cs], scr[:, sb2, :], gc[:, cs], ALU.mult),
                reads=(f"scr{sb2}", "gc"), writes=(f"zst{ec % 2}",))
        p.dma("sync", zT[ec * 128:(ec + 1) * 128, :], zst[:, ec % 2, :],
              reads=(f"zst{ec % 2}",), ch=2)
    return p.finish()


def build_l6b():
    p = Prog()
    zTd = p.din("zT", [C, TPC], BF16)
    x2T = p.din("x2T", [C, TPC], F32)
    oW1 = p.din("oW1", [C, E1], F32)
    ob1c = p.din("ob1c", [128, 9], F32)
    ob1t = p.din("ob1t", [76, 1], F32)
    oW2 = p.din("oW2", [E1, C], F32)
    ob2c = p.din("ob2c", [128, 8], F32)
    x3T = p.dout("x3T", [C, TPC], F32)
    xln3T = p.dout("xln3T", [C, TPC], BF16)

    zs = p.sb("zs", [128, 8, TPC], BF16)
    w1 = p.sb("w1", [128, 8, E1], BF16)
    w2 = p.sb("w2", [128, NHC, C], BF16)
    b1 = p.sb("b1", [128, NHC], F32)
    b2 = p.sb("b2", [128, 8], F32)
    r1 = p.sb("r1", [128, NHC, TPC], BF16)
    x2t = p.sb("x2t", [128, 2, 512], F32)
    x3b = zs
    xln = p.sb("xlnb", [128, 8, TPC], BF16)
    sq = xln
    Ab = p.sb("Ab", [128, TPC], F32)
    Bb = p.sb("Bb", [128, TPC], F32)
    rows = p.sb("rows", [1, 3, TPC], F32)
    st1 = p.sb("st1", [128, 2, 512], F32)
    scr = p.sb("scr", [128, 2, 512], F32)
    onesb = p.sb("onesb", [128, 1], BF16)
    ones1 = p.sb("ones1", [1, 128], F32)
    cc = p.const_cols({"eps5": 1e-5})
    p.op("gpsimd", lambda g: g.memset(onesb[:], 1.0), writes=("onesb",))
    p.op("gpsimd", lambda g: g.memset(ones1[:], 1.0), writes=("ones1",))

    p.dma("sync", zs[:], zTd.rearrange("(c p) t -> p c t", p=128), writes=("zs",))
    w1R = oW1.rearrange("(c p) h -> p c h", p=128)
    for hcw in range(NHC):
        ww = hc_w(hcw)
        p.dma("gpsimd", w1[:, :, hcw * 128:hcw * 128 + ww],
              w1R[:, :, hcw * 128:hcw * 128 + ww], writes=(f"w1{hcw}",), ch=hcw % 2)
    p.dma("gpsimd", w2[:, 0:9], oW2[0:9 * 128, :].rearrange("(c p) e -> p c e", p=128),
          writes=("w2",))
    p.dma("gpsimd", w2[0:76, 9], oW2[9 * 128:E1, :], writes=("w2",))
    p.dma("sync", b1[:, 0:9], ob1c, writes=("b1",), ch=1)
    p.dma("sync", b1[0:76, 9:10], ob1t, writes=("b1",), ch=1)
    p.dma("sync", b2[:], ob2c, writes=("b2",), ch=1)

    mr = MMRunner(p, "l6b")
    for hc in range(NHC):
        w = hc_w(hc)
        for tc in range(4):
            mms = [(lambda t, ps, s, e, dc=dc, hc=hc, tc=tc, w=w:
                    t.matmul(ps[0:w, :], w1[:, dc, hc * 128:hc * 128 + w],
                             zs[:, dc, tc * 512:(tc + 1) * 512], start=s, stop=e))
                   for dc in range(8)]
            mr.group(mms, lambda e, ps, hc=hc, tc=tc, w=w: e.activation(
                r1[0:w, hc, tc * 512:(tc + 1) * 512], ps[0:w, :], AF.Gelu,
                bias=b1[0:w, hc:hc + 1]),
                reads=(f"w1{hc}", "zs"), evac_reads=("b1",),
                evac_writes=("r1",), evac_eng="scalar")
    x2Tr = x2T.rearrange("(c p) t -> p c t", p=128)
    for tc in range(4):
        for ec in range(8):
            slot = mr.g % 2
            p.dma("sync", x2t[:, slot, :], x2Tr[:, ec, tc * 512:(tc + 1) * 512],
                  writes=(f"x2t{slot}",), ch=2)
            mms = [(lambda t, ps, s, e, hc=hc, ec=ec, tc=tc, w=hc_w(hc):
                    t.matmul(ps, w2[0:w, hc, ec * 128:(ec + 1) * 128],
                             r1[0:w, hc, tc * 512:(tc + 1) * 512], start=s, stop=e))
                   for hc in range(NHC)]

            def ev(e, ps, ec=ec, slot=slot):
                return e.scalar_tensor_tensor(st1[:, slot, :], ps, b2[:, ec:ec + 1],
                                              x2t[:, slot, :], ALU.add, ALU.add)
            mr.group(mms, ev, reads=("w2", "r1"), evac_reads=("b2", f"x2t{slot}"),
                     evac_writes=(f"st1{slot}",), evac_eng="vector")
            p.dma("sync", x3T[ec * 128:(ec + 1) * 128, tc * 512:(tc + 1) * 512],
                  st1[:, slot, :], reads=(f"st1{slot}",), ch=3)
            p.op("scalar", lambda s, ec=ec, tc=tc, slot=slot: s.copy(
                x3b[:, ec, tc * 512:(tc + 1) * 512], st1[:, slot, :]),
                reads=(f"st1{slot}",), writes=("zs",))
    feat_ln(p, mr, "ln3", x3b, "zs", sq, rows, Ab, Bb, onesb, ones1, cc, "eps5")
    feat_ln_apply(p, "ln3", x3b, "zs", Ab, Bb, scr, xln, "ln3sq")
    for ec in range(8):
        p.dma("sync", xln3T[ec * 128:(ec + 1) * 128, :], xln[:, ec, :],
              reads=("ln3sq",), ch=2)
    return p.finish()


def build_l7():
    p = Prog()
    xlnT = p.din("xlnT", [C, TPC], BF16)
    x3T = p.din("x3T", [C, TPC], F32)
    gwD = p.din("gwD", [C, 8], F32)
    w1D = p.din("w1D", [8, C, 512], F32)
    w3D = p.din("w3D", [8, C, 512], F32)
    w2D = p.din("w2D", [8, 512, C], F32)
    identd = p.din("identf", [128, 128], F32)
    outT = p.dout("outT", [C, TPC], F32)

    xln = p.sb("xln", [128, 8, TPC], BF16)
    gw = p.sb("gw", [128, 8, 8], BF16)
    w1 = p.sb("w1e", [128, 2, 8, 512], BF16)
    w3 = p.sb("w3e", [128, 2, 8, 512], BF16)
    w2 = p.sb("w2e", [128, 2, 4, C], BF16)
    identf = p.sb("identf_s", [128, 128], F32)
    lgt = p.sb("lgt", [128, 16, 8], F32)
    eqs = p.sb("eqs", [128, 8], F32)
    p2s = p.sb("p2s", [128, 8], F32)
    scs = p.sb("scs", [128, 8], F32)
    sfx = p.sb("sfx", [1, 8, TPC], BF16)
    Sb = p.sb("Sbc", [128, TPC], BF16)
    hb = p.sb("hb", [128, 4, TPC], BF16)
    sg = p.sb("sg", [128, 2, 512], BF16)
    scr2 = p.sb("scr2", [128, 2, 512], F32)
    yac = p.sb("yac", [128, 8, TPC], F32)
    x3t = p.sb("x3t", [128, 2, 512], F32)
    st1 = scr2
    stt = p.sb("sttl", [128, 6], F32)
    ones1 = p.sb("ones1", [1, 128], BF16)
    p.op("gpsimd", lambda g: g.memset(ones1[:], 1.0), writes=("ones1",))

    xlR = xlnT.rearrange("(c p) t -> p c t", p=128)
    for tq4 in range(4):
        p.dma("sync", xln[:, :, tq4 * 512:(tq4 + 1) * 512],
              xlR[:, :, tq4 * 512:(tq4 + 1) * 512], writes=(f"xln{tq4}",))
    p.dma("gpsimd", gw[:], gwD.rearrange("(c p) e -> p c e", p=128), writes=("gw",))
    p.dma("sync", identf[:], identd, writes=("identf",), ch=1)

    mrs = [MMRunner(p, "l7A", nbank=2), MMRunner(p, "l7B", nbank=2)]
    mr = mrs[0]
    for ti in range(NTILE):
        mms = [(lambda t, ps, s, e, dc=dc, ti=ti:
                t.matmul(ps[:, 0:8], xln[:, dc, ti * 128:(ti + 1) * 128],
                         gw[:, dc, :], start=s, stop=e)) for dc in range(8)]
        mr.group(mms, lambda e, ps, ti=ti: copy_any(e, lgt[:, ti, :], ps[:, 0:8]),
                 reads=(f"xln{ti // 4}", "gw"), evac_writes=("lgt",),
                 evac_eng="vector")
    for ti in range(NTILE):
        p.op("vector", lambda v, ti=ti: v.reduce_max(stt[:, 0:1], lgt[:, ti, :], axis=AX),
             reads=("lgt",), writes=("sm",))
        p.op("vector", lambda v: v.tensor_scalar_mul(stt[:, 1:2], stt[:, 0:1], -1.0),
             reads=("sm",), writes=("snm",))
        p.op("scalar", lambda s, ti=ti: s.activation(
            lgt[:, ti, :], lgt[:, ti, :], AF.Exp, bias=stt[:, 1:2],
            accum_out=stt[:, 2:3]),
            reads=("lgt", "snm"), writes=("lgt", "ssum"))
        p.op("vector", lambda v: v.reciprocal(stt[:, 3:4], stt[:, 2:3]),
             reads=("ssum",), writes=("srl",))
        p.op("vector", lambda v, ti=ti: v.tensor_scalar(
            lgt[:, ti, :], lgt[:, ti, :], stt[:, 3:4], None, ALU.mult),
            reads=("lgt", "srl"), writes=("lgt",))
        p.op("vector", lambda v, ti=ti: v.reduce_max(stt[:, 0:1], lgt[:, ti, :], axis=AX),
             reads=("lgt",), writes=("sm",))
        p.op("vector", lambda v, ti=ti: v.tensor_scalar(
            eqs[:], lgt[:, ti, :], stt[:, 0:1], None, ALU.is_ge),
            reads=("lgt", "sm"), writes=("eqs",))
        p.op("vector", lambda v, ti=ti: v.scalar_tensor_tensor(
            p2s[:], eqs[:], -2.0, lgt[:, ti, :], ALU.mult, ALU.add),
            reads=("eqs", "lgt"), writes=("p2s",))
        p.op("vector", lambda v: v.reduce_max(stt[:, 4:5], p2s[:], axis=AX),
             reads=("p2s",), writes=("top2",))
        p.op("vector", lambda v, ti=ti: v.tensor_scalar(
            eqs[:], lgt[:, ti, :], stt[:, 4:5], None, ALU.is_ge),
            reads=("lgt", "top2"), writes=("eqs",))
        p.op("vector", lambda v, ti=ti: v.tensor_tensor(
            scs[:], lgt[:, ti, :], eqs[:], ALU.mult),
            reads=("lgt", "eqs"), writes=("scs",))
        for e8 in range(8):
            bkx = mr.banks[mr.g % mr.nbank]
            keyx = mr.bkeys[mr.g % mr.nbank]
            mr.g += 1
            p.op("tensor", lambda t, bk=bkx, e8=e8: t.transpose(
                bk[0:1, 0:128], scs[:, e8:e8 + 1], identf[:]),
                reads=("scs", "identf"), writes=(keyx,))
            p.op("vector", lambda v, bk=bkx, ti=ti, e8=e8: v.tensor_copy(
                sfx[0:1, e8, ti * 128:(ti + 1) * 128], bk[0:1, 0:128]),
                reads=(keyx,), writes=("sfx",))

    def ld_exp(e8):
        sl = e8 % 2
        p.dma("gpsimd", w1[:, sl], w1D[e8].rearrange("(c p) h -> p c h", p=128),
              writes=(f"w1e{sl}",), ch=0)
        p.dma("gpsimd", w3[:, sl], w3D[e8].rearrange("(c p) h -> p c h", p=128),
              writes=(f"w3e{sl}",), ch=1)
        p.dma("gpsimd", w2[:, sl], w2D[e8].rearrange("(c p) v -> p c v", p=128),
              writes=(f"w2e{sl}",), ch=2)

    ld_exp(0)
    for e8 in range(8):
        sl = e8 % 2
        if e8 + 1 < 8:
            ld_exp(e8 + 1)
        for tc in range(4):
            cs = slice(tc * 512, (tc + 1) * 512)
            mms = [(lambda t, ps, s, e, e8=e8, cs=cs:
                    t.matmul(ps[:, 0:512], ones1[:], sfx[0:1, e8, cs], start=s, stop=e))]
            mr.group(mms, lambda e, ps, cs=cs: copy_any(e, Sb[:, cs], ps[:, 0:512]),
                     reads=("sfx", "ones1"), evac_writes=("Sbc",), evac_eng="vector")
        p.begin_lanes(2)
        for hc in range(4):
            slot = hc % 2
            p.lane(slot)
            mr2 = mrs[slot]
            for tc in range(4):
                cs = slice(tc * 512, (tc + 1) * 512)
                gm = [(lambda t, ps, s, e, dc=dc, hc=hc, cs=cs, sl=sl:
                       t.matmul(ps, w1[:, sl, dc, hc * 128:(hc + 1) * 128],
                                xln[:, dc, cs], start=s, stop=e))
                      for dc in range(8)]

                def ev1(e, ps, slot=slot, cs=cs):
                    return e.tensor_tensor(scr2[:, slot, :], ps, Sb[:, cs], ALU.mult)
                mr2.group(gm, ev1, reads=(f"w1e{sl}", f"xln{tc}"), evac_reads=("Sbc",),
                          evac_writes=(f"scr2{slot}",), evac_eng="vector")
                p.op("scalar", lambda s, slot=slot: s.activation(
                    sg[:, slot, :], scr2[:, slot, :], AF.Silu),
                    reads=(f"scr2{slot}",), writes=(f"sg{slot}",))
                um = [(lambda t, ps, s, e, dc=dc, hc=hc, cs=cs, sl=sl:
                       t.matmul(ps, w3[:, sl, dc, hc * 128:(hc + 1) * 128],
                                xln[:, dc, cs], start=s, stop=e))
                      for dc in range(8)]

                def ev3(e, ps, slot=slot, cs=cs):
                    return e.tensor_tensor(scr2[:, slot, :], ps, Sb[:, cs], ALU.mult)
                mr2.group(um, ev3, reads=(f"w3e{sl}", f"xln{tc}"), evac_reads=("Sbc",),
                          evac_writes=(f"scr2{slot}",), evac_eng="vector")
                p.op("vector", lambda v, slot=slot, hc=hc, cs=cs: v.tensor_tensor(
                    hb[:, hc, cs], scr2[:, slot, :], sg[:, slot, :], ALU.mult),
                    reads=(f"scr2{slot}", f"sg{slot}"), writes=(f"hb{slot}",))
        p.end_lanes()
        p.begin_lanes(2)
        for ec in range(8):
            p.lane(ec % 2)
            mr2 = mrs[ec % 2]
            for tc in range(4):
                mms = [(lambda t, ps, s, e, hc=hc, ec=ec, tc=tc, sl=sl:
                        t.matmul(ps, w2[:, sl, hc, ec * 128:(ec + 1) * 128],
                                 hb[:, hc, tc * 512:(tc + 1) * 512], start=s, stop=e))
                       for hc in range(4)]
                if e8 == 0:
                    ev = (lambda e, ps, ec=ec, tc=tc: copy_any(
                        e, yac[:, ec, tc * 512:(tc + 1) * 512], ps))
                else:
                    ev = (lambda e, ps, ec=ec, tc=tc: e.tensor_tensor(
                        yac[:, ec, tc * 512:(tc + 1) * 512],
                        yac[:, ec, tc * 512:(tc + 1) * 512], ps, ALU.add))
                mr2.group(mms, ev, reads=(f"w2e{sl}", "hb0", "hb1"),
                          evac_writes=(f"yac{ec % 2}",), evac_eng="vector")
        p.end_lanes()
    x3Tr = x3T.rearrange("(c p) t -> p c t", p=128)
    for tc in range(4):
        for ec in range(8):
            slot = ec % 2
            p.dma("sync", x3t[:, slot, :], x3Tr[:, ec, tc * 512:(tc + 1) * 512],
                  writes=(f"x3t{slot}",), ch=2)
            p.op("vector", lambda v, ec=ec, tc=tc, slot=slot: v.tensor_tensor(
                st1[:, slot, :], yac[:, ec, tc * 512:(tc + 1) * 512],
                x3t[:, slot, :], ALU.add),
                reads=("yac0", "yac1", f"x3t{slot}"), writes=(f"scr2{slot}",))
            p.dma("sync", outT[ec * 128:(ec + 1) * 128, tc * 512:(tc + 1) * 512],
                  st1[:, slot, :], reads=(f"scr2{slot}",), ch=1)
    return p.finish()


_NC_CACHE = {}


def _get_nc(name, builder):
    if name not in _NC_CACHE:
        _NC_CACHE[name] = builder()
    return _NC_CACHE[name]


def _run(name, builder, in_maps):
    nc = _get_nc(name, builder)
    res = run_bass_kernel_spmd(nc, in_maps, core_ids=list(range(NCORE)))
    return res.results


def _split_tokens(arr):
    out = []
    for c in range(NCORE):
        b, qq = c // 4, c % 4
        out.append(np.ascontiguousarray(arr[b, qq * TPC:(qq + 1) * TPC]))
    return out


def _colsof(vec):
    return np.ascontiguousarray(np.asarray(vec).reshape(8, 128).T)


def kernel(**inputs):
    I = {k: np.asarray(v) for k, v in inputs.items()}
    x = I["L0_attn_x"]
    ident_bf = np.eye(128, dtype=BF)
    ident_f = np.eye(128, dtype=np.float32)

    wqT = np.ascontiguousarray(I["L0_attn_wq"].T)
    wkT = np.ascontiguousarray(I["L0_attn_wk"].T)
    wvT = np.ascontiguousarray(I["L0_attn_wv"].T)
    r1 = _run("l1", build_l1,
              [{"x": xs, "wqT": wqT, "wkT": wkT, "wvT": wvT, "identbf": ident_bf}
               for xs in _split_tokens(x)])

    qf = [np.concatenate([r1[4 * b + q]["q_f"] for q in range(4)], axis=1) for b in range(2)]
    kf = [np.concatenate([r1[4 * b + q]["k_f"] for q in range(4)], axis=1) for b in range(2)]
    vt = [np.concatenate([r1[4 * b + q]["v_t"] for q in range(4)], axis=0) for b in range(2)]

    tri = np.triu(np.ones((128, 128), np.float32))
    m4 = np.zeros((128, 4, 512), np.float32)
    for t in range(4):
        m4[:, t, t * 128:(t + 1) * 128] = tri
        m4[:, t, (t + 1) * 128:] = 1.0
    masks4 = np.ascontiguousarray(m4.reshape(128, 2048)).astype(BF)
    in2 = []
    for c in range(NCORE):
        b, hg = c // 4, c % 4
        rs = slice(hg * 256, (hg + 1) * 256)
        in2.append({"qh": np.ascontiguousarray(qf[b][rs]),
                    "kh": np.ascontiguousarray(kf[b][rs]),
                    "vh": np.ascontiguousarray(vt[b][:, rs]),
                    "masks4": masks4, "identf": ident_f})
    r2 = _run("l2", build_l2, in2)
    atf = [np.concatenate([r2[4 * b + hg]["attn"] for hg in range(4)], axis=1)
           for b in range(2)]

    woT = np.ascontiguousarray(I["L0_attn_wo"].T)
    wgT = np.ascontiguousarray(I["L0_ffn_w_gate"].T)
    wuT = np.ascontiguousarray(I["L0_ffn_w_up"].T)
    in3 = []
    for c in range(NCORE):
        b, qq = c // 4, c % 4
        ts_ = slice(qq * TPC, (qq + 1) * TPC)
        in3.append({"atT": np.ascontiguousarray(atf[b][ts_].T),
                    "xT": np.ascontiguousarray(x[b, ts_].T),
                    "woT": woT, "wgT": wgT, "wuT": wuT})
    r3ab = _run("l3ab", build_l3ab, in3)

    wdT = np.ascontiguousarray(I["L0_ffn_w_down"].T)
    r3c = _run("l3c", build_l3c,
               [{"hT": r3ab[c]["hT"], "x1T": r3ab[c]["x1T"], "wdT": wdT}
                for c in range(NCORE)])

    in4 = []
    w4 = {}
    for nm in ("r", "k", "v"):
        b1 = I[f"L1_attn_{nm}b1"]
        w4[nm + "W1"] = np.ascontiguousarray(I[f"L1_attn_{nm}W1"])
        w4[nm + "b1c"] = np.ascontiguousarray(b1[:9 * 128].reshape(9, 128).T)
        w4[nm + "b1t"] = np.ascontiguousarray(b1[9 * 128:].reshape(76, 1))
        w4[nm + "W2"] = np.ascontiguousarray(I[f"L1_attn_{nm}W2"])
        w4[nm + "b2c"] = _colsof(I[f"L1_attn_{nm}b2"])
        w4[nm + "muc"] = _colsof(I[f"L1_attn_mu_{nm}"])
    hsel = np.stack([np.r_[np.ones(64), np.zeros(64)],
                     np.r_[np.zeros(64), np.ones(64)]], axis=1).astype(np.float32)
    sel2 = np.ascontiguousarray(hsel.T)
    w4b = {"remc_d": _colsof(I["L1_attn_rem_mult"]),
           "mixc_d": _colsof(I["L1_attn_iclr_mix"]),
           "sel2_d": sel2, "hsel_d": hsel}
    for nm in ("a", "d", "g"):
        w4b[nm + "A"] = np.ascontiguousarray(I[f"L1_attn_{nm}A"])
        w4b[nm + "B"] = np.ascontiguousarray(I[f"L1_attn_{nm}B"])
        bkey = {"a": "L1_attn_ab", "d": "L1_attn_db", "g": None}[nm]
        bvec = I[bkey] if bkey else np.zeros(C, np.float32)
        w4b[nm + "bc"] = _colsof(bvec)
        w4b[nm + "muc"] = _colsof(I[f"L1_attn_mu_{nm}"])
    r4a = _run("l4ab", build_l4ab,
               [{"xlnT": r3c[c]["xln2T"], **w4, **w4b} for c in range(NCORE)])
    r4b = r4a

    def _gather_feat(key, res, dt=None):
        out = []
        for b in range(2):
            full = np.concatenate([res[4 * b + q][key] for q in range(4)], axis=1)
            out.append(full)
        return out

    kapf = _gather_feat("kap_f", r4b)
    ktlf = _gather_feat("ktil_f", r4b)
    rhof = _gather_feat("rho_f", r4b)
    vf = _gather_feat("v_f", r4a)
    rf = _gather_feat("r_f", r4a)
    df = _gather_feat("d_f", r4b)
    mAB = np.concatenate([np.tril(np.ones((128, 128), np.float32), -1)] * 2, axis=1)
    mSU = np.triu(np.ones((128, 128), np.float32), 1)
    mIU = np.triu(np.ones((128, 128), np.float32), 0)
    in5 = []
    for c in range(NCORE):
        b, hg = c // 4, c % 4
        rs = slice(hg * 256, (hg + 1) * 256)
        in5.append({"kapP": np.ascontiguousarray(kapf[b][rs]),
                    "ktlP": np.ascontiguousarray(ktlf[b][rs]),
                    "rhoP": np.ascontiguousarray(rhof[b][rs]),
                    "vP": np.ascontiguousarray(vf[b][rs]),
                    "rP": np.ascontiguousarray(rf[b][rs]),
                    "dP": np.ascontiguousarray(df[b][rs]),
                    "mABd": mAB, "mSUd": mSU, "mIUd": mIU,
                    "identbf": ident_bf})
    r5 = _run("l5", build_l5, in5)
    yfull = [np.concatenate([r5[4 * b + hg]["yP"] for hg in range(4)], axis=0)
             for b in range(2)]

    in6 = []
    for c in range(NCORE):
        b, qq = c // 4, c % 4
        cs = slice(qq * TPC, (qq + 1) * TPC)
        in6.append({"yT": np.ascontiguousarray(yfull[b][:, cs]),
                    "rT": r4a[c]["r_f"], "rhoT": r4b[c]["rho_f"],
                    "vT": r4a[c]["v_f"], "gT": r4b[c]["gate_f"],
                    "lngc": _colsof(I["L1_attn_ln_g"]),
                    "lnbc": _colsof(I["L1_attn_ln_b"]),
                    "bmc": _colsof(I["L1_attn_bonus_mult"]),
                    "hsel_d": hsel, "sel2_d": sel2})
    r6a = _run("l6a", build_l6a, in6)

    ob1 = I["L1_attn_ob1"]
    w6 = {"oW1": np.ascontiguousarray(I["L1_attn_oW1"]),
          "ob1c": np.ascontiguousarray(ob1[:9 * 128].reshape(9, 128).T),
          "ob1t": np.ascontiguousarray(ob1[9 * 128:].reshape(76, 1)),
          "oW2": np.ascontiguousarray(I["L1_attn_oW2"]),
          "ob2c": _colsof(I["L1_attn_ob2"])}
    r6b = _run("l6b", build_l6b,
               [{"zT": r6a[c]["zT"], "x2T": r3c[c]["x2T"], **w6} for c in range(NCORE)])

    w7 = {"gwD": np.ascontiguousarray(I["L1_ffn_gate_w"]),
          "w1D": np.ascontiguousarray(I["L1_ffn_w1"]),
          "w3D": np.ascontiguousarray(I["L1_ffn_w3"]),
          "w2D": np.ascontiguousarray(I["L1_ffn_w2"]),
          "identf": ident_f}
    r7 = _run("l7", build_l7,
              [{"xlnT": r6b[c]["xln3T"], "x3T": r6b[c]["x3T"], **w7}
               for c in range(NCORE)])

    out = np.empty((2, 8192, C), np.float32)
    for c in range(NCORE):
        b, qq = c // 4, c % 4
        out[b, qq * TPC:(qq + 1) * TPC] = r7[c]["outT"].T
    return out
